# version 20
# baseline (speedup 1.0000x reference)
import numpy as np
import ml_dtypes
from contextlib import ExitStack

import concourse.bass as bass
import concourse.mybir as mybir
from concourse.bass_utils import run_bass_kernel_spmd
from concourse.library_overlay import lower_extended_insts
from concourse.library_config import local_scatter as LS_LIB

F32 = mybir.dt.float32
F32R = mybir.dt.float32r
BF16 = mybir.dt.bfloat16
I16 = mybir.dt.int16
U32 = mybir.dt.uint32
ALU = mybir.AluOpType
AF = mybir.ActivationFunctionType
AX = mybir.AxisListType

B, S, D = 8, 2048, 1024
H, KD, HD = 16, 64, 128
DV = 2048
DCD = 4096
NEXP = 4096
NT = S // 128
EPS = 1e-5
FCH = 256
NCH = S // FCH
NTC = FCH // 128

DEBUG = False
DBG_BARRIERS = False
DBG_HEADS = 'qk,evac,av,epi,tp'
DBG_EPI_N = 10**9
DBG_F1_N = 10**9
DBG_F1_TRACE = False


def host_tables():
    t = {}
    f32 = np.float32
    angle = (1.0 / (10000.0 ** np.linspace(0.0, 1.0, KD // 2, dtype=f32))).astype(f32)
    angle = np.repeat(angle, 2).astype(f32)
    idx = np.arange(S, dtype=f32)
    arg = (idx[:, None] * angle[None, :]).astype(f32)
    sin = np.sin(arg.astype(np.float64)).astype(f32)
    cos = np.cos(arg.astype(np.float64)).astype(f32)
    t["COSB"] = np.concatenate([cos.T, cos.T], axis=0).astype(ml_dtypes.bfloat16)
    t["SINB"] = np.concatenate([sin.T, sin.T], axis=0).astype(ml_dtypes.bfloat16)

    decay = np.log1p(-(2.0 ** (-5.0 - np.arange(H, dtype=np.float64))))
    s64 = np.arange(S, dtype=np.float64)
    rs = (1.0 - np.exp(decay[:, None] * (s64[None, :] + 1.0))) / (1.0 - np.exp(decay[:, None]))
    t["QS16"] = (np.exp(decay[:, None] * s64[None, :]) / np.sqrt(rs)).astype(f32)
    t["KS16"] = (0.125 * np.exp(-decay[:, None] * s64[None, :])).astype(f32)

    selq = np.zeros((16, 8, 128), dtype=f32)
    for dt_ in range(8):
        selq[2 * dt_, dt_, 0:64] = 1.0
        selq[2 * dt_ + 1, dt_, 64:128] = 1.0
    t["SELQ"] = selq

    m = np.zeros((128, 4, 512), dtype=f32)
    for r in range(4):
        for p in range(128):
            m[p, r, 128 * r + p:] = 1.0
    t["MASKS"] = m.astype(ml_dtypes.bfloat16)

    PT = np.zeros((128, 128), dtype=f32)
    for i in range(64):
        PT[2 * i + 1, 2 * i] = -1.0
        PT[2 * i, 2 * i + 1] = 1.0
    t["PT"] = PT.astype(ml_dtypes.bfloat16)

    t["IDNB"] = np.eye(128, dtype=f32).astype(ml_dtypes.bfloat16)
    t["ONESC"] = np.ones((1, 128), dtype=f32)
    t["IOTA8"] = np.tile(np.arange(8, dtype=f32), (128, 1)).astype(f32)
    xy = np.arange(32) >> 3
    z = np.arange(4)
    up = (z[None, :] > xy[:, None]).astype(f32)
    t["MTRI2"] = np.tile(up.reshape(1, 128), (128, 1)).astype(f32)
    return t


class Buf:
    __slots__ = ("name", "wr", "rd")

    def __init__(self, name):
        self.name = name
        self.wr = {}
        self.rd = {}


def _merge(dst, src):
    for k, v in src.items():
        if dst.get(k, 0) < v:
            dst[k] = v


class Sched:
    ENGINES = ("S", "P", "T", "V", "A")

    def __init__(self, nc, es):
        self.nc = nc
        self.es = es
        self.ops = {e: [] for e in self.ENGINES}
        self.sems = {}
        self.cnt = {}
        self.lastwait = {e: {} for e in self.ENGINES}
        self.checkpoints = []
        for e in self.ENGINES:
            self._mk(e)

    def _mk(self, name):
        self.sems[name] = self.es.enter_context(self.nc.semaphore("sem_" + name))
        self.cnt[name] = 0

    def dma_sem(self, name):
        if name not in self.sems:
            self._mk(name)
        return name

    def buf(self, name):
        return Buf(name)

    def op(self, eng, fn, reads=(), writes=(), writes_d=(), sem=None, inc=1,
           extra=()):
        waits = {}
        for b in reads:
            _merge(waits, b.wr)
        for b in writes:
            _merge(waits, b.wr)
            _merge(waits, b.rd)
        for b in writes_d:
            _merge(waits, b.rd)
        for tok in extra:
            if tok:
                _merge(waits, {tok[0]: tok[1]})
        semname = sem if sem is not None else eng
        if inc:
            self.cnt[semname] += inc
            tok = (semname, self.cnt[semname])
        else:
            tok = None
        lw = self.lastwait[eng]
        wl = []
        for s, v in waits.items():
            if lw.get(s, 0) < v:
                wl.append((s, v))
                lw[s] = v
        self.ops[eng].append((wl, fn, semname if inc else None, inc))
        if tok:
            td = {tok[0]: tok[1]}
            for b in reads:
                _merge(b.rd, td)
            for b in writes:
                b.wr = dict(td)
                b.rd = {}
            for b in writes_d:
                _merge(b.wr, td)
        else:
            pass
        return tok

    def dma(self, eng, out, in_, sem, reads=(), writes=(), writes_d=(), extra=(),
            transpose=False, accum=False):
        if accum:
            fn = lambda e, o=out, i=in_: e.dma_start(
                out=o, in_=i, accum_op=mybir.AluOpType.add)
        elif transpose:
            fn = lambda e, o=out, i=in_: e.dma_start(out=o, in_=i, transpose=True)
        else:
            fn = lambda e, o=out, i=in_: e.dma_start(out=o, in_=i)
        prev = self.cnt.get(sem, 0)
        ex = list(extra)
        if prev:
            ex.append((sem, prev))
        return self.op(eng, fn, reads=reads, writes=writes, writes_d=writes_d,
                       sem=sem, inc=16, extra=ex)

    def load_group(self, sem, loads):
        used = set()
        bufs = []
        for (eng, out, in_, buf) in loads:
            s = self.dma_sem(sem + "_p") if eng == "P" else self.dma_sem(sem)
            used.add(s)
            self.op(eng, lambda e, o=out, i=in_: e.dma_start(out=o, in_=i),
                    sem=s, inc=16)
            bufs.append(buf)
        final = {s: self.cnt[s] for s in used}
        for b in bufs:
            b.wr = dict(final)
            b.rd = {}

    def barrier(self):
        allsems = {s: v for s, v in self.cnt.items() if v > 0}
        for e in self.ENGINES:
            lw = self.lastwait[e]
            wl = []
            for s, v in allsems.items():
                if lw.get(s, 0) < v:
                    wl.append((s, v))
                    lw[s] = v
            if wl:
                self.ops[e].append((wl, lambda eng: None, None, 0))
        self.checkpoints.append({e: len(self.ops[e]) for e in self.ENGINES})

    def emit(self, block, upto=None):
        decos = {"S": block.sync, "P": block.gpsimd, "T": block.tensor,
                 "V": block.vector, "A": block.scalar}
        sems = self.sems
        cut = self.checkpoints[upto] if upto is not None else None
        for e in self.ENGINES:
            ops = self.ops[e] if cut is None else self.ops[e][:cut[e]]

            def make(ops):
                def run(eng):
                    for wl, fn, semname, inc in ops:
                        for s, v in wl:
                            eng.wait_ge(sems[s], v)
                        instr = fn(eng)
                        if semname is not None:
                            instr.then_inc(sems[semname], inc)
                return run

            decos[e](make(ops))


def build_program(dbg=False, upto=None):
    nc = bass.Bass("TRN2", target_bir_lowering=False)
    dt = nc.dram_tensor

    X = dt("X", [S, D], F32, kind="ExternalInput").ap()
    Wq = dt("Wq", [4, 128, 8, 256], F32, kind="ExternalInput").ap()
    Wk = dt("Wk", [4, 128, 8, 256], F32, kind="ExternalInput").ap()
    Wv = dt("Wv", [4, 128, 8, 512], F32, kind="ExternalInput").ap()
    Wg = dt("Wg", [4, 128, 8, 512], F32, kind="ExternalInput").ap()
    Wo = dt("Wo", [4, 128, 4, D], F32, kind="ExternalInput").ap()
    BQ = dt("BQ", [128, 8], F32, kind="ExternalInput").ap()
    BK = dt("BK", [128, 8], F32, kind="ExternalInput").ap()
    BV = dt("BV", [1, DV], F32, kind="ExternalInput").ap()
    BGA = dt("BGA", [1, DV], F32, kind="ExternalInput").ap()
    BO = dt("BO", [1, D], F32, kind="ExternalInput").ap()
    FWg = dt("FWg", [32, 128, 8, 128], F32, kind="ExternalInput").ap()
    FWu = dt("FWu", [32, 128, 8, 128], F32, kind="ExternalInput").ap()
    FWd = dt("FWd", [32, 128, D], F32, kind="ExternalInput").ap()
    FBg = dt("FBg", [128, 32], F32, kind="ExternalInput").ap()
    FBu = dt("FBu", [128, 32], F32, kind="ExternalInput").ap()
    FBd = dt("FBd", [1, D], F32, kind="ExternalInput").ap()
    FWq = dt("FWq", [128, 8, 512], F32, kind="ExternalInput").ap()
    KEYS = dt("KEYS", [128, 8 * 64], F32, kind="ExternalInput").ap()
    DET = dt("DET", [128, 8, NEXP], F32, kind="ExternalInput").ap()
    UEW = dt("UEW", [128, 32, D], F32, kind="ExternalInput").ap()
    CC = dt("CC", [128, 8], F32, kind="ExternalInput").ap()
    WMOD = dt("WMOD", [12, 128, 8, 512], F32, kind="ExternalInput").ap()

    COSB = dt("COSB", [128, S], BF16, kind="ExternalInput").ap()
    SINB = dt("SINB", [128, S], BF16, kind="ExternalInput").ap()
    QS16 = dt("QS16", [16, S], F32, kind="ExternalInput").ap()
    KS16 = dt("KS16", [16, S], F32, kind="ExternalInput").ap()
    SELQ = dt("SELQ", [16, 8 * 128], F32, kind="ExternalInput").ap()
    MASKS = dt("MASKS", [128, 4 * 512], BF16, kind="ExternalInput").ap()
    PTD = dt("PT", [128, 128], BF16, kind="ExternalInput").ap()
    IDNB = dt("IDNB", [128, 128], BF16, kind="ExternalInput").ap()
    ONESC = dt("ONESC", [1, 128], F32, kind="ExternalInput").ap()
    IOTA8 = dt("IOTA8", [128, 8], F32, kind="ExternalInput").ap()
    MTRI2 = dt("MTRI2", [128, 128], F32, kind="ExternalInput").ap()

    IDNF = dt("IDNF", [128, 128], F32, kind="ExternalInput").ap()

    OUT = dt("OUT", [S, D], F32, kind="ExternalOutput").ap()
    sk = "ExternalOutput" if dbg else "Internal"
    PG = [dt(f"P{g}s", [S, D], F32, kind=sk).ap() for g in range(4)]
    X2S = dt("X2S", [S, D], F32, kind=sk).ap()
    H2S = dt("H2S", [128, NT, 8, 128], BF16, kind=sk).ap()
    EXS = dt("EXS", [S, D], F32, kind=sk).ap()
    MODS = dt("MODS", [1, 6 * D], F32, kind=sk).ap()
    DBGR = dt("DBGR", [128, 5120], F32, kind=sk).ap() if dbg else None


    es = ExitStack()
    with es:
        sc = Sched(nc, es)

        def sb(stack, name, shape, dtype):
            return stack.enter_context(nc.sbuf_tensor(name, shape, dtype))

        def psm(stack, name, shape, dtype=F32):
            return stack.enter_context(nc.psum_tensor(name, shape, dtype))

        idnb = sb(es, "idnb", [128, 128], BF16)
        idnf = sb(es, "idnf", [128, 128], F32)
        onesc_r = sb(es, "onesc_r", [1, 128], F32R)
        onesc_b = sb(es, "onesc_b", [1, 128], BF16)
        iota8 = sb(es, "iota8", [128, 8], F32)
        epsc = sb(es, "epsc", [128, 1], F32)
        b_epsc = sc.buf("epsc")
        b_idnb = sc.buf("idnb")
        b_idnf = sc.buf("idnf")
        b_onesr = sc.buf("onesc_r")
        b_onesb = sc.buf("onesc_b")
        b_iota8 = sc.buf("iota8")
        pWO = psm(es, "pWO", [128, D], F32)
        b_pWO = sc.buf("pWO")

        LD = sc.dma_sem("ldc")
        sc.load_group(LD, [
            ("S", idnb[:], IDNB, b_idnb),
            ("S", idnf[:], IDNF, b_idnf),
            ("P", onesc_r[:], ONESC, b_onesr),
            ("P", onesc_b[:], ONESC, b_onesb),
            ("S", iota8[:], IOTA8, b_iota8),
        ])
        sc.op("P", lambda e: e.load_library(LS_LIB), inc=0)
        sc.op("V", lambda e: e.memset(epsc[:], EPS), writes=[b_epsc])

        def bcast_tile(dstap, b_dst, rowap, b_row, ncols, pbuf, b_pbuf):
            for chk in range(ncols // 512):
                sc.op("T", lambda e, chk=chk: e.matmul(
                    pbuf[:, 0:512], onesc_r[:],
                    rowap[:, chk * 512:(chk + 1) * 512], start=True, stop=True),
                    reads=[b_onesr, b_row], writes=[b_pbuf])
                sc.op("V", lambda e, chk=chk: e.tensor_copy(
                    out=dstap[:, chk * 512:(chk + 1) * 512], in_=pbuf[:, 0:512]),
                    reads=[b_pbuf], writes_d=[b_dst])

        with ExitStack() as esM:
            cc = sb(esM, "cc", [128, 8], F32)
            silc_f = sb(esM, "silc_f", [128, 8], F32)
            silc_r = sb(esM, "silc_r", [128, 8], F32R)
            wmod_r = [sb(esM, "wmod_r0", [128, 8, 512], F32R),
                      sb(esM, "wmod_r1", [128, 8, 512], F32R)]
            modrow = sb(esM, "modrow", [1, 6 * D], F32)
            tmprow = sb(esM, "tmprow", [1, D], F32)
            sgnrow = sb(esM, "sgnrow", [1, D], F32)
            pM = psm(esM, "pM", [1, 512], F32)
            b_cc = sc.buf("cc")
            b_silcr = sc.buf("silc_r")
            b_wmod = [sc.buf("wmod_r0"), sc.buf("wmod_r1")]
            b_modrow = sc.buf("modrow")
            b_tmprow = sc.buf("tmprow")
            b_sgnrow = sc.buf("sgnrow")
            b_pM = sc.buf("pM")
            WL = sc.dma_sem("wl0")
            WL1 = sc.dma_sem("wl1")
            MSP = sc.dma_sem("msp")

            b_silcf = sc.buf("silc_f")
            sc.dma("P", cc[:], CC, WL, writes=[b_cc])
            sc.op("A", lambda e: e.activation(out=silc_f[:], in_=cc[:],
                                              func=AF.Silu),
                  reads=[b_cc], writes=[b_silcf])
            sc.op("V", lambda e: e.tensor_copy(out=silc_r[:], in_=silc_f[:]),
                  reads=[b_silcf], writes=[b_silcr])
            sc.dma("P", wmod_r[0][:], WMOD[0], WL, writes=[b_wmod[0]])
            sc.dma("P", wmod_r[1][:], WMOD[1], WL1, writes=[b_wmod[1]])
            for ch in range(12):
                par = ch % 2
                for k in range(8):
                    sc.op("T", lambda e, k=k, par=par: e.matmul(
                        pM[:], silc_r[:, k:k + 1], wmod_r[par][:, k, :],
                        start=(k == 0), stop=(k == 7)),
                        reads=[b_silcr, b_wmod[par]],
                        writes=([b_pM] if k == 0 else []),
                        writes_d=([b_pM] if k > 0 else []),
                        inc=(1 if k == 7 else 0))
                if ch + 2 < 12:
                    sc.dma("P", wmod_r[par][:], WMOD[ch + 2],
                           [WL, WL1][par], writes=[b_wmod[par]])
                sc.op("V", lambda e, ch=ch: e.tensor_copy(
                    out=modrow[:, ch * 512:(ch + 1) * 512], in_=pM[:]),
                    reads=[b_pM], writes_d=[b_modrow])
            for off in (1 * D, 4 * D):
                sc.op("V", lambda e, off=off: e.tensor_scalar(
                    out=modrow[:, off:off + D], in0=modrow[:, off:off + D],
                    scalar1=1.0, scalar2=None, op0=ALU.add),
                    reads=[b_modrow], writes_d=[b_modrow])
            for off in (2 * D, 5 * D):
                sc.op("V", lambda e, off=off: e.tensor_scalar(
                    out=sgnrow[:], in0=modrow[:, off:off + D], scalar1=0.0,
                    scalar2=2.0, op0=ALU.is_ge, op1=ALU.mult),
                    reads=[b_modrow], writes=[b_sgnrow])
                sc.op("V", lambda e: e.tensor_scalar(
                    out=sgnrow[:], in0=sgnrow[:], scalar1=-1.0, scalar2=None,
                    op0=ALU.add), reads=[b_sgnrow], writes_d=[b_sgnrow])
                sc.op("A", lambda e, off=off: e.activation(
                    out=tmprow[:], in_=modrow[:, off:off + D], func=AF.Abs),
                    reads=[b_modrow], writes=[b_tmprow])
                sc.op("V", lambda e: e.tensor_scalar(
                    out=tmprow[:], in0=tmprow[:], scalar1=0.25,
                    scalar2=None, op0=ALU.max),
                    reads=[b_tmprow], writes_d=[b_tmprow])
                sc.op("V", lambda e, off=off: e.tensor_tensor(
                    out=modrow[:, off:off + D], in0=tmprow[:], in1=sgnrow[:],
                    op=ALU.mult),
                    reads=[b_tmprow, b_sgnrow, b_modrow], writes_d=[b_modrow])
            sc.dma("S", MODS, modrow[:], MSP, reads=[b_modrow])
            sc.barrier()

        with ExitStack() as esA:
            cosT = sb(esA, "cosT", [128, S], BF16)
            sinT = sb(esA, "sinT", [128, S], BF16)
            qs16 = sb(esA, "qs16", [16, S], F32R)
            ks16 = sb(esA, "ks16", [16, S], F32R)
            selq = sb(esA, "selq", [16, 8, 128], F32R)
            masks = sb(esA, "masks", [128, 4, 512], BF16)
            ptm = sb(esA, "ptm", [128, 128], BF16)
            bq = sb(esA, "bq", [128, 8], F32)
            bk = sb(esA, "bk", [128, 8], F32)
            saB = sb(esA, "saB", [128, D], F32)
            caB = sb(esA, "caB", [128, D], F32)
            gaB = sb(esA, "gaB", [128, D], F32)
            hT = sb(esA, "hT", [128, 8, S], BF16)
            b_cos = sc.buf("cosT"); b_sin = sc.buf("sinT")
            b_qs16 = sc.buf("qs16"); b_ks16 = sc.buf("ks16")
            b_selq = sc.buf("selq"); b_masks = sc.buf("masks")
            b_ptm = sc.buf("ptm"); b_bq = sc.buf("bq"); b_bk = sc.buf("bk")
            b_saB = sc.buf("saB"); b_caB = sc.buf("caB"); b_gaB = sc.buf("gaB")
            b_hT = sc.buf("hT")

            AL = sc.dma_sem("al")
            sc.load_group(AL, [
                ("P", cosT[:], COSB, b_cos),
                ("P", sinT[:], SINB, b_sin),
                ("P", qs16[:], QS16, b_qs16),
                ("P", ks16[:], KS16, b_ks16),
                ("P", selq[:].rearrange("a k p -> a (k p)"), SELQ, b_selq),
                ("S", masks[:].rearrange("p r n -> p (r n)"), MASKS, b_masks),
                ("S", ptm[:], PTD, b_ptm),
                ("S", bq[:], BQ, b_bq),
                ("S", bk[:], BK, b_bk),
            ])

            with ExitStack() as esL:
                modr = sb(esL, "modr", [1, D], F32R)
                b_modr = sc.buf("modr")
                MRL = sc.dma_sem("mrl")
                for (dst, bdst, off) in ((saB, b_saB, 0), (caB, b_caB, D),
                                         (gaB, b_gaB, 2 * D)):
                    sc.dma("P", modr[:], MODS[:, off:off + D], MRL,
                           writes=[b_modr])
                    bcast_tile(dst, bdst, modr, b_modr, D, pWO, b_pWO)
                xt = [sb(esL, "xt0", [128, D], F32), sb(esL, "xt1", [128, D], F32)]
                lnsq = sb(esL, "lnsq", [128, D], F32)
                lntmp = sb(esL, "lntmp", [128, D], F32)
                lnh = [sb(esL, "lnh0", [128, D], BF16),
                       sb(esL, "lnh1", [128, D], BF16)]
                scl = [sb(esL, "scl0", [128, 4], F32),
                       sb(esL, "scl1", [128, 4], F32)]
                pT1 = psm(esL, "pT1", [128, D], BF16)
                b_xt = [sc.buf("xt0"), sc.buf("xt1")]
                b_lnsq = sc.buf("lnsq"); b_lntmp = sc.buf("lntmp")
                b_lnh = [sc.buf("lnh0"), sc.buf("lnh1")]
                b_scl = [sc.buf("scl0"), sc.buf("scl1")]
                b_pT1 = sc.buf("pT1")
                XL = [sc.dma_sem("xl0"), sc.dma_sem("xl1")]

                def ln_tile(xsrc, bx, cB, bcB, sB, bsB, par):
                    scp, bscp = scl[par], b_scl[par]
                    lnhp, blnh = lnh[par], b_lnh[par]
                    sc.op("V", lambda e: e.reduce_sum(out=scp[:, 0:1], in_=xsrc[:],
                                                      axis=AX.X),
                          reads=[bx], writes=[bscp])
                    sc.op("A", lambda e: e.activation(out=lnsq[:], in_=xsrc[:],
                                                      func=AF.Square,
                                                      accum_out=scp[:, 1:2]),
                          reads=[bx], writes=[b_lnsq], writes_d=[bscp])
                    sc.op("V", lambda e: e.tensor_scalar(
                        out=scp[:, 2:3], in0=scp[:, 0:1], scalar1=1.0 / D,
                        scalar2=None, op0=ALU.mult),
                        reads=[bscp], writes_d=[bscp])
                    sc.op("V", lambda e: e.tensor_tensor(
                        out=scp[:, 3:4], in0=scp[:, 2:3], in1=scp[:, 2:3],
                        op=ALU.mult), reads=[bscp], writes_d=[bscp])
                    sc.op("V", lambda e: e.scalar_tensor_tensor(
                        out=scp[:, 3:4], in0=scp[:, 1:2], scalar=1.0 / D,
                        in1=scp[:, 3:4], op0=ALU.mult, op1=ALU.subtract),
                        reads=[bscp], writes_d=[bscp])
                    sc.op("A", lambda e: e.activation(out=scp[:, 3:4],
                                                      in_=scp[:, 3:4],
                                                      func=AF.Sqrt, bias=epsc[:]),
                          reads=[bscp, b_epsc], writes_d=[bscp])
                    sc.op("V", lambda e: e.reciprocal(out=scp[:, 3:4],
                                                      in_=scp[:, 3:4]),
                          reads=[bscp], writes_d=[bscp])
                    sc.op("V", lambda e: e.tensor_scalar(
                        out=lntmp[:], in0=xsrc[:], scalar1=scp[:, 2:3],
                        scalar2=scp[:, 3:4], op0=ALU.subtract, op1=ALU.mult),
                        reads=[bx, bscp], writes=[b_lntmp])
                    sc.op("V", lambda e: e.tensor_tensor(
                        out=lntmp[:], in0=lntmp[:], in1=cB[:], op=ALU.mult),
                        reads=[b_lntmp, bcB], writes_d=[b_lntmp])
                    sc.op("V", lambda e: e.tensor_tensor(
                        out=lnhp[:], in0=lntmp[:], in1=sB[:], op=ALU.add),
                        reads=[b_lntmp, bsB], writes=[blnh])

                def ln_tp(i):
                    par = i % 2
                    for k in range(8):
                        sc.op("T", lambda e, k=k, par=par: e.transpose(
                            pT1[:, k * 128:(k + 1) * 128],
                            lnh[par][:, k * 128:(k + 1) * 128], idnb[:]),
                            reads=[b_lnh[par], b_idnb],
                            writes=([b_pT1] if k == 0 else []),
                            writes_d=([b_pT1] if k > 0 else []),
                            inc=(1 if k == 7 else 0))
                    sc.op("A", lambda e, i=i: e.activation(
                        out=hT[:, :, i * 128:(i + 1) * 128],
                        in_=pT1[:].rearrange("p (k t) -> p k t", k=8),
                        func=AF.Copy),
                        reads=[b_pT1], writes_d=[b_hT])

                for i in range(NT + 1):
                    par = i % 2
                    if i < NT:
                        sc.dma("S", xt[par][:], X[i * 128:(i + 1) * 128, :],
                               XL[par], writes=[b_xt[par]])
                        ln_tile(xt[par], b_xt[par], caB, b_caB, saB, b_saB, par)
                    if i > 0:
                        ln_tp(i - 1)
                sc.barrier()

            qrT = sb(esA, "qrT", [128, 2, S], BF16)
            krT = sb(esA, "krT", [128, 2, S], BF16)
            v_aug = sb(esA, "v_aug", [128, NT, 4, 129], BF16)
            gts = sb(esA, "gts", [128, NT, 512], BF16)
            qkT = sb(esA, "qkT", [128, NT, 512], BF16)
            avst = sb(esA, "avst", [128, NT, 129], F32)
            sqd = sb(esA, "sqd", [128, 128], F32)
            gatc = sb(esA, "gatc", [128, NT, 128], BF16)
            gatedT = sb(esA, "gatedT", [128, 4, S], BF16)
            amean = sb(esA, "amean", [128, NT], F32)
            asum2 = sb(esA, "asum2", [128, NT], F32)
            aden = sb(esA, "aden", [128, NT], F32)
            atmp = sb(esA, "atmp", [128, NT], F32)
            wq_sl = sb(esA, "wq_sl", [128, 8, 256], BF16)
            wk_sl = sb(esA, "wk_sl", [128, 8, 256], BF16)
            wvg_sl = sb(esA, "wvg_sl", [128, 8, 512], BF16)
            wo_sl = sb(esA, "wo_sl", [128, 4, D], BF16)
            bvrow = sb(esA, "bvrow", [1, 512], BF16)
            bgrow = sb(esA, "bgrow", [1, 512], BF16)
            qs_bf = sb(esA, "qs_bf", [128, 512], BF16)
            t1b = sb(esA, "t1b", [128, 512], BF16)
            t2b = sb(esA, "t2b", [128, 512], BF16)
            spl_t = sb(esA, "spl0", [128, D], F32)
            spl_t2 = sb(esA, "spl1", [128, D], F32)
            borow = spl_t[0:1, :]
            spl = [spl_t, spl_t2]
            b_qrT = sc.buf("qrT"); b_krT = sc.buf("krT"); b_vaug = sc.buf("v_aug")
            b_gts = sc.buf("gts")
            b_qkTj = [sc.buf(f"qkTj{j}") for j in range(NT)]
            b_gatedT = sc.buf("gatedT")
            b_avst = sc.buf("avst")
            b_sqd = sc.buf("sqd"); b_gatc = sc.buf("gatc")
            b_amean = sc.buf("amean"); b_asum2 = sc.buf("asum2")
            b_aden = sc.buf("aden"); b_atmp = sc.buf("atmp")
            b_wq = sc.buf("wq_sl"); b_wk = sc.buf("wk_sl"); b_wvg = sc.buf("wvg_sl")
            b_wo = sc.buf("wo_sl")
            b_bvrow = sc.buf("bvrow"); b_bgrow = sc.buf("bgrow")
            b_qsbf = sc.buf("qs_bf"); b_t1 = sc.buf("t1b"); b_t2 = sc.buf("t2b")
            b_spl_t = sc.buf("spl0")
            b_spl = [b_spl_t, sc.buf("spl1")]
            WQL = sc.dma_sem("wql"); WKL = sc.dma_sem("wkl")
            WVL = sc.dma_sem("wvl"); BVL = sc.dma_sem("bvl"); BGL = sc.dma_sem("bgl")
            WOL = sc.dma_sem("wol"); BOL = sc.dma_sem("bol")
            SPS = [sc.dma_sem("sp0"), sc.dma_sem("sp1")]
            GTS = [sc.dma_sem(f"gtt{q}") for q in range(4)]

            pA0 = psm(esA, "pA0", [128, 512], F32)
            pA1 = psm(esA, "pA1", [128, 512], F32)
            pA2 = psm(esA, "pA2", [128, 512], F32)
            pA3 = psm(esA, "pA3", [128, 512], F32)
            pAV = [psm(esA, f"pAV{j}", [128, 132], F32) for j in range(2)]
            b_pA0 = sc.buf("pA0"); b_pA1 = sc.buf("pA1")
            b_pA2 = sc.buf("pA2"); b_pA3 = sc.buf("pA3")
            b_pAV = [sc.buf("pAV0"), sc.buf("pAV1")]

            for hh in range(4):
                sc.op("V", lambda e, hh=hh: e.memset(v_aug[:, :, hh, 128:129], 1.0),
                      writes_d=[b_vaug])

            def a_loads(g):
                sc.dma("P", wq_sl[:], Wq[g], WQL, writes=[b_wq])
                sc.dma("P", wk_sl[:], Wk[g], WKL, writes=[b_wk])
                sc.dma("P", wvg_sl[:], Wv[g], WVL, writes=[b_wvg])
                sc.dma("P", bvrow[:], BV[:, g * 512:(g + 1) * 512], BVL,
                       writes=[b_bvrow])
                sc.dma("P", bgrow[:], BGA[:, g * 512:(g + 1) * 512], BGL,
                       writes=[b_bgrow])

            def qk_proj(g, wsl, bw, bias_sb, bbias, sc16, bsc16, dstT, bdst):
                for m in range(2):
                    dt_ = g * 2 + m
                    for chk in range(4):
                        for k in range(8):
                            sc.op("T", lambda e, m=m, chk=chk, k=k, wsl=wsl:
                                  e.matmul(
                                pA0[:], wsl[:, k, m * 128:(m + 1) * 128],
                                hT[:, k, chk * 512:(chk + 1) * 512],
                                start=(k == 0), stop=(k == 7)),
                                reads=[bw, b_hT],
                                writes=([b_pA0] if k == 0 else []),
                                writes_d=([b_pA0] if k > 0 else []),
                                inc=(1 if k == 7 else 0))
                        sc.op("T", lambda e, dt_=dt_, chk=chk, sc16=sc16:
                              e.matmul(
                            pA2[:], selq[:, dt_, :],
                            sc16[:, chk * 512:(chk + 1) * 512],
                            start=True, stop=True),
                            reads=[b_selq, bsc16], writes=[b_pA2])
                        sc.op("A", lambda e, dt_=dt_, bias_sb=bias_sb:
                              e.activation(
                            out=qs_bf[:], in_=pA0[:], func=AF.Identity,
                            bias=bias_sb[:, dt_:dt_ + 1]),
                            reads=[b_pA0, bbias], writes=[b_qsbf])
                        sc.op("T", lambda e: e.matmul(
                            pA1[:], ptm[:], qs_bf[:], start=True, stop=True),
                            reads=[b_ptm, b_qsbf], writes=[b_pA1])
                        sc.op("V", lambda e, chk=chk: e.tensor_tensor(
                            out=t1b[:], in0=qs_bf[:],
                            in1=cosT[:, chk * 512:(chk + 1) * 512],
                            op=ALU.mult),
                            reads=[b_qsbf, b_cos], writes=[b_t1])
                        sc.op("V", lambda e, chk=chk: e.tensor_tensor(
                            out=t2b[:], in0=pA1[:],
                            in1=sinT[:, chk * 512:(chk + 1) * 512],
                            op=ALU.mult),
                            reads=[b_pA1, b_sin], writes=[b_t2])
                        sc.op("V", lambda e: e.tensor_tensor(
                            out=t2b[:], in0=t2b[:], in1=t1b[:], op=ALU.add),
                            reads=[b_t1, b_t2], writes_d=[b_t2])
                        sc.op("V", lambda e, m=m, chk=chk, dstT=dstT:
                              e.tensor_tensor(
                            out=dstT[:, m, chk * 512:(chk + 1) * 512],
                            in0=t2b[:], in1=pA2[:], op=ALU.mult),
                            reads=[b_t2, b_pA2], writes_d=[bdst])

            def a_proj(g):
                qk_proj(g, wq_sl, b_wq, bq, b_bq, qs16, b_qs16, qrT, b_qrT)
                qk_proj(g, wk_sl, b_wk, bk, b_bk, ks16, b_ks16, krT, b_krT)
                pvg = [pA0, pA1]
                bpvg = [b_pA0, b_pA1]
                for i in range(NT):
                    pv, bpv = pvg[i % 2], bpvg[i % 2]
                    for k in range(8):
                        sc.op("T", lambda e, i=i, k=k, pv=pv: e.matmul(
                            pv[:], hT[:, k, i * 128:(i + 1) * 128],
                            wvg_sl[:, k, :], start=(k == 0), stop=False),
                            reads=[b_wvg, b_hT],
                            writes=([bpv] if k == 0 else []),
                            writes_d=([bpv] if k > 0 else []),
                            inc=(1 if k == 7 else 0))
                    sc.op("T", lambda e, pv=pv: e.matmul(
                        pv[:], onesc_b[:], bvrow[:], start=False, stop=True),
                        reads=[b_onesb, b_bvrow], writes_d=[bpv])
                    sc.op("A", lambda e, i=i, pv=pv: e.activation(
                        out=v_aug[:, i, :, 0:128],
                        in_=pv[:].rearrange("p (h t) -> p h t", h=4),
                        func=AF.Copy),
                        reads=[bpv], writes_d=[b_vaug])
                sc.dma("P", wvg_sl[:], Wg[g], WVL, writes=[b_wvg])
                for i in range(NT):
                    pv, bpv = pvg[i % 2], bpvg[i % 2]
                    for k in range(8):
                        sc.op("T", lambda e, i=i, k=k, pv=pv: e.matmul(
                            pv[:], hT[:, k, i * 128:(i + 1) * 128],
                            wvg_sl[:, k, :], start=(k == 0), stop=False),
                            reads=[b_wvg, b_hT],
                            writes=([bpv] if k == 0 else []),
                            writes_d=([bpv] if k > 0 else []),
                            inc=(1 if k == 7 else 0))
                    sc.op("T", lambda e, pv=pv: e.matmul(
                        pv[:], onesc_b[:], bgrow[:], start=False, stop=True),
                        reads=[b_onesb, b_bgrow], writes_d=[bpv])
                    sc.op("A", lambda e, i=i, pv=pv: e.activation(
                        out=gts[:, i, :], in_=pv[:], func=AF.Silu),
                        reads=[bpv], writes_d=[b_gts])

            pq = [pA0, pA1, pA2, pA3]
            bpq = [b_pA0, b_pA1, b_pA2, b_pA3]

            def a_qkb(hl, c):
                mrow = hl // 2
                prow = 64 * (hl % 2)
                jmax = 4 * c + 4
                for j in range(jmax if 'qk' in DBG_HEADS else 0):
                    jj = j % 4
                    sc.op("T", lambda e, j=j, c=c, jj=jj, mrow=mrow,
                          prow=prow: e.matmul(
                        pq[jj][:],
                        krT[prow:prow + 64, mrow, j * 128:(j + 1) * 128],
                        qrT[prow:prow + 64, mrow, c * 512:(c + 1) * 512],
                        start=True, stop=True),
                        reads=[b_krT, b_qrT], writes=[bpq[jj]])
                    r = j - 4 * c
                    if 'evac' not in DBG_HEADS:
                        continue
                    if r >= 0:
                        sc.op("V", lambda e, j=j, r=r, jj=jj: e.tensor_tensor(
                            out=qkT[:, j, :], in0=pq[jj][:],
                            in1=masks[:, r, :], op=ALU.mult),
                            reads=[bpq[jj], b_masks],
                            writes=[b_qkTj[j]])
                    elif j % 2 == 0:
                        sc.op("V", lambda e, j=j, jj=jj: e.tensor_copy(
                            out=qkT[:, j, :], in_=pq[jj][:]),
                            reads=[bpq[jj]], writes=[b_qkTj[j]])
                    else:
                        sc.op("A", lambda e, j=j, jj=jj: e.activation(
                            out=qkT[:, j, :], in_=pq[jj][:],
                            func=AF.Copy),
                            reads=[bpq[jj]], writes=[b_qkTj[j]])

            def a_avb(hl, c):
                for r in range(4 if 'av' in DBG_HEADS else 0):
                    st = 4 * c + r
                    pa = pAV[st % 2]
                    bpa = b_pAV[st % 2]
                    njs = st + 1
                    for j in range(njs):
                        sc.op("T", lambda e, j=j, r=r, hl=hl, pa=pa,
                              njs=njs: e.matmul(
                            pa[:, 0:129],
                            qkT[:, j, r * 128:(r + 1) * 128],
                            v_aug[:, j, hl, 0:129],
                            start=(j == 0), stop=(j == njs - 1)),
                            reads=[b_qkTj[j], b_vaug],
                            writes=([bpa] if j == 0 else []),
                            writes_d=([bpa] if j > 0 else []))
                    if 'epi' not in DBG_HEADS:
                        continue
                    sc.op("A", lambda e, pa=pa, st=st: e.activation(
                        out=avst[:, st, 0:129], in_=pa[:, 0:129],
                        func=AF.Copy),
                        reads=[bpa], writes_d=[b_avst])

            def a_epi(hl):
                av0 = avst[:, :, 0:128]
                dcol = avst[:, :, 128:129].rearrange("p s o -> p (s o)")
                for st in range(NT):
                    sc.op("A", lambda e, st=st: e.activation(
                        out=sqd[:], in_=avst[:, st, 0:128],
                        func=AF.Square, accum_out=asum2[:, st:st + 1]),
                        reads=([b_avst] if st == 0 else
                               [b_avst, b_sqd]),
                        writes=([b_sqd, b_asum2] if st == 0 else []),
                        writes_d=([b_sqd, b_asum2] if st > 0 else []))
                sc.op("V", lambda e: e.reduce_sum(out=amean[:], in_=av0,
                                                  axis=AX.X),
                      reads=[b_avst], writes=[b_amean])
                sc.op("V", lambda e: e.tensor_scalar(
                    out=amean[:], in0=amean[:], scalar1=1.0 / 128,
                    scalar2=None, op0=ALU.mult),
                    reads=[b_amean], writes_d=[b_amean])
                sc.op("V", lambda e: e.tensor_scalar(
                    out=aden[:], in0=dcol, scalar1=-1.0, scalar2=1.0,
                    op0=ALU.mult, op1=ALU.max),
                    reads=[b_avst], writes=[b_aden])
                sc.op("V", lambda e: e.tensor_tensor(
                    out=aden[:], in0=dcol, in1=aden[:], op=ALU.max),
                    reads=[b_avst, b_aden], writes_d=[b_aden])
                sc.op("V", lambda e: e.tensor_tensor(
                    out=aden[:], in0=aden[:], in1=aden[:], op=ALU.mult),
                    reads=[b_aden], writes_d=[b_aden])
                sc.op("V", lambda e: e.tensor_scalar(
                    out=asum2[:], in0=asum2[:], scalar1=1.0 / 128,
                    scalar2=None, op0=ALU.mult),
                    reads=[b_asum2], writes_d=[b_asum2])
                sc.op("V", lambda e: e.tensor_tensor(
                    out=atmp[:], in0=amean[:], in1=amean[:], op=ALU.mult),
                    reads=[b_amean], writes=[b_atmp])
                sc.op("V", lambda e: e.tensor_tensor(
                    out=asum2[:], in0=asum2[:], in1=atmp[:],
                    op=ALU.subtract),
                    reads=[b_atmp, b_asum2], writes_d=[b_asum2])
                sc.op("V", lambda e: e.scalar_tensor_tensor(
                    out=asum2[:], in0=aden[:], scalar=EPS, in1=asum2[:],
                    op0=ALU.mult, op1=ALU.add),
                    reads=[b_aden, b_asum2], writes_d=[b_asum2])
                sc.op("A", lambda e: e.activation(
                    out=asum2[:], in_=asum2[:], func=AF.Sqrt),
                    reads=[b_asum2], writes_d=[b_asum2])
                sc.op("V", lambda e: e.reciprocal(out=asum2[:], in_=asum2[:]),
                      reads=[b_asum2], writes_d=[b_asum2])
                sc.op("V", lambda e: e.tensor_tensor(
                    out=av0, in0=av0,
                    in1=amean[:].unsqueeze(2).to_broadcast([128, NT, 128]),
                    op=ALU.subtract),
                    reads=[b_amean, b_avst], writes_d=[b_avst])
                sc.op("V", lambda e: e.tensor_tensor(
                    out=av0, in0=av0,
                    in1=asum2[:].unsqueeze(2).to_broadcast([128, NT, 128]),
                    op=ALU.mult),
                    reads=[b_asum2, b_avst], writes_d=[b_avst])
                sc.op("V", lambda e, hl=hl: e.tensor_tensor(
                    out=gatc[:], in0=av0,
                    in1=gts[:].rearrange("p s (h t) -> p s h t", h=4)[
                        :, :, hl, :],
                    op=ALU.mult),
                    reads=[b_avst, b_gts], writes=[b_gatc])
                if 'tp' in DBG_HEADS:
                    for st in range(NT):
                        sc.dma("S", gatedT[:, hl, st * 128:(st + 1) * 128],
                               gatc[:, st, :], GTS[st % 4], reads=[b_gatc],
                               writes_d=[b_gatedT], transpose=True)

            def a_heads(g):
                for hl in range(4):
                    a_qkb(hl, 0)
                    if hl > 0 and 'epi' in DBG_HEADS:
                        a_epi(hl - 1)
                    a_avb(hl, 0)
                    for c in range(1, 4):
                        a_qkb(hl, c)
                        a_avb(hl, c)
                if 'epi' in DBG_HEADS:
                    a_epi(3)

            def a_wo_prep(g):
                sc.dma("P", wo_sl[:], Wo[g], WOL, writes=[b_wo])
                for kk in range(4):
                    sc.op("V", lambda e, kk=kk: e.tensor_tensor(
                        out=wo_sl[:, kk, :], in0=wo_sl[:, kk, :], in1=gaB[:],
                        op=ALU.mult), reads=[b_gaB, b_wo], writes_d=[b_wo])
                if g == 0:
                    sc.dma("S", borow, BO, BOL, writes=[b_spl_t])
                    sc.op("V", lambda e: e.tensor_tensor(
                        out=bvrow[:], in0=borow[:, 0:512],
                        in1=gaB[0:1, 0:512], op=ALU.mult),
                        reads=[b_spl_t, b_gaB], writes=[b_bvrow])
                    sc.op("V", lambda e: e.tensor_tensor(
                        out=bgrow[:], in0=borow[:, 512:1024],
                        in1=gaB[0:1, 512:1024], op=ALU.mult),
                        reads=[b_spl_t, b_gaB], writes=[b_bgrow])

            b_pWOh = [sc.buf("pWOh0"), sc.buf("pWOh1")]

            def a_wo_tiles(g):
                for i in range(NT):
                    par = i % 2
                    for half in range(2):
                        for kk in range(4):
                            first = (kk == 0)
                            stop_here = (kk == 3 and g != 0)
                            sc.op("T", lambda e, i=i, kk=kk, half=half,
                                  stop_here=stop_here: e.matmul(
                                pWO[:, half * 512:(half + 1) * 512],
                                gatedT[:, kk, i * 128:(i + 1) * 128],
                                wo_sl[:, kk, half * 512:(half + 1) * 512],
                                start=(kk == 0), stop=stop_here),
                                reads=[b_gatedT, b_wo],
                                writes=([b_pWOh[half]] if first else []),
                                writes_d=([b_pWOh[half]] if not first else []),
                                inc=(1 if kk == 3 else 0))
                        if g == 0:
                            brow, bbrow = [(bvrow, b_bvrow),
                                           (bgrow, b_bgrow)][half]
                            sc.op("T", lambda e, brow=brow, half=half: e.matmul(
                                pWO[:, half * 512:(half + 1) * 512], onesc_b[:],
                                brow[:], start=False, stop=True),
                                reads=[b_onesb, bbrow],
                                writes_d=[b_pWOh[half]])
                        sc.op("V", lambda e, par=par, half=half: e.tensor_copy(
                            out=spl[par][:, half * 512:(half + 1) * 512],
                            in_=pWO[:, half * 512:(half + 1) * 512]),
                            reads=[b_pWOh[half]],
                            writes=([b_spl[par]] if half == 0 else []),
                            writes_d=([b_spl[par]] if half == 1 else []))
                    sc.dma("S", PG[g][i * 128:(i + 1) * 128, :], spl[par][:],
                           SPS[par], reads=[b_spl[par]])

            a_loads(0)
            a_proj(0)
            for g in range(4):
                a_wo_prep(g)
                a_heads(g)
                if g == 0:
                    a_wo_tiles(0)
                    a_loads(1)
                    a_proj(1)
                else:
                    if g + 1 < 4:
                        a_loads(g + 1)
                        a_proj(g + 1)
                    a_wo_tiles(g)
            sc.barrier()

        with ExitStack() as esF:
            det = sb(esF, "det", [128, 8, NEXP], BF16)
            ue = sb(esF, "ue", [128, 32, D], BF16)
            wqf = sb(esF, "wqf", [128, 8, 512], F32R)
            keys = sb(esF, "keys", [128, 8, 64], F32)
            sbB = sb(esF, "sbB", [128, D], F32)
            cbB = sb(esF, "cbB", [128, D], F32)
            b_det = sc.buf("det"); b_ue = sc.buf("ue"); b_wqf = sc.buf("wqf")
            b_keys = sc.buf("keys")
            b_sbB = sc.buf("sbB"); b_cbB = sc.buf("cbB")
            FL = sc.dma_sem("fl")
            mtri2 = sb(esF, "mtri2", [128, 32, 4], F32)
            b_mtri2 = sc.buf("mtri2")
            sc.load_group(FL, [
                ("P", det[:], DET, b_det),
                ("P", ue[:], UEW, b_ue),
                ("P", wqf[:], FWq, b_wqf),
                ("S", keys[:].rearrange("p a k -> p (a k)"), KEYS, b_keys),
                ("S", mtri2[:].rearrange("p a b -> p (a b)"), MTRI2, b_mtri2),
            ])
            with ExitStack() as esF0:
                modr2 = sb(esF0, "modr2", [1, D], F32R)
                b_modr2 = sc.buf("modr2")
                MRL2 = sc.dma_sem("mrl2")
                for (dstp, bdst, off) in ((sbB, b_sbB, 3 * D), (cbB, b_cbB, 4 * D)):
                    sc.dma("P", modr2[:], MODS[:, off:off + D], MRL2,
                           writes=[b_modr2])
                    bcast_tile(dstp, bdst, modr2, b_modr2, D, pWO, b_pWO)
                sc.barrier()

            pgl = sb(esF, "pgl0", [128, D], F32)
            lnh2 = sb(esF, "lnh2", [128, D], BF16)
            keep2 = sb(esF, "keep2", [128, D], F32)
            scl2 = sb(esF, "scl2", [128, 4], F32)
            h2t = sb(esF, "h2t", [128, 8, 128], BF16)
            h2tf = sb(esF, "h2tf", [128, 8, 128], F32R)
            qft = sb(esF, "qft", [128, 4, 128], F32)
            simsbd = [sb(esF, "simsb0", [128, 512], F32),
                      sb(esF, "simsb1", [128, 512], F32)]
            qfed = [s[0:64, :].rearrange("p (a b) -> p a b", a=4)
                    for s in simsbd]
            mx1 = sb(esF, "mx1", [128, 2, 4, 8], F32)
            ix1u = sb(esF, "ix1u", [128, 2, 4, 8], U32)
            ix1f = sb(esF, "ix1f", [128, 2, 4, 8], F32)
            sc8 = sb(esF, "sc8", [128, 4, 8], F32)
            pku = sb(esF, "pku", [128, 4, 8], U32)
            iu = sb(esF, "iu", [128, 4, 8], U32)
            if_ = sb(esF, "if_", [128, 4, 8], F32)
            jf = sb(esF, "jf", [128, 4, 8], F32)
            cmp8 = sb(esF, "cmp8", [128, 4, 8, 8], BF16)
            idxf = sb(esF, "idxf", [128, 4, 8], F32)
            tmp48 = sb(esF, "tmp48", [128, 4, 8], F32)
            wsm = sb(esF, "wsm", [128, 4, 8], F32)
            zrow = sb(esF, "zrow", [128, 4, 1], F32)
            eq5 = sb(esF, "eq5", [128, 32, 4, 8], F32)
            kr2 = sb(esF, "kr2", [128, 32], F32)
            rt4 = sb(esF, "rt4", [128, 32, 4], F32)
            idx16 = sb(esF, "idx16", [128, 32], I16)
            w32b = sb(esF, "w32b", [128, 32], BF16)
            qsc1 = sb(esF, "qsc1", [128, 32], F32)
            qsc2 = sb(esF, "qsc2", [128, 32], F32)
            Sbf = sb(esF, "Sbf", [128, NEXP], BF16)
            wdense = sb(esF, "wdense", [128, NEXP], BF16)
            b_pgl = sc.buf("pgl0")
            b_lnh2 = sc.buf("lnh2"); b_keep2 = sc.buf("keep2"); b_scl2 = sc.buf("scl2")
            b_h2t = sc.buf("h2t"); b_h2tf = sc.buf("h2tf"); b_qft = sc.buf("qft")
            b_simsbd = [sc.buf("simsb0"), sc.buf("simsb1")]
            b_mx1 = sc.buf("mx1"); b_ix1u = sc.buf("ix1u"); b_ix1f = sc.buf("ix1f")
            b_sc8 = sc.buf("sc8"); b_pku = sc.buf("pku")
            b_iu = sc.buf("iu"); b_if = sc.buf("if_"); b_jf = sc.buf("jf")
            b_cmp8 = sc.buf("cmp8"); b_idxf = sc.buf("idxf")
            b_tmp48 = sc.buf("tmp48"); b_wsm = sc.buf("wsm"); b_zrow = sc.buf("zrow")
            b_eq5 = sc.buf("eq5")
            b_kr2 = sc.buf("kr2"); b_rt4 = sc.buf("rt4")
            b_idx16 = sc.buf("idx16"); b_w32b = sc.buf("w32b")
            b_qsc1 = sc.buf("qsc1"); b_qsc2 = sc.buf("qsc2")
            b_Sq = [sc.buf(f"Sbf{q}") for q in range(4)]
            b_wd = [sc.buf(f"wd{ec}") for ec in range(8)]
            wdT = Sbf[:].rearrange("p (e t) -> p e t", e=32)
            b_wdT = b_Sq
            PGL0 = sc.dma_sem("pgl0s")
            PGL1 = sc.dma_sem("pgl1s")
            X2W = sc.dma_sem("x2w")
            H2W = sc.dma_sem("h2w")
            EXW = sc.dma_sem("exw")

            pF0 = psm(esF, "pF0", [128, 512], F32)
            pF1 = psm(esF, "pF1", [128, 512], F32)
            pTRf = psm(esF, "pTRf", [128, D], F32)
            pTRb = psm(esF, "pTRb", [128, 2048], BF16)
            b_pF0 = sc.buf("pF0"); b_pF1 = sc.buf("pF1")
            b_pTRf = sc.buf("pTRf"); b_pTRb = sc.buf("pTRb")

            def stP1(ti, P, bP, semA, semB):
                sc.dma("S", P[:], PG[0][ti * 128:(ti + 1) * 128, :], semA,
                       writes=[bP])
                for src in (PG[1], PG[2], PG[3], X):
                    sc.dma("P", P[:], src[ti * 128:(ti + 1) * 128, :], semB,
                           reads=[bP], writes_d=[bP], accum=True)
                sc.dma("S", X2S[ti * 128:(ti + 1) * 128, :], P[:], X2W,
                       reads=[bP])

            def stP2(ti, P, bP):
                x2s, bx2 = P, bP
                sc.op("V", lambda e: e.reduce_sum(out=scl2[:, 0:1], in_=x2s[:],
                                                  axis=AX.X),
                      reads=[bx2], writes=[b_scl2])
                sc.op("A", lambda e: e.activation(out=keep2[:], in_=x2s[:],
                                                  func=AF.Square,
                                                  accum_out=scl2[:, 1:2]),
                      reads=[bx2], writes=[b_keep2], writes_d=[b_scl2])
                sc.op("V", lambda e: e.tensor_scalar(
                    out=scl2[:, 2:3], in0=scl2[:, 0:1], scalar1=1.0 / D,
                    scalar2=None, op0=ALU.mult),
                    reads=[b_scl2], writes_d=[b_scl2])
                sc.op("V", lambda e: e.tensor_tensor(
                    out=scl2[:, 3:4], in0=scl2[:, 2:3], in1=scl2[:, 2:3],
                    op=ALU.mult), reads=[b_scl2], writes_d=[b_scl2])
                sc.op("V", lambda e: e.scalar_tensor_tensor(
                    out=scl2[:, 3:4], in0=scl2[:, 1:2], scalar=1.0 / D,
                    in1=scl2[:, 3:4], op0=ALU.mult, op1=ALU.subtract),
                    reads=[b_scl2], writes_d=[b_scl2])
                sc.op("A", lambda e: e.activation(out=scl2[:, 3:4],
                                                  in_=scl2[:, 3:4],
                                                  func=AF.Sqrt, bias=epsc[:]),
                      reads=[b_scl2, b_epsc], writes_d=[b_scl2])
                sc.op("V", lambda e: e.reciprocal(out=scl2[:, 3:4],
                                                  in_=scl2[:, 3:4]),
                      reads=[b_scl2], writes_d=[b_scl2])
                sc.op("V", lambda e: e.tensor_scalar(
                    out=keep2[:], in0=x2s[:], scalar1=scl2[:, 2:3],
                    scalar2=scl2[:, 3:4], op0=ALU.subtract, op1=ALU.mult),
                    reads=[bx2, b_scl2], writes=[b_keep2])
                sc.op("V", lambda e: e.tensor_tensor(
                    out=keep2[:], in0=keep2[:], in1=cbB[:], op=ALU.mult),
                    reads=[b_keep2, b_cbB], writes_d=[b_keep2])
                sc.op("V", lambda e: e.tensor_tensor(
                    out=keep2[:], in0=keep2[:], in1=sbB[:], op=ALU.add),
                    reads=[b_keep2, b_sbB], writes_d=[b_keep2])
                sc.op("V", lambda e: e.tensor_copy(out=lnh2[:], in_=keep2[:]),
                      reads=[b_keep2], writes=[b_lnh2])

            def stP3(ti, hp, bhp):
                for k in range(8):
                    sc.op("T", lambda e, k=k: e.transpose(
                        pTRb[:, k * 128:(k + 1) * 128],
                        lnh2[:, k * 128:(k + 1) * 128], idnb[:]),
                        reads=[b_lnh2, b_idnb],
                        writes=([b_pTRb] if k == 0 else []),
                        writes_d=([b_pTRb] if k > 0 else []),
                        inc=(1 if k == 7 else 0))
                sc.op("A", lambda e, hp=hp: e.activation(
                    out=hp[:].rearrange("p k t -> p (k t)"),
                    in_=pTRb[:, 0:1024], func=AF.Copy),
                    reads=[b_pTRb], writes=[bhp])
                sc.dma("S", H2S[:, ti, :, :], hp[:], H2W, reads=[bhp])
                for k in range(8):
                    sc.op("T", lambda e, k=k: e.transpose(
                        pTRf[:, k * 128:(k + 1) * 128],
                        keep2[:, k * 128:(k + 1) * 128], idnf[:]),
                        reads=[b_keep2, b_idnf],
                        writes=([b_pTRf] if k == 0 else []),
                        writes_d=([b_pTRf] if k > 0 else []),
                        inc=(1 if k == 7 else 0))
                sc.op("V", lambda e: e.tensor_copy(
                    out=h2tf[:].rearrange("p k t -> p (k t)"), in_=pTRf[:]),
                    reads=[b_pTRf], writes=[b_h2tf])

            def stP4(ti):
                simsb, b_simsb = simsbd[ti % 2], b_simsbd[ti % 2]
                qfe = qfed[ti % 2]
                for m in range(4):
                    for k in range(8):
                        sc.op("T", lambda e, m=m, k=k: e.matmul(
                            pTRf[:, m * 128:(m + 1) * 128],
                            wqf[:, k, m * 128:(m + 1) * 128],
                            h2tf[:, k, :], start=(k == 0), stop=(k == 7)),
                            reads=[b_wqf, b_h2tf],
                            writes=([b_pTRf] if m == 0 and k == 0 else []),
                            writes_d=([b_pTRf] if m > 0 or k > 0 else []),
                            inc=(1 if k == 7 else 0))
                sc.op("V", lambda e: e.tensor_copy(
                    out=qft[:].rearrange("p m t -> p (m t)"),
                    in_=pTRf[:, 0:512]),
                    reads=[b_pTRf], writes=[b_qft])
                for hh in range(4):
                    p_, h_ = hh // 2, (hh % 2) * 2 + 1
                    rt = p_ * 2 + h_ // 2
                    sc.op("V", lambda e, hh=hh, rt=rt: e.tensor_copy(
                        out=qfe[:, hh, :], in_=qft[64:128, rt, :]),
                        reads=[b_qft], writes_d=[b_simsb])
                for ph in range(8):
                    p_, h_ = ph // 4, ph % 4
                    rt = p_ * 2 + h_ // 2
                    if h_ % 2 == 0:
                        lh = (lambda e, ph=ph, rt=rt: e.matmul(
                            pTRf[:, 512 + ph * 64:576 + ph * 64],
                            qft[0:64, rt, :], keys[0:64, ph, :],
                            start=True, stop=True))
                    else:
                        hh = p_ * 2 + (h_ - 1) // 2
                        lh = (lambda e, ph=ph, hh=hh: e.matmul(
                            pTRf[:, 512 + ph * 64:576 + ph * 64],
                            qfe[:, hh, :], keys[0:64, ph, :],
                            start=True, stop=True))
                    sc.op("T", lh,
                        reads=[b_qft, b_simsb, b_keys],
                        writes_d=[b_pTRf],
                        inc=(1 if ph == 7 else 0))
                sc.op("V", lambda e: e.tensor_copy(out=simsb[:],
                                                   in_=pTRf[:, 512:1024]),
                      reads=[b_pTRf], writes=[b_simsb])

            def stP5(ti):
                simsb, b_simsb = simsbd[ti % 2], b_simsbd[ti % 2]
                all64 = eq5[:].rearrange("p a b c -> p (a b c)")[
                    :, 0:256].rearrange("p (h i) -> p h i", h=4)
                b_all64 = b_eq5
                for ph in range(8):
                    p_, h_ = ph // 4, ph % 4
                    sc.op("V", lambda e, ph=ph, p_=p_, h_=h_: e.max(
                        out=mx1[:, p_, h_, :], in_=simsb[:, ph * 64:(ph + 1) * 64]),
                        reads=[b_simsb], writes_d=[b_mx1])
                for ph in range(8):
                    p_, h_ = ph // 4, ph % 4
                    sc.op("V", lambda e, ph=ph, p_=p_, h_=h_: e.max_index(
                        out=ix1u[:, p_, h_, :], in_max=mx1[:, p_, h_, :],
                        in_values=simsb[:, ph * 64:(ph + 1) * 64]),
                        reads=[b_simsb, b_mx1], writes_d=[b_ix1u])
                sc.op("V", lambda e: e.tensor_copy(
                    out=ix1f[:].rearrange("p a h k -> p (a h k)"),
                    in_=ix1u[:].rearrange("p a h k -> p (a h k)")),
                    reads=[b_ix1u], writes=[b_ix1f])
                sc.op("V", lambda e: e.tensor_tensor(
                    out=all64[:].rearrange("p h (i j) -> p h i j", i=8),
                    in0=mx1[:, 0, :, :].unsqueeze(3).to_broadcast(
                        [128, 4, 8, 8]),
                    in1=mx1[:, 1, :, :].unsqueeze(2).to_broadcast(
                        [128, 4, 8, 8]),
                    op=ALU.add),
                    reads=[b_mx1], writes=[b_all64])
                for h_ in range(4):
                    sc.op("V", lambda e, h_=h_: e.max(
                        out=sc8[:, h_, :], in_=all64[:, h_, :]),
                        reads=[b_all64], writes_d=[b_sc8])
                for h_ in range(4):
                    sc.op("V", lambda e, h_=h_: e.max_index(
                        out=pku[:, h_, :], in_max=sc8[:, h_, :],
                        in_values=all64[:, h_, :]),
                        reads=[b_all64, b_sc8], writes_d=[b_pku])
                sc.op("V", lambda e: e.tensor_scalar(
                    out=iu[:].rearrange("p h k -> p (h k)"),
                    in0=pku[:].rearrange("p h k -> p (h k)"),
                    scalar1=3, scalar2=None, op0=ALU.logical_shift_right),
                    reads=[b_pku], writes=[b_iu])
                sc.op("V", lambda e: e.tensor_copy(
                    out=if_[:].rearrange("p h k -> p (h k)"),
                    in_=iu[:].rearrange("p h k -> p (h k)")),
                    reads=[b_iu], writes=[b_if])
                sc.op("V", lambda e: e.tensor_scalar(
                    out=iu[:].rearrange("p h k -> p (h k)"),
                    in0=pku[:].rearrange("p h k -> p (h k)"),
                    scalar1=7, scalar2=None, op0=ALU.bitwise_and),
                    reads=[b_pku, b_iu], writes_d=[b_iu])
                sc.op("V", lambda e: e.tensor_copy(
                    out=jf[:].rearrange("p h k -> p (h k)"),
                    in_=iu[:].rearrange("p h k -> p (h k)")),
                    reads=[b_iu], writes=[b_jf])

                def gather8(valsrc, possrc, bpos, dst, bdst):
                    sc.op("V", lambda e, possrc=possrc: e.tensor_tensor(
                        out=cmp8[:],
                        in0=iota8[:].unsqueeze(1).unsqueeze(2).to_broadcast(
                            [128, 4, 8, 8]),
                        in1=possrc[:].unsqueeze(3).to_broadcast(
                            [128, 4, 8, 8]),
                        op=ALU.is_equal),
                        reads=[bpos, b_iota8], writes=[b_cmp8])
                    sc.op("V", lambda e, valsrc=valsrc: e.tensor_tensor(
                        out=cmp8[:], in0=cmp8[:],
                        in1=valsrc.unsqueeze(2).to_broadcast(
                            [128, 4, 8, 8]),
                        op=ALU.mult),
                        reads=[b_ix1f, b_cmp8], writes_d=[b_cmp8])
                    sc.op("V", lambda e, dst=dst: e.reduce_sum(
                        out=dst[:], in_=cmp8[:], axis=AX.X),
                        reads=[b_cmp8], writes=[bdst])
                gather8(ix1f[:, 0, :, :], if_, b_if, idxf, b_idxf)
                sc.op("V", lambda e: e.tensor_scalar(
                    out=idxf[:].rearrange("p h k -> p (h k)"),
                    in0=idxf[:].rearrange("p h k -> p (h k)"),
                    scalar1=64.0, scalar2=None, op0=ALU.mult),
                    reads=[b_idxf], writes_d=[b_idxf])
                gather8(ix1f[:, 1, :, :], jf, b_jf, tmp48, b_tmp48)
                sc.op("V", lambda e: e.tensor_tensor(
                    out=idxf[:].rearrange("p h k -> p (h k)"),
                    in0=idxf[:].rearrange("p h k -> p (h k)"),
                    in1=tmp48[:].rearrange("p h k -> p (h k)"), op=ALU.add),
                    reads=[b_tmp48, b_idxf], writes_d=[b_idxf])
                sc.op("V", lambda e: e.tensor_tensor(
                    out=wsm[:], in0=sc8[:],
                    in1=sc8[:, :, 0:1].to_broadcast([128, 4, 8]),
                    op=ALU.subtract),
                    reads=[b_sc8], writes=[b_wsm])
                sc.op("A", lambda e: e.activation(
                    out=wsm[:].rearrange("p h k -> p (h k)"),
                    in_=wsm[:].rearrange("p h k -> p (h k)"), func=AF.Exp),
                    reads=[b_wsm], writes_d=[b_wsm])
                sc.op("V", lambda e: e.reduce_sum(out=zrow[:], in_=wsm[:],
                                                  axis=AX.X),
                      reads=[b_wsm], writes=[b_zrow])
                sc.op("V", lambda e: e.reciprocal(
                    out=zrow[:].rearrange("p h o -> p (h o)"),
                    in_=zrow[:].rearrange("p h o -> p (h o)")),
                    reads=[b_zrow], writes_d=[b_zrow])
                sc.op("V", lambda e: e.tensor_tensor(
                    out=wsm[:], in0=wsm[:],
                    in1=zrow[:].to_broadcast([128, 4, 8]), op=ALU.mult),
                    reads=[b_zrow, b_wsm], writes_d=[b_wsm])
                idxfl = idxf[:].rearrange("p h k -> p (h k)")
                sc.op("V", lambda e: e.tensor_tensor(
                    out=eq5[:],
                    in0=idxfl.unsqueeze(2).unsqueeze(3).to_broadcast(
                        [128, 32, 4, 8]),
                    in1=idxf[:].unsqueeze(1).to_broadcast([128, 32, 4, 8]),
                    op=ALU.is_equal),
                    reads=[b_idxf], writes=[b_eq5])
                sc.op("V", lambda e: e.reduce_sum(out=rt4[:], in_=eq5[:],
                                                  axis=AX.X),
                      reads=[b_eq5], writes=[b_rt4])
                sc.op("V", lambda e: e.reduce_sum(out=kr2[:], in_=rt4[:],
                                                  axis=AX.X),
                      reads=[b_rt4], writes=[b_kr2])
                sc.op("V", lambda e: e.tensor_tensor(
                    out=eq5[:], in0=eq5[:],
                    in1=mtri2[:].unsqueeze(3).to_broadcast([128, 32, 4, 8]),
                    op=ALU.mult),
                    reads=[b_eq5, b_mtri2], writes_d=[b_eq5])
                sc.op("V", lambda e: e.reduce_sum(out=rt4[:], in_=eq5[:],
                                                  axis=AX.X),
                      reads=[b_eq5], writes=[b_rt4])
                sc.op("V", lambda e: e.reduce_sum(
                    out=tmp48[:].rearrange("p h k -> p (h k)"), in_=rt4[:],
                    axis=AX.X),
                      reads=[b_rt4], writes=[b_tmp48])
                sc.op("V", lambda e: e.tensor_tensor(
                    out=kr2[:], in0=kr2[:],
                    in1=tmp48[:].rearrange("p h k -> p (h k)"),
                    op=ALU.subtract),
                    reads=[b_tmp48, b_kr2], writes_d=[b_kr2])
                sc.op("V", lambda e: e.tensor_scalar(
                    out=kr2[:], in0=kr2[:],
                    scalar1=1.0, scalar2=None, op0=ALU.is_gt),
                    reads=[b_kr2], writes_d=[b_kr2])
                sc.op("V", lambda e: e.tensor_tensor(
                    out=eq5[:], in0=eq5[:],
                    in1=wsm[:].unsqueeze(1).to_broadcast([128, 32, 4, 8]),
                    op=ALU.mult),
                    reads=[b_eq5, b_wsm], writes_d=[b_eq5])
                sc.op("V", lambda e: e.reduce_sum(out=rt4[:], in_=eq5[:],
                                                  axis=AX.X),
                      reads=[b_eq5], writes=[b_rt4])
                sc.op("V", lambda e: e.reduce_sum(
                    out=tmp48[:].rearrange("p h k -> p (h k)"), in_=rt4[:],
                    axis=AX.X),
                      reads=[b_rt4], writes=[b_tmp48])
                sc.op("V", lambda e: e.tensor_tensor(
                    out=wsm[:], in0=wsm[:],
                    in1=tmp48[:], op=ALU.add),
                    reads=[b_tmp48, b_wsm], writes_d=[b_wsm])
                sc.op("V", lambda e: e.tensor_scalar(
                    out=tmp48[:].rearrange("p h k -> p (h k)"),
                    in0=idxf[:].rearrange("p h k -> p (h k)"),
                    scalar1=1.0, scalar2=None, op0=ALU.add),
                    reads=[b_idxf, b_tmp48], writes_d=[b_tmp48])
                sc.op("V", lambda e: e.tensor_tensor(
                    out=tmp48[:].rearrange("p h k -> p (h k)"),
                    in0=tmp48[:].rearrange("p h k -> p (h k)"),
                    in1=kr2[:], op=ALU.mult),
                    reads=[b_kr2, b_tmp48], writes_d=[b_tmp48])
                sc.op("V", lambda e: e.tensor_tensor(
                    out=idxf[:], in0=idxf[:], in1=tmp48[:], op=ALU.subtract),
                    reads=[b_tmp48, b_idxf], writes_d=[b_idxf])

            def stP6(ti):
                sc.op("V", lambda e: e.tensor_copy(
                    out=w32b[:], in_=wsm[:].rearrange("p h k -> p (h k)")),
                    reads=[b_wsm], writes=[b_w32b])
                idxflat = idxf[:].rearrange("p h k -> p (h k)")
                for qd in range(4):
                    sc.op("V", lambda e, qd=qd: e.tensor_scalar(
                        out=qsc1[:], in0=idxflat, scalar1=float(1024 * qd),
                        scalar2=None, op0=ALU.subtract),
                        reads=[b_idxf], writes=[b_qsc1])
                    sc.op("V", lambda e: e.tensor_scalar(
                        out=qsc2[:], in0=qsc1[:], scalar1=0.0, scalar2=None,
                        op0=ALU.is_ge),
                        reads=[b_qsc1], writes=[b_qsc2])
                    sc.op("V", lambda e: e.scalar_tensor_tensor(
                        out=qsc2[:], in0=qsc1[:], scalar=1024.0, in1=qsc2[:],
                        op0=ALU.is_lt, op1=ALU.mult),
                        reads=[b_qsc1, b_qsc2], writes_d=[b_qsc2])
                    sc.op("V", lambda e: e.tensor_scalar(
                        out=qsc1[:], in0=qsc1[:], scalar1=1.0, scalar2=None,
                        op0=ALU.add),
                        reads=[b_qsc1], writes_d=[b_qsc1])
                    sc.op("V", lambda e: e.tensor_tensor(
                        out=qsc1[:], in0=qsc1[:], in1=qsc2[:], op=ALU.mult),
                        reads=[b_qsc1, b_qsc2], writes_d=[b_qsc1])
                    sc.op("V", lambda e: e.tensor_scalar(
                        out=qsc1[:], in0=qsc1[:], scalar1=-1.0, scalar2=None,
                        op0=ALU.add),
                        reads=[b_qsc1], writes_d=[b_qsc1])
                    sc.op("V", lambda e: e.tensor_copy(out=idx16[:], in_=qsc1[:]),
                          reads=[b_qsc1], writes=[b_idx16])
                    sc.op("P", lambda e, qd=qd: e.local_scatter(
                        out_ap=Sbf[:, qd * 1024:(qd + 1) * 1024],
                        data_ap=w32b[:], idxs_ap=idx16[:],
                        channels=128, num_elems=1024, num_idxs=32),
                        reads=[b_w32b, b_idx16], writes=[b_Sq[qd]])

            def stP7a(ti, hp, bhp):
                for ec in range(8):
                    pe_ = [pF0, pF1][ec % 2]
                    bpe = [b_pF0, b_pF1][ec % 2]
                    for k in range(8):
                        sc.op("T", lambda e, ec=ec, k=k, pe_=pe_, hp=hp: e.matmul(
                            pe_[:], hp[:, k, :],
                            det[:, k, ec * 512:(ec + 1) * 512],
                            start=(k == 0), stop=(k == 7)),
                            reads=[bhp, b_det],
                            writes=([bpe] if k == 0 else []),
                            writes_d=([bpe] if k > 0 else []),
                            inc=(1 if k == 7 else 0))
                    sc.op("A", lambda e, ec=ec, pe_=pe_: e.activation(
                        out=wdense[:, ec * 512:(ec + 1) * 512], in_=pe_[:],
                        func=AF.Silu),
                        reads=[bpe], writes=[b_wd[ec]])

            def stP7b(ti):
                for ec in range(8):
                    sc.op("V", lambda e, ec=ec: e.tensor_tensor(
                        out=wdense[:, ec * 512:(ec + 1) * 512],
                        in0=wdense[:, ec * 512:(ec + 1) * 512],
                        in1=Sbf[:, ec * 512:(ec + 1) * 512], op=ALU.mult),
                        reads=[b_wd[ec], b_Sq[ec // 2]], writes_d=[b_wd[ec]])

            def stP8(ti, xv, bxv):
                for half in range(2):
                    for kb in range(16):
                        sc.op("T", lambda e, half=half, kb=kb: e.transpose(
                            pTRb[:, kb * 128:(kb + 1) * 128],
                            wdense[:, half * 2048 + kb * 128:
                                   half * 2048 + (kb + 1) * 128], idnb[:]),
                            reads=[b_wd[half * 4 + kb // 4], b_idnb],
                            writes=([b_pTRb] if kb == 0 else []),
                            writes_d=([b_pTRb] if kb > 0 else []),
                            inc=(1 if kb % 4 == 3 else 0))
                    sc.op("V", lambda e, half=half: e.tensor_copy(
                        out=wdT[:, half * 16:(half + 1) * 16, :].rearrange(
                            "p e t -> p (e t)"),
                        in_=pTRb[:]),
                        reads=[b_pTRb],
                        writes=[b_Sq[half * 2], b_Sq[half * 2 + 1]])
                for et in range(32):
                    for half in range(2):
                        first = (et == 0 and half == 0)
                        last = (et == 31 and half == 1)
                        sc.op("T", lambda e, et=et, half=half, first=first: e.matmul(
                            pWO[:, half * 512:(half + 1) * 512],
                            wdT[:, et, :], ue[:, et, half * 512:(half + 1) * 512],
                            start=(et == 0), stop=(et == 31)),
                            reads=[b_Sq[et // 8], b_ue],
                            writes=([b_pWO] if first else []),
                            writes_d=([b_pWO] if not first else []),
                            inc=(1 if last else 0))
                sc.op("V", lambda e, xv=xv: e.tensor_copy(out=xv[:], in_=pWO[:]),
                      reads=[b_pWO], writes=[bxv])
                sc.dma("S", EXS[ti * 128:(ti + 1) * 128, :], xv[:], EXW,
                       reads=[bxv])

            pgl2 = sb(esF, "pgl2", [128, D], F32)
            b_pgl2 = sc.buf("pgl2")
            h2t2 = sb(esF, "h2t2", [128, 8, 128], BF16)
            b_h2t2 = sc.buf("h2t2")
            PGL2 = sc.dma_sem("pgl2s")
            PGL3 = sc.dma_sem("pgl3s")
            PP = [(pgl, b_pgl, PGL0, PGL1), (pgl2, b_pgl2, PGL2, PGL3)]
            HH = [(h2t, b_h2t), (h2t2, b_h2t2)]

            stP1(0, *PP[0][0:2], PP[0][2], PP[0][3])
            for ti in range(NT + 1):
                par = ti % 2
                if ti < NT:
                    stP2(ti, PP[par][0], PP[par][1])
                    if ti + 1 < NT:
                        stP1(ti + 1, *PP[1 - par][0:2], PP[1 - par][2],
                             PP[1 - par][3])
                if ti > 0:
                    stP7a(ti - 1, *HH[1 - par])
                    stP5(ti - 1)
                    stP6(ti - 1)
                    stP7b(ti - 1)
                if ti < NT:
                    stP3(ti, *HH[par])
                    stP4(ti)
                if ti > 0:
                    stP8(ti - 1, PP[par][0], PP[par][1])
            sc.barrier()

        with ExitStack() as esG:
            gbB2 = sb(esG, "gbB2", [128, D], F32)
            modr3 = sb(esG, "modr3", [1, D], F32R)
            fbg = sb(esG, "fbg", [128, 32], F32)
            fbu = sb(esG, "fbu", [128, 32], F32)
            bdrow = sb(esG, "bdrow", [1, D], F32)
            bdrow_b = sb(esG, "bdrow_b", [1, D], BF16)
            h2cd = [sb(esG, "h2c0", [128, 4, 8, 128], BF16),
                    sb(esG, "h2c1", [128, 4, 8, 128], BF16)]
            wgc = [sb(esG, "wgc0", [128, 4, 8, 128], BF16),
                   sb(esG, "wgc1", [128, 4, 8, 128], BF16)]
            wuc = [sb(esG, "wuc0", [128, 4, 8, 128], BF16),
                   sb(esG, "wuc1", [128, 4, 8, 128], BF16)]
            wdc = [sb(esG, "wdc0", [128, 4, 512], BF16),
                   sb(esG, "wdc1", [128, 4, 512], BF16)]
            silu_g = sb(esG, "silu_g", [128, 512], BF16)
            etile = sb(esG, "etile", [128, 32, 512], BF16)
            x2l = sb(esG, "x2l", [128, D], F32)
            exl = sb(esG, "exl", [128, D], F32)
            outs = sb(esG, "outs", [128, 4, D], F32)
            outb = [sb(esG, "outb0", [128, D], F32), sb(esG, "outb1", [128, D], F32)]
            b_gbB2 = sc.buf("gbB2"); b_modr3 = sc.buf("modr3")
            b_fbg = sc.buf("fbg"); b_fbu = sc.buf("fbu")
            b_bdrow = sc.buf("bdrow"); b_bdrowb = sc.buf("bdrow_b")
            b_h2cd = [sc.buf("h2c0"), sc.buf("h2c1")]
            b_wgc = [sc.buf("wgc0"), sc.buf("wgc1")]
            b_wuc = [sc.buf("wuc0"), sc.buf("wuc1")]
            b_wdc = [sc.buf("wdc0"), sc.buf("wdc1")]
            b_silug = sc.buf("silu_g"); b_etile = sc.buf("etile")
            b_x2l = sc.buf("x2l"); b_exl = sc.buf("exl")
            b_outs = sc.buf("outs")
            b_outb = [sc.buf("outb0"), sc.buf("outb1")]
            GL = sc.dma_sem("gl")
            H2L = sc.dma_sem("h2l")
            X2L = sc.dma_sem("x2l_s")
            EXL = sc.dma_sem("exl_s")
            WGS = [sc.dma_sem("wgs0"), sc.dma_sem("wgs1")]
            WUS = [sc.dma_sem("wus0"), sc.dma_sem("wus1")]
            WDS = [sc.dma_sem("wds0"), sc.dma_sem("wds1")]
            OST = [sc.dma_sem("ost0"), sc.dma_sem("ost1")]

            pG = psm(esG, "pG", [128, 512], F32)
            pU = psm(esG, "pU", [128, 512], F32)
            pM2 = [psm(esG, f"pM2{j}", [128, 512], F32) for j in range(4)]
            b_pG = sc.buf("pG"); b_pU = sc.buf("pU")
            b_pM2 = [sc.buf(f"pM2{j}") for j in range(4)]

            sc.load_group(GL, [
                ("S", fbg[:], FBg, b_fbg),
                ("S", fbu[:], FBu, b_fbu),
                ("S", bdrow[:], FBd, b_bdrow),
                ("P", modr3[:], MODS[:, 5 * D:6 * D], b_modr3),
            ])
            bcast_tile(gbB2, b_gbB2, modr3, b_modr3, D, pWO, b_pWO)
            sc.op("V", lambda e: e.tensor_copy(out=bdrow_b[:], in_=bdrow[:]),
                  reads=[b_bdrow], writes=[b_bdrowb])

            FCH2 = 512

            def ld_gu(m, par):
                sc.dma("P", wgc[par][:],
                       FWg.rearrange("m p k n -> p m k n")[:, m:m + 4],
                       WGS[par], writes=[b_wgc[par]])
                sc.dma("P", wuc[par][:],
                       FWu.rearrange("m p k n -> p m k n")[:, m:m + 4],
                       WUS[par], writes=[b_wuc[par]])

            ld_gu(0, 0)
            sc.dma("S", h2cd[0][:], H2S[:, 0:4, :, :], H2L, writes=[b_h2cd[0]])
            for ci in range(S // FCH2):
                t0 = ci * 4
                h2c, b_h2c = h2cd[ci % 2], b_h2cd[ci % 2]
                for m in range(32):
                    par = (m // 4) % 2
                    mm = m % 4
                    if mm == 0:
                        if m + 4 < 32:
                            ld_gu(m + 4, 1 - par)
                        elif ci + 1 < S // FCH2:
                            ld_gu(0, 0)
                    for k in range(8):
                        sc.op("T", lambda e, k=k, par=par, mm=mm, h2c=h2c:
                              e.matmul(
                            pG[:], wgc[par][:, mm, k, :],
                            h2c[:, :, k, :], start=(k == 0), stop=(k == 7)),
                            reads=[b_wgc[par], b_h2c],
                            writes=([b_pG] if k == 0 else []),
                            writes_d=([b_pG] if k > 0 else []),
                            inc=(1 if k == 7 else 0))
                    for k in range(8):
                        sc.op("T", lambda e, k=k, par=par, mm=mm, h2c=h2c:
                              e.matmul(
                            pU[:], wuc[par][:, mm, k, :],
                            h2c[:, :, k, :], start=(k == 0), stop=(k == 7)),
                            reads=[b_wuc[par], b_h2c],
                            writes=([b_pU] if k == 0 else []),
                            writes_d=([b_pU] if k > 0 else []),
                            inc=(1 if k == 7 else 0))
                    sc.op("A", lambda e, m=m: e.activation(
                        out=silu_g[:], in_=pG[:], func=AF.Silu,
                        bias=fbg[:, m:m + 1]),
                        reads=[b_pG, b_fbg], writes=[b_silug])
                    sc.op("V", lambda e, m=m: e.scalar_tensor_tensor(
                        out=etile[:, m, :], in0=pU[:], scalar=fbu[:, m:m + 1],
                        in1=silu_g[:], op0=ALU.add, op1=ALU.mult),
                        reads=[b_pU, b_fbu, b_silug], writes_d=[b_etile])
                if ci + 1 < S // FCH2:
                    sc.dma("S", h2cd[(ci + 1) % 2][:],
                           H2S[:, t0 + 4:t0 + 8, :, :], H2L,
                           writes=[b_h2cd[(ci + 1) % 2]])
                def ld_wd(m, wpar, half):
                    sc.dma("P", wdc[wpar][:],
                           FWd.rearrange("m p n -> p m n")[
                               :, m:m + 4, half * 512:(half + 1) * 512],
                           WDS[wpar], writes=[b_wdc[wpar]])

                ld_wd(0, 0, 0)
                for half in range(2):
                    for m in range(32):
                        wpar = (m // 4) % 2
                        mm = m % 4
                        if mm == 0:
                            if m + 4 < 32:
                                ld_wd(m + 4, 1 - wpar, half)
                            elif half == 0:
                                ld_wd(0, 0, 1)
                        for tt in range(4):
                            first = (m == 0)
                            sc.op("T", lambda e, m=m, tt=tt, wpar=wpar,
                                  mm=mm: e.matmul(
                                pM2[tt][:],
                                etile[:, m, tt * 128:(tt + 1) * 128],
                                wdc[wpar][:, mm, :], start=(m == 0), stop=False),
                                reads=[b_etile, b_wdc[wpar]],
                                writes=([b_pM2[tt]] if first else []),
                                writes_d=([b_pM2[tt]] if not first else []),
                                inc=(1 if tt == 3 else 0))
                    for tt in range(4):
                        sc.op("T", lambda e, tt=tt, half=half: e.matmul(
                            pM2[tt][:], onesc_b[:],
                            bdrow_b[:, half * 512:(half + 1) * 512],
                            start=False, stop=True),
                            reads=[b_onesb, b_bdrowb], writes_d=[b_pM2[tt]])
                        sc.op("V", lambda e, tt=tt, half=half: e.tensor_copy(
                            out=outs[:, tt, half * 512:(half + 1) * 512],
                            in_=pM2[tt][:]),
                            reads=[b_pM2[tt]], writes_d=[b_outs])
                for tt in range(4):
                    ti = t0 + tt
                    par = ti % 2
                    sc.dma("S", x2l[:], X2S[ti * 128:(ti + 1) * 128, :], X2L,
                           writes=[b_x2l])
                    sc.dma("S", exl[:], EXS[ti * 128:(ti + 1) * 128, :], EXL,
                           writes=[b_exl])
                    sc.op("V", lambda e, par=par, tt=tt: e.tensor_tensor(
                        out=outb[par][:], in0=exl[:], in1=outs[:, tt, :],
                        op=ALU.add),
                        reads=[b_x2l, b_exl, b_outs], writes=[b_outb[par]])
                    sc.op("V", lambda e, par=par: e.tensor_tensor(
                        out=outb[par][:], in0=outb[par][:], in1=gbB2[:],
                        op=ALU.mult),
                        reads=[b_gbB2, b_outb[par]], writes_d=[b_outb[par]])
                    sc.op("V", lambda e, par=par: e.tensor_tensor(
                        out=outb[par][:], in0=outb[par][:], in1=x2l[:],
                        op=ALU.add),
                        reads=[b_x2l, b_outb[par]], writes_d=[b_outb[par]])
                    sc.dma("S", OUT[ti * 128:(ti + 1) * 128, :], outb[par][:],
                           OST[par], reads=[b_outb[par]])
            sc.barrier()


        block = es.enter_context(nc.Block())
        sc.emit(block, upto=upto)
    return nc


def make_in_maps(inputs, ncores=B):
    t = host_tables()
    x = np.ascontiguousarray(inputs["L0_attn_x"], dtype=np.float32)
    c = np.ascontiguousarray(inputs["L0_c"], dtype=np.float32)

    def dmaj(v, cols):
        return np.ascontiguousarray(v.reshape(cols, 128).T)

    def wblk(w, G, N):
        return np.ascontiguousarray(
            np.asarray(w, np.float32).reshape(8, 128, G, N).transpose(2, 1, 0, 3))

    keys = inputs["L0_ffn_keys"]
    keysT = np.transpose(keys, (3, 2, 0, 1)).reshape(64, 8 * 64)
    keysT = np.ascontiguousarray(np.concatenate([keysT, keysT], axis=0))
    det = np.ascontiguousarray(
        inputs["L0_ffn_down_embed"].T.reshape(8, 128, 4096).transpose(1, 0, 2))
    ue = np.ascontiguousarray(
        np.asarray(inputs["L0_ffn_up_embed"], np.float32).reshape(32, 128, 1024)
        .transpose(1, 0, 2))
    wo = np.ascontiguousarray(
        np.asarray(inputs["L0_attn_Wo"], np.float32).reshape(4, 4, 128, 1024)
        .transpose(0, 2, 1, 3))
    fwq = np.ascontiguousarray(
        np.asarray(inputs["L0_ffn_Wq"], np.float32).reshape(8, 128, 512)
        .transpose(1, 0, 2))
    fwd = np.ascontiguousarray(
        np.asarray(inputs["L0_ffn_Wd"], np.float32).reshape(32, 128, 1024))
    wmod = np.ascontiguousarray(
        np.asarray(inputs["L0_wmod"], np.float32).reshape(8, 128, 12, 512)
        .transpose(2, 1, 0, 3))

    common = {
        "Wq": wblk(inputs["L0_attn_Wq"], 4, 256),
        "Wk": wblk(inputs["L0_attn_Wk"], 4, 256),
        "Wv": wblk(inputs["L0_attn_Wv"], 4, 512),
        "Wg": wblk(inputs["L0_attn_Wg"], 4, 512),
        "Wo": wo,
        "BQ": dmaj(inputs["L0_attn_bq"], 8), "BK": dmaj(inputs["L0_attn_bk"], 8),
        "BV": inputs["L0_attn_bv"].reshape(1, -1),
        "BGA": inputs["L0_attn_bg"].reshape(1, -1),
        "BO": inputs["L0_attn_bo"].reshape(1, -1),
        "FWg": wblk(inputs["L0_ffn_Wg"], 32, 128),
        "FWu": wblk(inputs["L0_ffn_Wu"], 32, 128),
        "FWd": fwd,
        "FBg": dmaj(inputs["L0_ffn_bg"], 32), "FBu": dmaj(inputs["L0_ffn_bu"], 32),
        "FBd": inputs["L0_ffn_bd"].reshape(1, -1),
        "FWq": fwq,
        "KEYS": keysT, "DET": det, "UEW": ue,
        "WMOD": wmod,
        "COSB": t["COSB"], "SINB": t["SINB"], "QS16": t["QS16"],
        "KS16": t["KS16"],
        "SELQ": t["SELQ"].reshape(16, 8 * 128),
        "MASKS": t["MASKS"].reshape(128, 4 * 512),
        "PT": t["PT"], "IDNB": t["IDNB"], "IDNF": np.eye(128, dtype=np.float32),
        "ONESC": t["ONESC"], "IOTA8": t["IOTA8"], "MTRI2": t["MTRI2"],
    }
    in_maps = []
    for b in range(ncores):
        m = dict(common)
        m["X"] = np.ascontiguousarray(x[b])
        m["CC"] = dmaj(c[b], 8)
        in_maps.append(m)
    return in_maps


def kernel(**inputs):
    nc = build_program(dbg=DEBUG)
    lower_extended_insts(nc)
    in_maps = make_in_maps(inputs, B)
    res = run_bass_kernel_spmd(nc, in_maps, core_ids=list(range(B)))
    out = np.stack([res.results[b]["OUT"] for b in range(B)], axis=0)
    return out.astype(np.float32)


# version 21
# speedup vs baseline: 1.0288x; 1.0288x over previous
import numpy as np
import ml_dtypes
from contextlib import ExitStack

import concourse.bass as bass
import concourse.mybir as mybir
from concourse.bass_utils import run_bass_kernel_spmd
from concourse.library_overlay import lower_extended_insts
from concourse.library_config import local_scatter as LS_LIB

F32 = mybir.dt.float32
F32R = mybir.dt.float32r
BF16 = mybir.dt.bfloat16
I16 = mybir.dt.int16
U32 = mybir.dt.uint32
ALU = mybir.AluOpType
AF = mybir.ActivationFunctionType
AX = mybir.AxisListType

B, S, D = 8, 2048, 1024
H, KD, HD = 16, 64, 128
DV = 2048
DCD = 4096
NEXP = 4096
NT = S // 128
EPS = 1e-5
FCH = 256
NCH = S // FCH
NTC = FCH // 128

DEBUG = False
DBG_BARRIERS = False
DBG_HEADS = 'qk,evac,av,epi,tp'
DBG_EPI_N = 10**9
DBG_F1_N = 10**9
DBG_F1_TRACE = False


def host_tables():
    t = {}
    f32 = np.float32
    angle = (1.0 / (10000.0 ** np.linspace(0.0, 1.0, KD // 2, dtype=f32))).astype(f32)
    angle = np.repeat(angle, 2).astype(f32)
    idx = np.arange(S, dtype=f32)
    arg = (idx[:, None] * angle[None, :]).astype(f32)
    sin = np.sin(arg.astype(np.float64)).astype(f32)
    cos = np.cos(arg.astype(np.float64)).astype(f32)
    t["COSB"] = np.concatenate([cos.T, cos.T], axis=0).astype(ml_dtypes.bfloat16)
    t["SINB"] = np.concatenate([sin.T, sin.T], axis=0).astype(ml_dtypes.bfloat16)

    decay = np.log1p(-(2.0 ** (-5.0 - np.arange(H, dtype=np.float64))))
    s64 = np.arange(S, dtype=np.float64)
    rs = (1.0 - np.exp(decay[:, None] * (s64[None, :] + 1.0))) / (1.0 - np.exp(decay[:, None]))
    t["QS16"] = (np.exp(decay[:, None] * s64[None, :]) / np.sqrt(rs)).astype(f32)
    t["KS16"] = (0.125 * np.exp(-decay[:, None] * s64[None, :])).astype(f32)

    selq = np.zeros((16, 8, 128), dtype=f32)
    for dt_ in range(8):
        selq[2 * dt_, dt_, 0:64] = 1.0
        selq[2 * dt_ + 1, dt_, 64:128] = 1.0
    t["SELQ"] = selq

    m = np.zeros((128, 4, 512), dtype=f32)
    for r in range(4):
        for p in range(128):
            m[p, r, 128 * r + p:] = 1.0
    t["MASKS"] = m.astype(ml_dtypes.bfloat16)

    PT = np.zeros((128, 128), dtype=f32)
    for i in range(64):
        PT[2 * i + 1, 2 * i] = -1.0
        PT[2 * i, 2 * i + 1] = 1.0
    t["PT"] = PT.astype(ml_dtypes.bfloat16)

    t["IDNB"] = np.eye(128, dtype=f32).astype(ml_dtypes.bfloat16)
    t["ONESC"] = np.ones((1, 128), dtype=f32)
    t["IOTA8"] = np.tile(np.arange(8, dtype=f32), (128, 1)).astype(f32)
    xy = np.arange(32) >> 3
    z = np.arange(4)
    up = (z[None, :] > xy[:, None]).astype(f32)
    t["MTRI2"] = np.tile(up.reshape(1, 128), (128, 1)).astype(f32)
    return t


class Buf:
    __slots__ = ("name", "wr", "rd")

    def __init__(self, name):
        self.name = name
        self.wr = {}
        self.rd = {}


def _merge(dst, src):
    for k, v in src.items():
        if dst.get(k, 0) < v:
            dst[k] = v


class Sched:
    ENGINES = ("S", "P", "T", "V", "A")

    def __init__(self, nc, es):
        self.nc = nc
        self.es = es
        self.ops = {e: [] for e in self.ENGINES}
        self.sems = {}
        self.cnt = {}
        self.lastwait = {e: {} for e in self.ENGINES}
        self.checkpoints = []
        for e in self.ENGINES:
            self._mk(e)

    def _mk(self, name):
        self.sems[name] = self.es.enter_context(self.nc.semaphore("sem_" + name))
        self.cnt[name] = 0

    def dma_sem(self, name):
        if name not in self.sems:
            self._mk(name)
        return name

    def buf(self, name):
        return Buf(name)

    def op(self, eng, fn, reads=(), writes=(), writes_d=(), sem=None, inc=1,
           extra=()):
        waits = {}
        for b in reads:
            _merge(waits, b.wr)
        for b in writes:
            _merge(waits, b.wr)
            _merge(waits, b.rd)
        for b in writes_d:
            _merge(waits, b.rd)
        for tok in extra:
            if tok:
                _merge(waits, {tok[0]: tok[1]})
        semname = sem if sem is not None else eng
        if inc:
            self.cnt[semname] += inc
            tok = (semname, self.cnt[semname])
        else:
            tok = None
        lw = self.lastwait[eng]
        wl = []
        for s, v in waits.items():
            if lw.get(s, 0) < v:
                wl.append((s, v))
                lw[s] = v
        self.ops[eng].append((wl, fn, semname if inc else None, inc))
        if tok:
            td = {tok[0]: tok[1]}
            for b in reads:
                _merge(b.rd, td)
            for b in writes:
                b.wr = dict(td)
                b.rd = {}
            for b in writes_d:
                _merge(b.wr, td)
        else:
            pass
        return tok

    def dma(self, eng, out, in_, sem, reads=(), writes=(), writes_d=(), extra=(),
            transpose=False, accum=False):
        if accum:
            fn = lambda e, o=out, i=in_: e.dma_start(
                out=o, in_=i, accum_op=mybir.AluOpType.add)
        elif transpose:
            fn = lambda e, o=out, i=in_: e.dma_start(out=o, in_=i, transpose=True)
        else:
            fn = lambda e, o=out, i=in_: e.dma_start(out=o, in_=i)
        prev = self.cnt.get(sem, 0)
        ex = list(extra)
        if prev:
            ex.append((sem, prev))
        return self.op(eng, fn, reads=reads, writes=writes, writes_d=writes_d,
                       sem=sem, inc=16, extra=ex)

    def load_group(self, sem, loads):
        used = set()
        bufs = []
        for (eng, out, in_, buf) in loads:
            s = self.dma_sem(sem + "_p") if eng == "P" else self.dma_sem(sem)
            used.add(s)
            self.op(eng, lambda e, o=out, i=in_: e.dma_start(out=o, in_=i),
                    sem=s, inc=16)
            bufs.append(buf)
        final = {s: self.cnt[s] for s in used}
        for b in bufs:
            b.wr = dict(final)
            b.rd = {}

    def barrier(self):
        allsems = {s: v for s, v in self.cnt.items() if v > 0}
        for e in self.ENGINES:
            lw = self.lastwait[e]
            wl = []
            for s, v in allsems.items():
                if lw.get(s, 0) < v:
                    wl.append((s, v))
                    lw[s] = v
            if wl:
                self.ops[e].append((wl, lambda eng: None, None, 0))
        self.checkpoints.append({e: len(self.ops[e]) for e in self.ENGINES})

    def emit(self, block, upto=None):
        decos = {"S": block.sync, "P": block.gpsimd, "T": block.tensor,
                 "V": block.vector, "A": block.scalar}
        sems = self.sems
        cut = self.checkpoints[upto] if upto is not None else None
        for e in self.ENGINES:
            ops = self.ops[e] if cut is None else self.ops[e][:cut[e]]

            def make(ops):
                def run(eng):
                    for wl, fn, semname, inc in ops:
                        for s, v in wl:
                            eng.wait_ge(sems[s], v)
                        instr = fn(eng)
                        if semname is not None:
                            instr.then_inc(sems[semname], inc)
                return run

            decos[e](make(ops))


def build_program(dbg=False, upto=None):
    nc = bass.Bass("TRN2", target_bir_lowering=False)
    dt = nc.dram_tensor

    X = dt("X", [S, D], F32, kind="ExternalInput").ap()
    Wq = dt("Wq", [4, 128, 8, 256], F32, kind="ExternalInput").ap()
    Wk = dt("Wk", [4, 128, 8, 256], F32, kind="ExternalInput").ap()
    Wv = dt("Wv", [4, 128, 8, 512], F32, kind="ExternalInput").ap()
    Wg = dt("Wg", [4, 128, 8, 512], F32, kind="ExternalInput").ap()
    Wo = dt("Wo", [4, 128, 4, D], F32, kind="ExternalInput").ap()
    BQ = dt("BQ", [128, 8], F32, kind="ExternalInput").ap()
    BK = dt("BK", [128, 8], F32, kind="ExternalInput").ap()
    BV = dt("BV", [1, DV], F32, kind="ExternalInput").ap()
    BGA = dt("BGA", [1, DV], F32, kind="ExternalInput").ap()
    BO = dt("BO", [1, D], F32, kind="ExternalInput").ap()
    FWg = dt("FWg", [32, 128, 8, 128], BF16, kind="ExternalInput").ap()
    FWu = dt("FWu", [32, 128, 8, 128], BF16, kind="ExternalInput").ap()
    FWd = dt("FWd", [32, 128, D], BF16, kind="ExternalInput").ap()
    FBg = dt("FBg", [128, 32], F32, kind="ExternalInput").ap()
    FBu = dt("FBu", [128, 32], F32, kind="ExternalInput").ap()
    FBd = dt("FBd", [1, D], F32, kind="ExternalInput").ap()
    FWq = dt("FWq", [128, 8, 512], F32, kind="ExternalInput").ap()
    KEYS = dt("KEYS", [128, 8 * 64], F32, kind="ExternalInput").ap()
    DET = dt("DET", [128, 8, NEXP], F32, kind="ExternalInput").ap()
    UEW = dt("UEW", [128, 32, D], F32, kind="ExternalInput").ap()
    CC = dt("CC", [128, 8], F32, kind="ExternalInput").ap()
    WMOD = dt("WMOD", [12, 128, 8, 512], F32, kind="ExternalInput").ap()

    COSB = dt("COSB", [128, S], BF16, kind="ExternalInput").ap()
    SINB = dt("SINB", [128, S], BF16, kind="ExternalInput").ap()
    QS16 = dt("QS16", [16, S], F32, kind="ExternalInput").ap()
    KS16 = dt("KS16", [16, S], F32, kind="ExternalInput").ap()
    SELQ = dt("SELQ", [16, 8 * 128], F32, kind="ExternalInput").ap()
    MASKS = dt("MASKS", [128, 4 * 512], BF16, kind="ExternalInput").ap()
    PTD = dt("PT", [128, 128], BF16, kind="ExternalInput").ap()
    IDNB = dt("IDNB", [128, 128], BF16, kind="ExternalInput").ap()
    ONESC = dt("ONESC", [1, 128], F32, kind="ExternalInput").ap()
    IOTA8 = dt("IOTA8", [128, 8], F32, kind="ExternalInput").ap()
    MTRI2 = dt("MTRI2", [128, 128], F32, kind="ExternalInput").ap()

    IDNF = dt("IDNF", [128, 128], F32, kind="ExternalInput").ap()

    OUT = dt("OUT", [S, D], F32, kind="ExternalOutput").ap()
    sk = "ExternalOutput" if dbg else "Internal"
    PG = [dt(f"P{g}s", [S, D], F32, kind=sk).ap() for g in range(4)]
    X2S = dt("X2S", [S, D], F32, kind=sk).ap()
    H2S = dt("H2S", [128, NT, 8, 128], BF16, kind=sk).ap()
    EXS = dt("EXS", [S, D], F32, kind=sk).ap()
    MODS = dt("MODS", [1, 6 * D], F32, kind=sk).ap()
    DBGR = dt("DBGR", [128, 5120], F32, kind=sk).ap() if dbg else None


    es = ExitStack()
    with es:
        sc = Sched(nc, es)

        def sb(stack, name, shape, dtype):
            return stack.enter_context(nc.sbuf_tensor(name, shape, dtype))

        def psm(stack, name, shape, dtype=F32):
            return stack.enter_context(nc.psum_tensor(name, shape, dtype))

        idnb = sb(es, "idnb", [128, 128], BF16)
        idnf = sb(es, "idnf", [128, 128], F32)
        onesc_r = sb(es, "onesc_r", [1, 128], F32R)
        onesc_b = sb(es, "onesc_b", [1, 128], BF16)
        iota8 = sb(es, "iota8", [128, 8], F32)
        epsc = sb(es, "epsc", [128, 1], F32)
        b_epsc = sc.buf("epsc")
        b_idnb = sc.buf("idnb")
        b_idnf = sc.buf("idnf")
        b_onesr = sc.buf("onesc_r")
        b_onesb = sc.buf("onesc_b")
        b_iota8 = sc.buf("iota8")
        pWO = psm(es, "pWO", [128, D], F32)
        b_pWO = sc.buf("pWO")

        LD = sc.dma_sem("ldc")
        sc.load_group(LD, [
            ("S", idnb[:], IDNB, b_idnb),
            ("S", idnf[:], IDNF, b_idnf),
            ("P", onesc_r[:], ONESC, b_onesr),
            ("P", onesc_b[:], ONESC, b_onesb),
            ("S", iota8[:], IOTA8, b_iota8),
        ])
        sc.op("P", lambda e: e.load_library(LS_LIB), inc=0)
        sc.op("V", lambda e: e.memset(epsc[:], EPS), writes=[b_epsc])

        def bcast_tile(dstap, b_dst, rowap, b_row, ncols, pbuf, b_pbuf):
            for chk in range(ncols // 512):
                sc.op("T", lambda e, chk=chk: e.matmul(
                    pbuf[:, 0:512], onesc_r[:],
                    rowap[:, chk * 512:(chk + 1) * 512], start=True, stop=True),
                    reads=[b_onesr, b_row], writes=[b_pbuf])
                sc.op("V", lambda e, chk=chk: e.tensor_copy(
                    out=dstap[:, chk * 512:(chk + 1) * 512], in_=pbuf[:, 0:512]),
                    reads=[b_pbuf], writes_d=[b_dst])

        with ExitStack() as esM:
            cc = sb(esM, "cc", [128, 8], F32)
            silc_f = sb(esM, "silc_f", [128, 8], F32)
            silc_r = sb(esM, "silc_r", [128, 8], F32R)
            wmod_r = [sb(esM, "wmod_r0", [128, 8, 512], F32R),
                      sb(esM, "wmod_r1", [128, 8, 512], F32R)]
            modrow = sb(esM, "modrow", [1, 6 * D], F32)
            tmprow = sb(esM, "tmprow", [1, D], F32)
            sgnrow = sb(esM, "sgnrow", [1, D], F32)
            pM = psm(esM, "pM", [1, 512], F32)
            b_cc = sc.buf("cc")
            b_silcr = sc.buf("silc_r")
            b_wmod = [sc.buf("wmod_r0"), sc.buf("wmod_r1")]
            b_modrow = sc.buf("modrow")
            b_tmprow = sc.buf("tmprow")
            b_sgnrow = sc.buf("sgnrow")
            b_pM = sc.buf("pM")
            WL = sc.dma_sem("wl0")
            WL1 = sc.dma_sem("wl1")
            MSP = sc.dma_sem("msp")

            b_silcf = sc.buf("silc_f")
            sc.dma("P", cc[:], CC, WL, writes=[b_cc])
            sc.op("A", lambda e: e.activation(out=silc_f[:], in_=cc[:],
                                              func=AF.Silu),
                  reads=[b_cc], writes=[b_silcf])
            sc.op("V", lambda e: e.tensor_copy(out=silc_r[:], in_=silc_f[:]),
                  reads=[b_silcf], writes=[b_silcr])
            sc.dma("P", wmod_r[0][:], WMOD[0], WL, writes=[b_wmod[0]])
            sc.dma("P", wmod_r[1][:], WMOD[1], WL1, writes=[b_wmod[1]])
            for ch in range(12):
                par = ch % 2
                for k in range(8):
                    sc.op("T", lambda e, k=k, par=par: e.matmul(
                        pM[:], silc_r[:, k:k + 1], wmod_r[par][:, k, :],
                        start=(k == 0), stop=(k == 7)),
                        reads=[b_silcr, b_wmod[par]],
                        writes=([b_pM] if k == 0 else []),
                        writes_d=([b_pM] if k > 0 else []),
                        inc=(1 if k == 7 else 0))
                if ch + 2 < 12:
                    sc.dma("P", wmod_r[par][:], WMOD[ch + 2],
                           [WL, WL1][par], writes=[b_wmod[par]])
                sc.op("V", lambda e, ch=ch: e.tensor_copy(
                    out=modrow[:, ch * 512:(ch + 1) * 512], in_=pM[:]),
                    reads=[b_pM], writes_d=[b_modrow])
            for off in (1 * D, 4 * D):
                sc.op("V", lambda e, off=off: e.tensor_scalar(
                    out=modrow[:, off:off + D], in0=modrow[:, off:off + D],
                    scalar1=1.0, scalar2=None, op0=ALU.add),
                    reads=[b_modrow], writes_d=[b_modrow])
            for off in (2 * D, 5 * D):
                sc.op("V", lambda e, off=off: e.tensor_scalar(
                    out=sgnrow[:], in0=modrow[:, off:off + D], scalar1=0.0,
                    scalar2=2.0, op0=ALU.is_ge, op1=ALU.mult),
                    reads=[b_modrow], writes=[b_sgnrow])
                sc.op("V", lambda e: e.tensor_scalar(
                    out=sgnrow[:], in0=sgnrow[:], scalar1=-1.0, scalar2=None,
                    op0=ALU.add), reads=[b_sgnrow], writes_d=[b_sgnrow])
                sc.op("A", lambda e, off=off: e.activation(
                    out=tmprow[:], in_=modrow[:, off:off + D], func=AF.Abs),
                    reads=[b_modrow], writes=[b_tmprow])
                sc.op("V", lambda e: e.tensor_scalar(
                    out=tmprow[:], in0=tmprow[:], scalar1=0.25,
                    scalar2=None, op0=ALU.max),
                    reads=[b_tmprow], writes_d=[b_tmprow])
                sc.op("V", lambda e, off=off: e.tensor_tensor(
                    out=modrow[:, off:off + D], in0=tmprow[:], in1=sgnrow[:],
                    op=ALU.mult),
                    reads=[b_tmprow, b_sgnrow, b_modrow], writes_d=[b_modrow])
            sc.dma("S", MODS, modrow[:], MSP, reads=[b_modrow])
            sc.barrier()

        with ExitStack() as esA:
            cosT = sb(esA, "cosT", [128, S], BF16)
            sinT = sb(esA, "sinT", [128, S], BF16)
            qs16 = sb(esA, "qs16", [16, S], F32R)
            ks16 = sb(esA, "ks16", [16, S], F32R)
            selq = sb(esA, "selq", [16, 8, 128], F32R)
            masks = sb(esA, "masks", [128, 4, 512], BF16)
            ptm = sb(esA, "ptm", [128, 128], BF16)
            bq = sb(esA, "bq", [128, 8], F32)
            bk = sb(esA, "bk", [128, 8], F32)
            saB = sb(esA, "saB", [128, D], F32)
            caB = sb(esA, "caB", [128, D], F32)
            gaB = sb(esA, "gaB", [128, D], F32)
            hT = sb(esA, "hT", [128, 8, S], BF16)
            b_cos = sc.buf("cosT"); b_sin = sc.buf("sinT")
            b_qs16 = sc.buf("qs16"); b_ks16 = sc.buf("ks16")
            b_selq = sc.buf("selq"); b_masks = sc.buf("masks")
            b_ptm = sc.buf("ptm"); b_bq = sc.buf("bq"); b_bk = sc.buf("bk")
            b_saB = sc.buf("saB"); b_caB = sc.buf("caB"); b_gaB = sc.buf("gaB")
            b_hT = sc.buf("hT")

            AL = sc.dma_sem("al")
            sc.load_group(AL, [
                ("P", cosT[:], COSB, b_cos),
                ("P", sinT[:], SINB, b_sin),
                ("P", qs16[:], QS16, b_qs16),
                ("P", ks16[:], KS16, b_ks16),
                ("P", selq[:].rearrange("a k p -> a (k p)"), SELQ, b_selq),
                ("S", masks[:].rearrange("p r n -> p (r n)"), MASKS, b_masks),
                ("S", ptm[:], PTD, b_ptm),
                ("S", bq[:], BQ, b_bq),
                ("S", bk[:], BK, b_bk),
            ])

            with ExitStack() as esL:
                modr = sb(esL, "modr", [1, D], F32R)
                b_modr = sc.buf("modr")
                MRL = sc.dma_sem("mrl")
                for (dst, bdst, off) in ((saB, b_saB, 0), (caB, b_caB, D),
                                         (gaB, b_gaB, 2 * D)):
                    sc.dma("P", modr[:], MODS[:, off:off + D], MRL,
                           writes=[b_modr])
                    bcast_tile(dst, bdst, modr, b_modr, D, pWO, b_pWO)
                xt = [sb(esL, "xt0", [128, D], F32), sb(esL, "xt1", [128, D], F32)]
                lnsq = sb(esL, "lnsq", [128, D], F32)
                lntmp = sb(esL, "lntmp", [128, D], F32)
                lnh = [sb(esL, "lnh0", [128, D], BF16),
                       sb(esL, "lnh1", [128, D], BF16)]
                scl = [sb(esL, "scl0", [128, 4], F32),
                       sb(esL, "scl1", [128, 4], F32)]
                pT1 = psm(esL, "pT1", [128, D], BF16)
                b_xt = [sc.buf("xt0"), sc.buf("xt1")]
                b_lnsq = sc.buf("lnsq"); b_lntmp = sc.buf("lntmp")
                b_lnh = [sc.buf("lnh0"), sc.buf("lnh1")]
                b_scl = [sc.buf("scl0"), sc.buf("scl1")]
                b_pT1 = sc.buf("pT1")
                XL = [sc.dma_sem("xl0"), sc.dma_sem("xl1")]

                def ln_tile(xsrc, bx, cB, bcB, sB, bsB, par):
                    scp, bscp = scl[par], b_scl[par]
                    lnhp, blnh = lnh[par], b_lnh[par]
                    sc.op("V", lambda e: e.reduce_sum(out=scp[:, 0:1], in_=xsrc[:],
                                                      axis=AX.X),
                          reads=[bx], writes=[bscp])
                    sc.op("A", lambda e: e.activation(out=lnsq[:], in_=xsrc[:],
                                                      func=AF.Square,
                                                      accum_out=scp[:, 1:2]),
                          reads=[bx], writes=[b_lnsq], writes_d=[bscp])
                    sc.op("V", lambda e: e.tensor_scalar(
                        out=scp[:, 2:3], in0=scp[:, 0:1], scalar1=1.0 / D,
                        scalar2=None, op0=ALU.mult),
                        reads=[bscp], writes_d=[bscp])
                    sc.op("V", lambda e: e.tensor_tensor(
                        out=scp[:, 3:4], in0=scp[:, 2:3], in1=scp[:, 2:3],
                        op=ALU.mult), reads=[bscp], writes_d=[bscp])
                    sc.op("V", lambda e: e.scalar_tensor_tensor(
                        out=scp[:, 3:4], in0=scp[:, 1:2], scalar=1.0 / D,
                        in1=scp[:, 3:4], op0=ALU.mult, op1=ALU.subtract),
                        reads=[bscp], writes_d=[bscp])
                    sc.op("A", lambda e: e.activation(out=scp[:, 3:4],
                                                      in_=scp[:, 3:4],
                                                      func=AF.Sqrt, bias=epsc[:]),
                          reads=[bscp, b_epsc], writes_d=[bscp])
                    sc.op("V", lambda e: e.reciprocal(out=scp[:, 3:4],
                                                      in_=scp[:, 3:4]),
                          reads=[bscp], writes_d=[bscp])
                    sc.op("V", lambda e: e.tensor_scalar(
                        out=lntmp[:], in0=xsrc[:], scalar1=scp[:, 2:3],
                        scalar2=scp[:, 3:4], op0=ALU.subtract, op1=ALU.mult),
                        reads=[bx, bscp], writes=[b_lntmp])
                    sc.op("V", lambda e: e.tensor_tensor(
                        out=lntmp[:], in0=lntmp[:], in1=cB[:], op=ALU.mult),
                        reads=[b_lntmp, bcB], writes_d=[b_lntmp])
                    sc.op("V", lambda e: e.tensor_tensor(
                        out=lnhp[:], in0=lntmp[:], in1=sB[:], op=ALU.add),
                        reads=[b_lntmp, bsB], writes=[blnh])

                def ln_tp(i):
                    par = i % 2
                    for k in range(8):
                        sc.op("T", lambda e, k=k, par=par: e.transpose(
                            pT1[:, k * 128:(k + 1) * 128],
                            lnh[par][:, k * 128:(k + 1) * 128], idnb[:]),
                            reads=[b_lnh[par], b_idnb],
                            writes=([b_pT1] if k == 0 else []),
                            writes_d=([b_pT1] if k > 0 else []),
                            inc=(1 if k == 7 else 0))
                    sc.op("A", lambda e, i=i: e.activation(
                        out=hT[:, :, i * 128:(i + 1) * 128],
                        in_=pT1[:].rearrange("p (k t) -> p k t", k=8),
                        func=AF.Copy),
                        reads=[b_pT1], writes_d=[b_hT])

                for i in range(NT + 1):
                    par = i % 2
                    if i < NT:
                        sc.dma("S", xt[par][:], X[i * 128:(i + 1) * 128, :],
                               XL[par], writes=[b_xt[par]])
                        ln_tile(xt[par], b_xt[par], caB, b_caB, saB, b_saB, par)
                    if i > 0:
                        ln_tp(i - 1)
                sc.barrier()

            qrT = sb(esA, "qrT", [128, 2, S], BF16)
            krT = sb(esA, "krT", [128, 2, S], BF16)
            v_aug = sb(esA, "v_aug", [128, NT, 4, 129], BF16)
            gts = sb(esA, "gts", [128, NT, 512], BF16)
            qkT = sb(esA, "qkT", [128, NT, 512], BF16)
            avst = sb(esA, "avst", [128, NT, 129], F32)
            sqd = sb(esA, "sqd", [128, 128], F32)
            gatc = sb(esA, "gatc", [128, NT, 128], BF16)
            gatedT = sb(esA, "gatedT", [128, 4, S], BF16)
            amean = sb(esA, "amean", [128, NT], F32)
            asum2 = sb(esA, "asum2", [128, NT], F32)
            aden = sb(esA, "aden", [128, NT], F32)
            atmp = sb(esA, "atmp", [128, NT], F32)
            wq_sl = sb(esA, "wq_sl", [128, 8, 256], BF16)
            wk_sl = sb(esA, "wk_sl", [128, 8, 256], BF16)
            wvg_sl = sb(esA, "wvg_sl", [128, 8, 512], BF16)
            wo_sl = sb(esA, "wo_sl", [128, 4, D], BF16)
            bvrow = sb(esA, "bvrow", [1, 512], BF16)
            bgrow = sb(esA, "bgrow", [1, 512], BF16)
            qs_bf = sb(esA, "qs_bf", [128, 512], BF16)
            t1b = sb(esA, "t1b", [128, 512], BF16)
            t2b = sb(esA, "t2b", [128, 512], BF16)
            spl_t = sb(esA, "spl0", [128, D], F32)
            spl_t2 = sb(esA, "spl1", [128, D], F32)
            borow = spl_t[0:1, :]
            spl = [spl_t, spl_t2]
            b_qrT = sc.buf("qrT"); b_krT = sc.buf("krT"); b_vaug = sc.buf("v_aug")
            b_gts = sc.buf("gts")
            b_qkTj = [sc.buf(f"qkTj{j}") for j in range(NT)]
            b_gatedT = sc.buf("gatedT")
            b_avst = sc.buf("avst")
            b_sqd = sc.buf("sqd"); b_gatc = sc.buf("gatc")
            b_amean = sc.buf("amean"); b_asum2 = sc.buf("asum2")
            b_aden = sc.buf("aden"); b_atmp = sc.buf("atmp")
            b_wq = sc.buf("wq_sl"); b_wk = sc.buf("wk_sl"); b_wvg = sc.buf("wvg_sl")
            b_wo = sc.buf("wo_sl")
            b_bvrow = sc.buf("bvrow"); b_bgrow = sc.buf("bgrow")
            b_qsbf = sc.buf("qs_bf"); b_t1 = sc.buf("t1b"); b_t2 = sc.buf("t2b")
            b_spl_t = sc.buf("spl0")
            b_spl = [b_spl_t, sc.buf("spl1")]
            WQL = sc.dma_sem("wql"); WKL = sc.dma_sem("wkl")
            WVL = sc.dma_sem("wvl"); BVL = sc.dma_sem("bvl"); BGL = sc.dma_sem("bgl")
            WOL = sc.dma_sem("wol"); BOL = sc.dma_sem("bol")
            SPS = [sc.dma_sem("sp0"), sc.dma_sem("sp1")]
            GTS = [sc.dma_sem(f"gtt{q}") for q in range(4)]

            pA0 = psm(esA, "pA0", [128, 512], F32)
            pA1 = psm(esA, "pA1", [128, 512], F32)
            pA2 = psm(esA, "pA2", [128, 512], F32)
            pA3 = psm(esA, "pA3", [128, 512], F32)
            pAV = [psm(esA, f"pAV{j}", [128, 132], F32) for j in range(2)]
            b_pA0 = sc.buf("pA0"); b_pA1 = sc.buf("pA1")
            b_pA2 = sc.buf("pA2"); b_pA3 = sc.buf("pA3")
            b_pAV = [sc.buf("pAV0"), sc.buf("pAV1")]

            for hh in range(4):
                sc.op("V", lambda e, hh=hh: e.memset(v_aug[:, :, hh, 128:129], 1.0),
                      writes_d=[b_vaug])

            def a_loads(g):
                sc.dma("P", wq_sl[:], Wq[g], WQL, writes=[b_wq])
                sc.dma("P", wk_sl[:], Wk[g], WKL, writes=[b_wk])
                sc.dma("P", wvg_sl[:], Wv[g], WVL, writes=[b_wvg])
                sc.dma("P", bvrow[:], BV[:, g * 512:(g + 1) * 512], BVL,
                       writes=[b_bvrow])
                sc.dma("P", bgrow[:], BGA[:, g * 512:(g + 1) * 512], BGL,
                       writes=[b_bgrow])

            def qk_proj(g, wsl, bw, bias_sb, bbias, sc16, bsc16, dstT, bdst):
                for m in range(2):
                    dt_ = g * 2 + m
                    for chk in range(4):
                        for k in range(8):
                            sc.op("T", lambda e, m=m, chk=chk, k=k, wsl=wsl:
                                  e.matmul(
                                pA0[:], wsl[:, k, m * 128:(m + 1) * 128],
                                hT[:, k, chk * 512:(chk + 1) * 512],
                                start=(k == 0), stop=(k == 7)),
                                reads=[bw, b_hT],
                                writes=([b_pA0] if k == 0 else []),
                                writes_d=([b_pA0] if k > 0 else []),
                                inc=(1 if k == 7 else 0))
                        sc.op("T", lambda e, dt_=dt_, chk=chk, sc16=sc16:
                              e.matmul(
                            pA2[:], selq[:, dt_, :],
                            sc16[:, chk * 512:(chk + 1) * 512],
                            start=True, stop=True),
                            reads=[b_selq, bsc16], writes=[b_pA2])
                        sc.op("A", lambda e, dt_=dt_, bias_sb=bias_sb:
                              e.activation(
                            out=qs_bf[:], in_=pA0[:], func=AF.Identity,
                            bias=bias_sb[:, dt_:dt_ + 1]),
                            reads=[b_pA0, bbias], writes=[b_qsbf])
                        sc.op("T", lambda e: e.matmul(
                            pA1[:], ptm[:], qs_bf[:], start=True, stop=True),
                            reads=[b_ptm, b_qsbf], writes=[b_pA1])
                        sc.op("V", lambda e, chk=chk: e.tensor_tensor(
                            out=t1b[:], in0=qs_bf[:],
                            in1=cosT[:, chk * 512:(chk + 1) * 512],
                            op=ALU.mult),
                            reads=[b_qsbf, b_cos], writes=[b_t1])
                        sc.op("V", lambda e, chk=chk: e.tensor_tensor(
                            out=t2b[:], in0=pA1[:],
                            in1=sinT[:, chk * 512:(chk + 1) * 512],
                            op=ALU.mult),
                            reads=[b_pA1, b_sin], writes=[b_t2])
                        sc.op("V", lambda e: e.tensor_tensor(
                            out=t2b[:], in0=t2b[:], in1=t1b[:], op=ALU.add),
                            reads=[b_t1, b_t2], writes_d=[b_t2])
                        sc.op("V", lambda e, m=m, chk=chk, dstT=dstT:
                              e.tensor_tensor(
                            out=dstT[:, m, chk * 512:(chk + 1) * 512],
                            in0=t2b[:], in1=pA2[:], op=ALU.mult),
                            reads=[b_t2, b_pA2], writes_d=[bdst])

            def a_proj(g):
                qk_proj(g, wq_sl, b_wq, bq, b_bq, qs16, b_qs16, qrT, b_qrT)
                qk_proj(g, wk_sl, b_wk, bk, b_bk, ks16, b_ks16, krT, b_krT)
                pvg = [pA0, pA1]
                bpvg = [b_pA0, b_pA1]
                for i in range(NT):
                    pv, bpv = pvg[i % 2], bpvg[i % 2]
                    for k in range(8):
                        sc.op("T", lambda e, i=i, k=k, pv=pv: e.matmul(
                            pv[:], hT[:, k, i * 128:(i + 1) * 128],
                            wvg_sl[:, k, :], start=(k == 0), stop=False),
                            reads=[b_wvg, b_hT],
                            writes=([bpv] if k == 0 else []),
                            writes_d=([bpv] if k > 0 else []),
                            inc=(1 if k == 7 else 0))
                    sc.op("T", lambda e, pv=pv: e.matmul(
                        pv[:], onesc_b[:], bvrow[:], start=False, stop=True),
                        reads=[b_onesb, b_bvrow], writes_d=[bpv])
                    sc.op("A", lambda e, i=i, pv=pv: e.activation(
                        out=v_aug[:, i, :, 0:128],
                        in_=pv[:].rearrange("p (h t) -> p h t", h=4),
                        func=AF.Copy),
                        reads=[bpv], writes_d=[b_vaug])
                sc.dma("P", wvg_sl[:], Wg[g], WVL, writes=[b_wvg])
                for i in range(NT):
                    pv, bpv = pvg[i % 2], bpvg[i % 2]
                    for k in range(8):
                        sc.op("T", lambda e, i=i, k=k, pv=pv: e.matmul(
                            pv[:], hT[:, k, i * 128:(i + 1) * 128],
                            wvg_sl[:, k, :], start=(k == 0), stop=False),
                            reads=[b_wvg, b_hT],
                            writes=([bpv] if k == 0 else []),
                            writes_d=([bpv] if k > 0 else []),
                            inc=(1 if k == 7 else 0))
                    sc.op("T", lambda e, pv=pv: e.matmul(
                        pv[:], onesc_b[:], bgrow[:], start=False, stop=True),
                        reads=[b_onesb, b_bgrow], writes_d=[bpv])
                    sc.op("A", lambda e, i=i, pv=pv: e.activation(
                        out=gts[:, i, :], in_=pv[:], func=AF.Silu),
                        reads=[bpv], writes_d=[b_gts])

            pq = [pA0, pA1, pA2, pA3]
            bpq = [b_pA0, b_pA1, b_pA2, b_pA3]

            def a_qkb(hl, c):
                mrow = hl // 2
                prow = 64 * (hl % 2)
                jmax = 4 * c + 4
                for j in range(jmax if 'qk' in DBG_HEADS else 0):
                    jj = j % 4
                    sc.op("T", lambda e, j=j, c=c, jj=jj, mrow=mrow,
                          prow=prow: e.matmul(
                        pq[jj][:],
                        krT[prow:prow + 64, mrow, j * 128:(j + 1) * 128],
                        qrT[prow:prow + 64, mrow, c * 512:(c + 1) * 512],
                        start=True, stop=True),
                        reads=[b_krT, b_qrT], writes=[bpq[jj]])
                    r = j - 4 * c
                    if 'evac' not in DBG_HEADS:
                        continue
                    if r >= 0:
                        sc.op("V", lambda e, j=j, r=r, jj=jj: e.tensor_tensor(
                            out=qkT[:, j, :], in0=pq[jj][:],
                            in1=masks[:, r, :], op=ALU.mult),
                            reads=[bpq[jj], b_masks],
                            writes=[b_qkTj[j]])
                    elif j % 2 == 0:
                        sc.op("V", lambda e, j=j, jj=jj: e.tensor_copy(
                            out=qkT[:, j, :], in_=pq[jj][:]),
                            reads=[bpq[jj]], writes=[b_qkTj[j]])
                    else:
                        sc.op("A", lambda e, j=j, jj=jj: e.activation(
                            out=qkT[:, j, :], in_=pq[jj][:],
                            func=AF.Copy),
                            reads=[bpq[jj]], writes=[b_qkTj[j]])

            def a_avb(hl, c):
                for r in range(4 if 'av' in DBG_HEADS else 0):
                    st = 4 * c + r
                    pa = pAV[st % 2]
                    bpa = b_pAV[st % 2]
                    njs = st + 1
                    for j in range(njs):
                        sc.op("T", lambda e, j=j, r=r, hl=hl, pa=pa,
                              njs=njs: e.matmul(
                            pa[:, 0:129],
                            qkT[:, j, r * 128:(r + 1) * 128],
                            v_aug[:, j, hl, 0:129],
                            start=(j == 0), stop=(j == njs - 1)),
                            reads=[b_qkTj[j], b_vaug],
                            writes=([bpa] if j == 0 else []),
                            writes_d=([bpa] if j > 0 else []))
                    if 'epi' not in DBG_HEADS:
                        continue
                    sc.op("A", lambda e, pa=pa, st=st: e.activation(
                        out=avst[:, st, 0:129], in_=pa[:, 0:129],
                        func=AF.Copy),
                        reads=[bpa], writes_d=[b_avst])

            def a_epi(hl):
                av0 = avst[:, :, 0:128]
                dcol = avst[:, :, 128:129].rearrange("p s o -> p (s o)")
                for st in range(NT):
                    sc.op("A", lambda e, st=st: e.activation(
                        out=sqd[:], in_=avst[:, st, 0:128],
                        func=AF.Square, accum_out=asum2[:, st:st + 1]),
                        reads=([b_avst] if st == 0 else
                               [b_avst, b_sqd]),
                        writes=([b_sqd, b_asum2] if st == 0 else []),
                        writes_d=([b_sqd, b_asum2] if st > 0 else []))
                sc.op("V", lambda e: e.reduce_sum(out=amean[:], in_=av0,
                                                  axis=AX.X),
                      reads=[b_avst], writes=[b_amean])
                sc.op("V", lambda e: e.tensor_scalar(
                    out=amean[:], in0=amean[:], scalar1=1.0 / 128,
                    scalar2=None, op0=ALU.mult),
                    reads=[b_amean], writes_d=[b_amean])
                sc.op("V", lambda e: e.tensor_scalar(
                    out=aden[:], in0=dcol, scalar1=-1.0, scalar2=1.0,
                    op0=ALU.mult, op1=ALU.max),
                    reads=[b_avst], writes=[b_aden])
                sc.op("V", lambda e: e.tensor_tensor(
                    out=aden[:], in0=dcol, in1=aden[:], op=ALU.max),
                    reads=[b_avst, b_aden], writes_d=[b_aden])
                sc.op("V", lambda e: e.tensor_tensor(
                    out=aden[:], in0=aden[:], in1=aden[:], op=ALU.mult),
                    reads=[b_aden], writes_d=[b_aden])
                sc.op("V", lambda e: e.tensor_scalar(
                    out=asum2[:], in0=asum2[:], scalar1=1.0 / 128,
                    scalar2=None, op0=ALU.mult),
                    reads=[b_asum2], writes_d=[b_asum2])
                sc.op("V", lambda e: e.tensor_tensor(
                    out=atmp[:], in0=amean[:], in1=amean[:], op=ALU.mult),
                    reads=[b_amean], writes=[b_atmp])
                sc.op("V", lambda e: e.tensor_tensor(
                    out=asum2[:], in0=asum2[:], in1=atmp[:],
                    op=ALU.subtract),
                    reads=[b_atmp, b_asum2], writes_d=[b_asum2])
                sc.op("V", lambda e: e.scalar_tensor_tensor(
                    out=asum2[:], in0=aden[:], scalar=EPS, in1=asum2[:],
                    op0=ALU.mult, op1=ALU.add),
                    reads=[b_aden, b_asum2], writes_d=[b_asum2])
                sc.op("A", lambda e: e.activation(
                    out=asum2[:], in_=asum2[:], func=AF.Sqrt),
                    reads=[b_asum2], writes_d=[b_asum2])
                sc.op("V", lambda e: e.reciprocal(out=asum2[:], in_=asum2[:]),
                      reads=[b_asum2], writes_d=[b_asum2])
                sc.op("V", lambda e: e.tensor_tensor(
                    out=av0, in0=av0,
                    in1=amean[:].unsqueeze(2).to_broadcast([128, NT, 128]),
                    op=ALU.subtract),
                    reads=[b_amean, b_avst], writes_d=[b_avst])
                sc.op("V", lambda e: e.tensor_tensor(
                    out=av0, in0=av0,
                    in1=asum2[:].unsqueeze(2).to_broadcast([128, NT, 128]),
                    op=ALU.mult),
                    reads=[b_asum2, b_avst], writes_d=[b_avst])
                sc.op("V", lambda e, hl=hl: e.tensor_tensor(
                    out=gatc[:], in0=av0,
                    in1=gts[:].rearrange("p s (h t) -> p s h t", h=4)[
                        :, :, hl, :],
                    op=ALU.mult),
                    reads=[b_avst, b_gts], writes=[b_gatc])
                if 'tp' in DBG_HEADS:
                    for st in range(NT):
                        sc.dma("S", gatedT[:, hl, st * 128:(st + 1) * 128],
                               gatc[:, st, :], GTS[st % 4], reads=[b_gatc],
                               writes_d=[b_gatedT], transpose=True)

            def a_heads(g):
                for hl in range(4):
                    a_qkb(hl, 0)
                    if hl > 0 and 'epi' in DBG_HEADS:
                        a_epi(hl - 1)
                    a_avb(hl, 0)
                    for c in range(1, 4):
                        a_qkb(hl, c)
                        a_avb(hl, c)
                if 'epi' in DBG_HEADS:
                    a_epi(3)

            def a_wo_prep(g):
                sc.dma("P", wo_sl[:], Wo[g], WOL, writes=[b_wo])
                for kk in range(4):
                    sc.op("V", lambda e, kk=kk: e.tensor_tensor(
                        out=wo_sl[:, kk, :], in0=wo_sl[:, kk, :], in1=gaB[:],
                        op=ALU.mult), reads=[b_gaB, b_wo], writes_d=[b_wo])
                if g == 0:
                    sc.dma("S", borow, BO, BOL, writes=[b_spl_t])
                    sc.op("V", lambda e: e.tensor_tensor(
                        out=bvrow[:], in0=borow[:, 0:512],
                        in1=gaB[0:1, 0:512], op=ALU.mult),
                        reads=[b_spl_t, b_gaB], writes=[b_bvrow])
                    sc.op("V", lambda e: e.tensor_tensor(
                        out=bgrow[:], in0=borow[:, 512:1024],
                        in1=gaB[0:1, 512:1024], op=ALU.mult),
                        reads=[b_spl_t, b_gaB], writes=[b_bgrow])

            b_pWOh = [sc.buf("pWOh0"), sc.buf("pWOh1")]

            def a_wo_tiles(g):
                for i in range(NT):
                    par = i % 2
                    for half in range(2):
                        for kk in range(4):
                            first = (kk == 0)
                            stop_here = (kk == 3 and g != 0)
                            sc.op("T", lambda e, i=i, kk=kk, half=half,
                                  stop_here=stop_here: e.matmul(
                                pWO[:, half * 512:(half + 1) * 512],
                                gatedT[:, kk, i * 128:(i + 1) * 128],
                                wo_sl[:, kk, half * 512:(half + 1) * 512],
                                start=(kk == 0), stop=stop_here),
                                reads=[b_gatedT, b_wo],
                                writes=([b_pWOh[half]] if first else []),
                                writes_d=([b_pWOh[half]] if not first else []),
                                inc=(1 if kk == 3 else 0))
                        if g == 0:
                            brow, bbrow = [(bvrow, b_bvrow),
                                           (bgrow, b_bgrow)][half]
                            sc.op("T", lambda e, brow=brow, half=half: e.matmul(
                                pWO[:, half * 512:(half + 1) * 512], onesc_b[:],
                                brow[:], start=False, stop=True),
                                reads=[b_onesb, bbrow],
                                writes_d=[b_pWOh[half]])
                        sc.op("V", lambda e, par=par, half=half: e.tensor_copy(
                            out=spl[par][:, half * 512:(half + 1) * 512],
                            in_=pWO[:, half * 512:(half + 1) * 512]),
                            reads=[b_pWOh[half]],
                            writes=([b_spl[par]] if half == 0 else []),
                            writes_d=([b_spl[par]] if half == 1 else []))
                    sc.dma("S", PG[g][i * 128:(i + 1) * 128, :], spl[par][:],
                           SPS[par], reads=[b_spl[par]])

            a_loads(0)
            a_proj(0)
            for g in range(4):
                a_wo_prep(g)
                a_heads(g)
                if g == 0:
                    a_wo_tiles(0)
                    a_loads(1)
                    a_proj(1)
                else:
                    if g + 1 < 4:
                        a_loads(g + 1)
                        a_proj(g + 1)
                    a_wo_tiles(g)
            sc.barrier()

        with ExitStack() as esF:
            det = sb(esF, "det", [128, 8, NEXP], BF16)
            ue = sb(esF, "ue", [128, 32, D], BF16)
            wqf = sb(esF, "wqf", [128, 8, 512], F32R)
            keys = sb(esF, "keys", [128, 8, 64], F32)
            sbB = sb(esF, "sbB", [128, D], F32)
            cbB = sb(esF, "cbB", [128, D], F32)
            b_det = sc.buf("det"); b_ue = sc.buf("ue"); b_wqf = sc.buf("wqf")
            b_keys = sc.buf("keys")
            b_sbB = sc.buf("sbB"); b_cbB = sc.buf("cbB")
            FL = sc.dma_sem("fl")
            mtri2 = sb(esF, "mtri2", [128, 32, 4], F32)
            b_mtri2 = sc.buf("mtri2")
            sc.load_group(FL, [
                ("P", det[:], DET, b_det),
                ("P", ue[:], UEW, b_ue),
                ("P", wqf[:], FWq, b_wqf),
                ("S", keys[:].rearrange("p a k -> p (a k)"), KEYS, b_keys),
                ("S", mtri2[:].rearrange("p a b -> p (a b)"), MTRI2, b_mtri2),
            ])
            with ExitStack() as esF0:
                modr2 = sb(esF0, "modr2", [1, D], F32R)
                b_modr2 = sc.buf("modr2")
                MRL2 = sc.dma_sem("mrl2")
                for (dstp, bdst, off) in ((sbB, b_sbB, 3 * D), (cbB, b_cbB, 4 * D)):
                    sc.dma("P", modr2[:], MODS[:, off:off + D], MRL2,
                           writes=[b_modr2])
                    bcast_tile(dstp, bdst, modr2, b_modr2, D, pWO, b_pWO)
                sc.barrier()

            pgl = sb(esF, "pgl0", [128, D], F32)
            lnh2 = sb(esF, "lnh2", [128, D], BF16)
            keep2 = sb(esF, "keep2", [128, D], F32)
            scl2 = sb(esF, "scl2", [128, 4], F32)
            h2t = sb(esF, "h2t", [128, 8, 128], BF16)
            h2tf = sb(esF, "h2tf", [128, 8, 128], F32R)
            qft = sb(esF, "qft", [128, 4, 128], F32)
            simsbd = [sb(esF, "simsb0", [128, 512], F32),
                      sb(esF, "simsb1", [128, 512], F32)]
            qfed = [s[0:64, :].rearrange("p (a b) -> p a b", a=4)
                    for s in simsbd]
            mx1 = sb(esF, "mx1", [128, 2, 4, 8], F32)
            ix1u = sb(esF, "ix1u", [128, 2, 4, 8], U32)
            ix1f = sb(esF, "ix1f", [128, 2, 4, 8], F32)
            sc8 = sb(esF, "sc8", [128, 4, 8], F32)
            pku = sb(esF, "pku", [128, 4, 8], U32)
            iu = sb(esF, "iu", [128, 4, 8], U32)
            if_ = sb(esF, "if_", [128, 4, 8], F32)
            jf = sb(esF, "jf", [128, 4, 8], F32)
            cmp8 = sb(esF, "cmp8", [128, 4, 8, 8], BF16)
            idxf = sb(esF, "idxf", [128, 4, 8], F32)
            tmp48 = sb(esF, "tmp48", [128, 4, 8], F32)
            wsm = sb(esF, "wsm", [128, 4, 8], F32)
            zrow = sb(esF, "zrow", [128, 4, 1], F32)
            eq5 = sb(esF, "eq5", [128, 32, 4, 8], F32)
            kr2 = sb(esF, "kr2", [128, 32], F32)
            rt4 = sb(esF, "rt4", [128, 32, 4], F32)
            idx16 = sb(esF, "idx16", [128, 32], I16)
            w32b = sb(esF, "w32b", [128, 32], BF16)
            qsc1 = sb(esF, "qsc1", [128, 32], F32)
            qsc2 = sb(esF, "qsc2", [128, 32], F32)
            Sbf = sb(esF, "Sbf", [128, NEXP], BF16)
            wdense = sb(esF, "wdense", [128, NEXP], BF16)
            b_pgl = sc.buf("pgl0")
            b_lnh2 = sc.buf("lnh2"); b_keep2 = sc.buf("keep2"); b_scl2 = sc.buf("scl2")
            b_h2t = sc.buf("h2t"); b_h2tf = sc.buf("h2tf"); b_qft = sc.buf("qft")
            b_simsbd = [sc.buf("simsb0"), sc.buf("simsb1")]
            b_mx1 = sc.buf("mx1"); b_ix1u = sc.buf("ix1u"); b_ix1f = sc.buf("ix1f")
            b_sc8 = sc.buf("sc8"); b_pku = sc.buf("pku")
            b_iu = sc.buf("iu"); b_if = sc.buf("if_"); b_jf = sc.buf("jf")
            b_cmp8 = sc.buf("cmp8"); b_idxf = sc.buf("idxf")
            b_tmp48 = sc.buf("tmp48"); b_wsm = sc.buf("wsm"); b_zrow = sc.buf("zrow")
            b_eq5 = sc.buf("eq5")
            b_kr2 = sc.buf("kr2"); b_rt4 = sc.buf("rt4")
            b_idx16 = sc.buf("idx16"); b_w32b = sc.buf("w32b")
            b_qsc1 = sc.buf("qsc1"); b_qsc2 = sc.buf("qsc2")
            b_Sq = [sc.buf(f"Sbf{q}") for q in range(4)]
            b_wd = [sc.buf(f"wd{ec}") for ec in range(8)]
            wdT = Sbf[:].rearrange("p (e t) -> p e t", e=32)
            b_wdT = b_Sq
            PGL0 = sc.dma_sem("pgl0s")
            PGL1 = sc.dma_sem("pgl1s")
            X2W = sc.dma_sem("x2w")
            H2W = sc.dma_sem("h2w")
            EXW = sc.dma_sem("exw")

            pF0 = psm(esF, "pF0", [128, 512], F32)
            pF1 = psm(esF, "pF1", [128, 512], F32)
            pTRf = psm(esF, "pTRf", [128, D], F32)
            pTRb = psm(esF, "pTRb", [128, 2048], BF16)
            b_pF0 = sc.buf("pF0"); b_pF1 = sc.buf("pF1")
            b_pTRf = sc.buf("pTRf"); b_pTRb = sc.buf("pTRb")

            def stP1(ti, P, bP, semA, semB):
                sc.dma("S", P[:], PG[0][ti * 128:(ti + 1) * 128, :], semA,
                       writes=[bP])
                for src in (PG[1], PG[2], PG[3], X):
                    sc.dma("P", P[:], src[ti * 128:(ti + 1) * 128, :], semB,
                           reads=[bP], writes_d=[bP], accum=True)
                sc.dma("S", X2S[ti * 128:(ti + 1) * 128, :], P[:], X2W,
                       reads=[bP])

            def stP2(ti, P, bP):
                x2s, bx2 = P, bP
                sc.op("V", lambda e: e.reduce_sum(out=scl2[:, 0:1], in_=x2s[:],
                                                  axis=AX.X),
                      reads=[bx2], writes=[b_scl2])
                sc.op("A", lambda e: e.activation(out=keep2[:], in_=x2s[:],
                                                  func=AF.Square,
                                                  accum_out=scl2[:, 1:2]),
                      reads=[bx2], writes=[b_keep2], writes_d=[b_scl2])
                sc.op("V", lambda e: e.tensor_scalar(
                    out=scl2[:, 2:3], in0=scl2[:, 0:1], scalar1=1.0 / D,
                    scalar2=None, op0=ALU.mult),
                    reads=[b_scl2], writes_d=[b_scl2])
                sc.op("V", lambda e: e.tensor_tensor(
                    out=scl2[:, 3:4], in0=scl2[:, 2:3], in1=scl2[:, 2:3],
                    op=ALU.mult), reads=[b_scl2], writes_d=[b_scl2])
                sc.op("V", lambda e: e.scalar_tensor_tensor(
                    out=scl2[:, 3:4], in0=scl2[:, 1:2], scalar=1.0 / D,
                    in1=scl2[:, 3:4], op0=ALU.mult, op1=ALU.subtract),
                    reads=[b_scl2], writes_d=[b_scl2])
                sc.op("A", lambda e: e.activation(out=scl2[:, 3:4],
                                                  in_=scl2[:, 3:4],
                                                  func=AF.Sqrt, bias=epsc[:]),
                      reads=[b_scl2, b_epsc], writes_d=[b_scl2])
                sc.op("V", lambda e: e.reciprocal(out=scl2[:, 3:4],
                                                  in_=scl2[:, 3:4]),
                      reads=[b_scl2], writes_d=[b_scl2])
                sc.op("V", lambda e: e.tensor_scalar(
                    out=keep2[:], in0=x2s[:], scalar1=scl2[:, 2:3],
                    scalar2=scl2[:, 3:4], op0=ALU.subtract, op1=ALU.mult),
                    reads=[bx2, b_scl2], writes=[b_keep2])
                sc.op("V", lambda e: e.tensor_tensor(
                    out=keep2[:], in0=keep2[:], in1=cbB[:], op=ALU.mult),
                    reads=[b_keep2, b_cbB], writes_d=[b_keep2])
                sc.op("V", lambda e: e.tensor_tensor(
                    out=keep2[:], in0=keep2[:], in1=sbB[:], op=ALU.add),
                    reads=[b_keep2, b_sbB], writes_d=[b_keep2])
                sc.op("V", lambda e: e.tensor_copy(out=lnh2[:], in_=keep2[:]),
                      reads=[b_keep2], writes=[b_lnh2])

            def stP3(ti, hp, bhp):
                for k in range(8):
                    sc.op("T", lambda e, k=k: e.transpose(
                        pTRb[:, k * 128:(k + 1) * 128],
                        lnh2[:, k * 128:(k + 1) * 128], idnb[:]),
                        reads=[b_lnh2, b_idnb],
                        writes=([b_pTRb] if k == 0 else []),
                        writes_d=([b_pTRb] if k > 0 else []),
                        inc=(1 if k == 7 else 0))
                sc.op("V", lambda e, hp=hp: e.tensor_copy(
                    out=hp[:].rearrange("p k t -> p (k t)"),
                    in_=pTRb[:, 0:1024]),
                    reads=[b_pTRb], writes=[bhp])
                sc.dma("S", H2S[:, ti, :, :], hp[:], H2W, reads=[bhp])
                for k in range(8):
                    sc.op("T", lambda e, k=k: e.transpose(
                        pTRf[:, k * 128:(k + 1) * 128],
                        keep2[:, k * 128:(k + 1) * 128], idnf[:]),
                        reads=[b_keep2, b_idnf],
                        writes=([b_pTRf] if k == 0 else []),
                        writes_d=([b_pTRf] if k > 0 else []),
                        inc=(1 if k == 7 else 0))
                sc.op("V", lambda e: e.tensor_copy(
                    out=h2tf[:].rearrange("p k t -> p (k t)"), in_=pTRf[:]),
                    reads=[b_pTRf], writes=[b_h2tf])

            def stP4(ti):
                simsb, b_simsb = simsbd[ti % 2], b_simsbd[ti % 2]
                qfe = qfed[ti % 2]
                for m in range(4):
                    for k in range(8):
                        sc.op("T", lambda e, m=m, k=k: e.matmul(
                            pTRf[:, m * 128:(m + 1) * 128],
                            wqf[:, k, m * 128:(m + 1) * 128],
                            h2tf[:, k, :], start=(k == 0), stop=(k == 7)),
                            reads=[b_wqf, b_h2tf],
                            writes=([b_pTRf] if m == 0 and k == 0 else []),
                            writes_d=([b_pTRf] if m > 0 or k > 0 else []),
                            inc=(1 if k == 7 else 0))
                sc.op("V", lambda e: e.tensor_copy(
                    out=qft[:].rearrange("p m t -> p (m t)"),
                    in_=pTRf[:, 0:512]),
                    reads=[b_pTRf], writes=[b_qft])
                for hh in range(4):
                    p_, h_ = hh // 2, (hh % 2) * 2 + 1
                    rt = p_ * 2 + h_ // 2
                    sc.op("V", lambda e, hh=hh, rt=rt: e.tensor_copy(
                        out=qfe[:, hh, :], in_=qft[64:128, rt, :]),
                        reads=[b_qft], writes_d=[b_simsb])
                for ph in range(8):
                    p_, h_ = ph // 4, ph % 4
                    rt = p_ * 2 + h_ // 2
                    if h_ % 2 == 0:
                        lh = (lambda e, ph=ph, rt=rt: e.matmul(
                            pTRf[:, 512 + ph * 64:576 + ph * 64],
                            qft[0:64, rt, :], keys[0:64, ph, :],
                            start=True, stop=True))
                    else:
                        hh = p_ * 2 + (h_ - 1) // 2
                        lh = (lambda e, ph=ph, hh=hh: e.matmul(
                            pTRf[:, 512 + ph * 64:576 + ph * 64],
                            qfe[:, hh, :], keys[0:64, ph, :],
                            start=True, stop=True))
                    sc.op("T", lh,
                        reads=[b_qft, b_simsb, b_keys],
                        writes_d=[b_pTRf],
                        inc=(1 if ph == 7 else 0))
                sc.op("V", lambda e: e.tensor_copy(out=simsb[:],
                                                   in_=pTRf[:, 512:1024]),
                      reads=[b_pTRf], writes=[b_simsb])

            def stP5(ti):
                simsb, b_simsb = simsbd[ti % 2], b_simsbd[ti % 2]
                all64 = eq5[:].rearrange("p a b c -> p (a b c)")[
                    :, 0:256].rearrange("p (h i) -> p h i", h=4)
                b_all64 = b_eq5
                for ph in range(8):
                    p_, h_ = ph // 4, ph % 4
                    sc.op("V", lambda e, ph=ph, p_=p_, h_=h_: e.max(
                        out=mx1[:, p_, h_, :], in_=simsb[:, ph * 64:(ph + 1) * 64]),
                        reads=[b_simsb], writes_d=[b_mx1])
                for ph in range(8):
                    p_, h_ = ph // 4, ph % 4
                    sc.op("V", lambda e, ph=ph, p_=p_, h_=h_: e.max_index(
                        out=ix1u[:, p_, h_, :], in_max=mx1[:, p_, h_, :],
                        in_values=simsb[:, ph * 64:(ph + 1) * 64]),
                        reads=[b_simsb, b_mx1], writes_d=[b_ix1u])
                sc.op("V", lambda e: e.tensor_copy(
                    out=ix1f[:].rearrange("p a h k -> p (a h k)"),
                    in_=ix1u[:].rearrange("p a h k -> p (a h k)")),
                    reads=[b_ix1u], writes=[b_ix1f])
                sc.op("V", lambda e: e.tensor_tensor(
                    out=all64[:].rearrange("p h (i j) -> p h i j", i=8),
                    in0=mx1[:, 0, :, :].unsqueeze(3).to_broadcast(
                        [128, 4, 8, 8]),
                    in1=mx1[:, 1, :, :].unsqueeze(2).to_broadcast(
                        [128, 4, 8, 8]),
                    op=ALU.add),
                    reads=[b_mx1], writes=[b_all64])
                for h_ in range(4):
                    sc.op("V", lambda e, h_=h_: e.max(
                        out=sc8[:, h_, :], in_=all64[:, h_, :]),
                        reads=[b_all64], writes_d=[b_sc8])
                for h_ in range(4):
                    sc.op("V", lambda e, h_=h_: e.max_index(
                        out=pku[:, h_, :], in_max=sc8[:, h_, :],
                        in_values=all64[:, h_, :]),
                        reads=[b_all64, b_sc8], writes_d=[b_pku])
                sc.op("V", lambda e: e.tensor_scalar(
                    out=iu[:].rearrange("p h k -> p (h k)"),
                    in0=pku[:].rearrange("p h k -> p (h k)"),
                    scalar1=3, scalar2=None, op0=ALU.logical_shift_right),
                    reads=[b_pku], writes=[b_iu])
                sc.op("V", lambda e: e.tensor_copy(
                    out=if_[:].rearrange("p h k -> p (h k)"),
                    in_=iu[:].rearrange("p h k -> p (h k)")),
                    reads=[b_iu], writes=[b_if])
                sc.op("V", lambda e: e.tensor_scalar(
                    out=iu[:].rearrange("p h k -> p (h k)"),
                    in0=pku[:].rearrange("p h k -> p (h k)"),
                    scalar1=7, scalar2=None, op0=ALU.bitwise_and),
                    reads=[b_pku, b_iu], writes_d=[b_iu])
                sc.op("V", lambda e: e.tensor_copy(
                    out=jf[:].rearrange("p h k -> p (h k)"),
                    in_=iu[:].rearrange("p h k -> p (h k)")),
                    reads=[b_iu], writes=[b_jf])

                def gather8(valsrc, possrc, bpos, dst, bdst):
                    sc.op("V", lambda e, possrc=possrc: e.tensor_tensor(
                        out=cmp8[:],
                        in0=iota8[:].unsqueeze(1).unsqueeze(2).to_broadcast(
                            [128, 4, 8, 8]),
                        in1=possrc[:].unsqueeze(3).to_broadcast(
                            [128, 4, 8, 8]),
                        op=ALU.is_equal),
                        reads=[bpos, b_iota8], writes=[b_cmp8])
                    sc.op("V", lambda e, valsrc=valsrc: e.tensor_tensor(
                        out=cmp8[:], in0=cmp8[:],
                        in1=valsrc.unsqueeze(2).to_broadcast(
                            [128, 4, 8, 8]),
                        op=ALU.mult),
                        reads=[b_ix1f, b_cmp8], writes_d=[b_cmp8])
                    sc.op("V", lambda e, dst=dst: e.reduce_sum(
                        out=dst[:], in_=cmp8[:], axis=AX.X),
                        reads=[b_cmp8], writes=[bdst])
                gather8(ix1f[:, 0, :, :], if_, b_if, idxf, b_idxf)
                sc.op("V", lambda e: e.tensor_scalar(
                    out=idxf[:].rearrange("p h k -> p (h k)"),
                    in0=idxf[:].rearrange("p h k -> p (h k)"),
                    scalar1=64.0, scalar2=None, op0=ALU.mult),
                    reads=[b_idxf], writes_d=[b_idxf])
                gather8(ix1f[:, 1, :, :], jf, b_jf, tmp48, b_tmp48)
                sc.op("V", lambda e: e.tensor_tensor(
                    out=idxf[:].rearrange("p h k -> p (h k)"),
                    in0=idxf[:].rearrange("p h k -> p (h k)"),
                    in1=tmp48[:].rearrange("p h k -> p (h k)"), op=ALU.add),
                    reads=[b_tmp48, b_idxf], writes_d=[b_idxf])
                sc.op("V", lambda e: e.tensor_tensor(
                    out=wsm[:], in0=sc8[:],
                    in1=sc8[:, :, 0:1].to_broadcast([128, 4, 8]),
                    op=ALU.subtract),
                    reads=[b_sc8], writes=[b_wsm])
                sc.op("A", lambda e: e.activation(
                    out=wsm[:].rearrange("p h k -> p (h k)"),
                    in_=wsm[:].rearrange("p h k -> p (h k)"), func=AF.Exp),
                    reads=[b_wsm], writes_d=[b_wsm])
                sc.op("V", lambda e: e.reduce_sum(out=zrow[:], in_=wsm[:],
                                                  axis=AX.X),
                      reads=[b_wsm], writes=[b_zrow])
                sc.op("V", lambda e: e.reciprocal(
                    out=zrow[:].rearrange("p h o -> p (h o)"),
                    in_=zrow[:].rearrange("p h o -> p (h o)")),
                    reads=[b_zrow], writes_d=[b_zrow])
                sc.op("V", lambda e: e.tensor_tensor(
                    out=wsm[:], in0=wsm[:],
                    in1=zrow[:].to_broadcast([128, 4, 8]), op=ALU.mult),
                    reads=[b_zrow, b_wsm], writes_d=[b_wsm])
                idxfl = idxf[:].rearrange("p h k -> p (h k)")
                sc.op("V", lambda e: e.tensor_tensor(
                    out=eq5[:],
                    in0=idxfl.unsqueeze(2).unsqueeze(3).to_broadcast(
                        [128, 32, 4, 8]),
                    in1=idxf[:].unsqueeze(1).to_broadcast([128, 32, 4, 8]),
                    op=ALU.is_equal),
                    reads=[b_idxf], writes=[b_eq5])
                sc.op("V", lambda e: e.reduce_sum(out=rt4[:], in_=eq5[:],
                                                  axis=AX.X),
                      reads=[b_eq5], writes=[b_rt4])
                sc.op("V", lambda e: e.reduce_sum(out=kr2[:], in_=rt4[:],
                                                  axis=AX.X),
                      reads=[b_rt4], writes=[b_kr2])
                sc.op("V", lambda e: e.tensor_tensor(
                    out=eq5[:], in0=eq5[:],
                    in1=mtri2[:].unsqueeze(3).to_broadcast([128, 32, 4, 8]),
                    op=ALU.mult),
                    reads=[b_eq5, b_mtri2], writes_d=[b_eq5])
                sc.op("V", lambda e: e.reduce_sum(out=rt4[:], in_=eq5[:],
                                                  axis=AX.X),
                      reads=[b_eq5], writes=[b_rt4])
                sc.op("V", lambda e: e.reduce_sum(
                    out=tmp48[:].rearrange("p h k -> p (h k)"), in_=rt4[:],
                    axis=AX.X),
                      reads=[b_rt4], writes=[b_tmp48])
                sc.op("V", lambda e: e.tensor_tensor(
                    out=kr2[:], in0=kr2[:],
                    in1=tmp48[:].rearrange("p h k -> p (h k)"),
                    op=ALU.subtract),
                    reads=[b_tmp48, b_kr2], writes_d=[b_kr2])
                sc.op("V", lambda e: e.tensor_scalar(
                    out=kr2[:], in0=kr2[:],
                    scalar1=1.0, scalar2=None, op0=ALU.is_gt),
                    reads=[b_kr2], writes_d=[b_kr2])
                sc.op("V", lambda e: e.tensor_tensor(
                    out=eq5[:], in0=eq5[:],
                    in1=wsm[:].unsqueeze(1).to_broadcast([128, 32, 4, 8]),
                    op=ALU.mult),
                    reads=[b_eq5, b_wsm], writes_d=[b_eq5])
                sc.op("V", lambda e: e.reduce_sum(out=rt4[:], in_=eq5[:],
                                                  axis=AX.X),
                      reads=[b_eq5], writes=[b_rt4])
                sc.op("V", lambda e: e.reduce_sum(
                    out=tmp48[:].rearrange("p h k -> p (h k)"), in_=rt4[:],
                    axis=AX.X),
                      reads=[b_rt4], writes=[b_tmp48])
                sc.op("V", lambda e: e.tensor_tensor(
                    out=wsm[:], in0=wsm[:],
                    in1=tmp48[:], op=ALU.add),
                    reads=[b_tmp48, b_wsm], writes_d=[b_wsm])
                sc.op("V", lambda e: e.tensor_scalar(
                    out=tmp48[:].rearrange("p h k -> p (h k)"),
                    in0=idxf[:].rearrange("p h k -> p (h k)"),
                    scalar1=1.0, scalar2=None, op0=ALU.add),
                    reads=[b_idxf, b_tmp48], writes_d=[b_tmp48])
                sc.op("V", lambda e: e.tensor_tensor(
                    out=tmp48[:].rearrange("p h k -> p (h k)"),
                    in0=tmp48[:].rearrange("p h k -> p (h k)"),
                    in1=kr2[:], op=ALU.mult),
                    reads=[b_kr2, b_tmp48], writes_d=[b_tmp48])
                sc.op("V", lambda e: e.tensor_tensor(
                    out=idxf[:], in0=idxf[:], in1=tmp48[:], op=ALU.subtract),
                    reads=[b_tmp48, b_idxf], writes_d=[b_idxf])

            def stP6(ti):
                sc.op("V", lambda e: e.tensor_copy(
                    out=w32b[:], in_=wsm[:].rearrange("p h k -> p (h k)")),
                    reads=[b_wsm], writes=[b_w32b])
                idxflat = idxf[:].rearrange("p h k -> p (h k)")
                for qd in range(4):
                    sc.op("V", lambda e, qd=qd: e.tensor_scalar(
                        out=qsc1[:], in0=idxflat, scalar1=float(1024 * qd),
                        scalar2=None, op0=ALU.subtract),
                        reads=[b_idxf], writes=[b_qsc1])
                    sc.op("V", lambda e: e.tensor_scalar(
                        out=qsc2[:], in0=qsc1[:], scalar1=0.0, scalar2=None,
                        op0=ALU.is_ge),
                        reads=[b_qsc1], writes=[b_qsc2])
                    sc.op("V", lambda e: e.scalar_tensor_tensor(
                        out=qsc2[:], in0=qsc1[:], scalar=1024.0, in1=qsc2[:],
                        op0=ALU.is_lt, op1=ALU.mult),
                        reads=[b_qsc1, b_qsc2], writes_d=[b_qsc2])
                    sc.op("V", lambda e: e.tensor_scalar(
                        out=qsc1[:], in0=qsc1[:], scalar1=1.0, scalar2=None,
                        op0=ALU.add),
                        reads=[b_qsc1], writes_d=[b_qsc1])
                    sc.op("V", lambda e: e.tensor_tensor(
                        out=qsc1[:], in0=qsc1[:], in1=qsc2[:], op=ALU.mult),
                        reads=[b_qsc1, b_qsc2], writes_d=[b_qsc1])
                    sc.op("V", lambda e: e.tensor_scalar(
                        out=qsc1[:], in0=qsc1[:], scalar1=-1.0, scalar2=None,
                        op0=ALU.add),
                        reads=[b_qsc1], writes_d=[b_qsc1])
                    sc.op("V", lambda e: e.tensor_copy(out=idx16[:], in_=qsc1[:]),
                          reads=[b_qsc1], writes=[b_idx16])
                    sc.op("P", lambda e, qd=qd: e.local_scatter(
                        out_ap=Sbf[:, qd * 1024:(qd + 1) * 1024],
                        data_ap=w32b[:], idxs_ap=idx16[:],
                        channels=128, num_elems=1024, num_idxs=32),
                        reads=[b_w32b, b_idx16], writes=[b_Sq[qd]])

            def stP7a(ti, hp, bhp):
                for ec in range(8):
                    pe_ = [pF0, pF1][ec % 2]
                    bpe = [b_pF0, b_pF1][ec % 2]
                    for k in range(8):
                        sc.op("T", lambda e, ec=ec, k=k, pe_=pe_, hp=hp: e.matmul(
                            pe_[:], hp[:, k, :],
                            det[:, k, ec * 512:(ec + 1) * 512],
                            start=(k == 0), stop=(k == 7)),
                            reads=[bhp, b_det],
                            writes=([bpe] if k == 0 else []),
                            writes_d=([bpe] if k > 0 else []),
                            inc=(1 if k == 7 else 0))
                    sc.op("A", lambda e, ec=ec, pe_=pe_: e.activation(
                        out=wdense[:, ec * 512:(ec + 1) * 512], in_=pe_[:],
                        func=AF.Silu),
                        reads=[bpe], writes=[b_wd[ec]])

            def stP7b(ti):
                for ec in range(8):
                    sc.op("V", lambda e, ec=ec: e.tensor_tensor(
                        out=wdense[:, ec * 512:(ec + 1) * 512],
                        in0=wdense[:, ec * 512:(ec + 1) * 512],
                        in1=Sbf[:, ec * 512:(ec + 1) * 512], op=ALU.mult),
                        reads=[b_wd[ec], b_Sq[ec // 2]], writes_d=[b_wd[ec]])

            def stP8(ti, xv, bxv):
                for half in range(2):
                    for kb in range(16):
                        sc.op("T", lambda e, half=half, kb=kb: e.transpose(
                            pTRb[:, kb * 128:(kb + 1) * 128],
                            wdense[:, half * 2048 + kb * 128:
                                   half * 2048 + (kb + 1) * 128], idnb[:]),
                            reads=[b_wd[half * 4 + kb // 4], b_idnb],
                            writes=([b_pTRb] if kb == 0 else []),
                            writes_d=([b_pTRb] if kb > 0 else []),
                            inc=(1 if kb % 4 == 3 else 0))
                    sc.op("V", lambda e, half=half: e.tensor_copy(
                        out=wdT[:, half * 16:(half + 1) * 16, :].rearrange(
                            "p e t -> p (e t)"),
                        in_=pTRb[:]),
                        reads=[b_pTRb],
                        writes=[b_Sq[half * 2], b_Sq[half * 2 + 1]])
                for et in range(32):
                    for half in range(2):
                        first = (et == 0 and half == 0)
                        last = (et == 31 and half == 1)
                        sc.op("T", lambda e, et=et, half=half, first=first: e.matmul(
                            pWO[:, half * 512:(half + 1) * 512],
                            wdT[:, et, :], ue[:, et, half * 512:(half + 1) * 512],
                            start=(et == 0), stop=(et == 31)),
                            reads=[b_Sq[et // 8], b_ue],
                            writes=([b_pWO] if first else []),
                            writes_d=([b_pWO] if not first else []),
                            inc=(1 if last else 0))
                sc.op("V", lambda e, xv=xv: e.tensor_copy(out=xv[:], in_=pWO[:]),
                      reads=[b_pWO], writes=[bxv])
                sc.dma("S", EXS[ti * 128:(ti + 1) * 128, :], xv[:], EXW,
                       reads=[bxv])

            pgl2 = sb(esF, "pgl2", [128, D], F32)
            b_pgl2 = sc.buf("pgl2")
            h2t2 = sb(esF, "h2t2", [128, 8, 128], BF16)
            b_h2t2 = sc.buf("h2t2")
            PGL2 = sc.dma_sem("pgl2s")
            PGL3 = sc.dma_sem("pgl3s")
            PP = [(pgl, b_pgl, PGL0, PGL1), (pgl2, b_pgl2, PGL2, PGL3)]
            HH = [(h2t, b_h2t), (h2t2, b_h2t2)]

            stP1(0, *PP[0][0:2], PP[0][2], PP[0][3])
            for ti in range(NT + 1):
                par = ti % 2
                if ti < NT:
                    stP2(ti, PP[par][0], PP[par][1])
                    if ti + 1 < NT:
                        stP1(ti + 1, *PP[1 - par][0:2], PP[1 - par][2],
                             PP[1 - par][3])
                if ti > 0:
                    stP7a(ti - 1, *HH[1 - par])
                    stP5(ti - 1)
                    stP6(ti - 1)
                    stP7b(ti - 1)
                if ti < NT:
                    stP3(ti, *HH[par])
                    stP4(ti)
                if ti > 0:
                    stP8(ti - 1, PP[par][0], PP[par][1])
            sc.barrier()

        with ExitStack() as esG:
            gbB2 = sb(esG, "gbB2", [128, D], F32)
            modr3 = sb(esG, "modr3", [1, D], F32R)
            fbg = sb(esG, "fbg", [128, 32], F32)
            fbu = sb(esG, "fbu", [128, 32], F32)
            bdrow = sb(esG, "bdrow", [1, D], F32)
            bdrow_b = sb(esG, "bdrow_b", [1, D], BF16)
            h2cd = [sb(esG, "h2c0", [128, 4, 8, 128], BF16),
                    sb(esG, "h2c1", [128, 4, 8, 128], BF16)]
            wgc = [sb(esG, "wgc0", [128, 4, 8, 128], BF16),
                   sb(esG, "wgc1", [128, 4, 8, 128], BF16)]
            wuc = [sb(esG, "wuc0", [128, 4, 8, 128], BF16),
                   sb(esG, "wuc1", [128, 4, 8, 128], BF16)]
            wdc = [sb(esG, "wdc0", [128, 4, 512], BF16),
                   sb(esG, "wdc1", [128, 4, 512], BF16)]
            silu_g = sb(esG, "silu_g", [128, 512], BF16)
            etile = sb(esG, "etile", [128, 32, 512], BF16)
            x2l = sb(esG, "x2l", [128, D], F32)
            exl = sb(esG, "exl", [128, D], F32)
            outs = sb(esG, "outs", [128, 4, D], F32)
            outb = [sb(esG, "outb0", [128, D], F32), sb(esG, "outb1", [128, D], F32)]
            b_gbB2 = sc.buf("gbB2"); b_modr3 = sc.buf("modr3")
            b_fbg = sc.buf("fbg"); b_fbu = sc.buf("fbu")
            b_bdrow = sc.buf("bdrow"); b_bdrowb = sc.buf("bdrow_b")
            b_h2cd = [sc.buf("h2c0"), sc.buf("h2c1")]
            b_wgc = [sc.buf("wgc0"), sc.buf("wgc1")]
            b_wuc = [sc.buf("wuc0"), sc.buf("wuc1")]
            b_wdc = [sc.buf("wdc0"), sc.buf("wdc1")]
            b_silug = sc.buf("silu_g"); b_etile = sc.buf("etile")
            b_x2l = sc.buf("x2l"); b_exl = sc.buf("exl")
            b_outs = sc.buf("outs")
            b_outb = [sc.buf("outb0"), sc.buf("outb1")]
            GL = sc.dma_sem("gl")
            H2L = sc.dma_sem("h2l")
            X2L = sc.dma_sem("x2l_s")
            EXL = sc.dma_sem("exl_s")
            WGS = [sc.dma_sem("wgs0"), sc.dma_sem("wgs1")]
            WUS = [sc.dma_sem("wus0"), sc.dma_sem("wus1")]
            WDS = [sc.dma_sem("wds0"), sc.dma_sem("wds1")]
            OST = [sc.dma_sem("ost0"), sc.dma_sem("ost1")]

            pG = psm(esG, "pG", [128, 512], F32)
            pU = psm(esG, "pU", [128, 512], F32)
            pM2 = [psm(esG, f"pM2{j}", [128, 512], F32) for j in range(4)]
            b_pG = sc.buf("pG"); b_pU = sc.buf("pU")
            b_pM2 = [sc.buf(f"pM2{j}") for j in range(4)]

            sc.load_group(GL, [
                ("S", fbg[:], FBg, b_fbg),
                ("S", fbu[:], FBu, b_fbu),
                ("S", bdrow[:], FBd, b_bdrow),
                ("P", modr3[:], MODS[:, 5 * D:6 * D], b_modr3),
            ])
            bcast_tile(gbB2, b_gbB2, modr3, b_modr3, D, pWO, b_pWO)
            sc.op("V", lambda e: e.tensor_copy(out=bdrow_b[:], in_=bdrow[:]),
                  reads=[b_bdrow], writes=[b_bdrowb])

            FCH2 = 512

            def ld_gu(m, par):
                sc.dma("P", wgc[par][:],
                       FWg.rearrange("m p k n -> p m k n")[:, m:m + 4],
                       WGS[par], writes=[b_wgc[par]])
                sc.dma("P", wuc[par][:],
                       FWu.rearrange("m p k n -> p m k n")[:, m:m + 4],
                       WUS[par], writes=[b_wuc[par]])

            ld_gu(0, 0)
            sc.dma("S", h2cd[0][:], H2S[:, 0:4, :, :], H2L, writes=[b_h2cd[0]])
            for ci in range(S // FCH2):
                t0 = ci * 4
                h2c, b_h2c = h2cd[ci % 2], b_h2cd[ci % 2]
                for m in range(32):
                    par = (m // 4) % 2
                    mm = m % 4
                    if mm == 0:
                        if m + 4 < 32:
                            ld_gu(m + 4, 1 - par)
                        elif ci + 1 < S // FCH2:
                            ld_gu(0, 0)
                    for k in range(8):
                        sc.op("T", lambda e, k=k, par=par, mm=mm, h2c=h2c:
                              e.matmul(
                            pG[:], wgc[par][:, mm, k, :],
                            h2c[:, :, k, :], start=(k == 0), stop=(k == 7)),
                            reads=[b_wgc[par], b_h2c],
                            writes=([b_pG] if k == 0 else []),
                            writes_d=([b_pG] if k > 0 else []),
                            inc=(1 if k == 7 else 0))
                    for k in range(8):
                        sc.op("T", lambda e, k=k, par=par, mm=mm, h2c=h2c:
                              e.matmul(
                            pU[:], wuc[par][:, mm, k, :],
                            h2c[:, :, k, :], start=(k == 0), stop=(k == 7)),
                            reads=[b_wuc[par], b_h2c],
                            writes=([b_pU] if k == 0 else []),
                            writes_d=([b_pU] if k > 0 else []),
                            inc=(1 if k == 7 else 0))
                    sc.op("A", lambda e, m=m: e.activation(
                        out=silu_g[:], in_=pG[:], func=AF.Silu,
                        bias=fbg[:, m:m + 1]),
                        reads=[b_pG, b_fbg], writes=[b_silug])
                    sc.op("V", lambda e, m=m: e.scalar_tensor_tensor(
                        out=etile[:, m, :], in0=pU[:], scalar=fbu[:, m:m + 1],
                        in1=silu_g[:], op0=ALU.add, op1=ALU.mult),
                        reads=[b_pU, b_fbu, b_silug], writes_d=[b_etile])
                if ci + 1 < S // FCH2:
                    sc.dma("S", h2cd[(ci + 1) % 2][:],
                           H2S[:, t0 + 4:t0 + 8, :, :], H2L,
                           writes=[b_h2cd[(ci + 1) % 2]])
                def ld_wd(m, wpar, half):
                    sc.dma("P", wdc[wpar][:],
                           FWd.rearrange("m p n -> p m n")[
                               :, m:m + 4, half * 512:(half + 1) * 512],
                           WDS[wpar], writes=[b_wdc[wpar]])

                ld_wd(0, 0, 0)
                for half in range(2):
                    for m in range(32):
                        wpar = (m // 4) % 2
                        mm = m % 4
                        if mm == 0:
                            if m + 4 < 32:
                                ld_wd(m + 4, 1 - wpar, half)
                            elif half == 0:
                                ld_wd(0, 0, 1)
                        for tt in range(4):
                            first = (m == 0)
                            sc.op("T", lambda e, m=m, tt=tt, wpar=wpar,
                                  mm=mm: e.matmul(
                                pM2[tt][:],
                                etile[:, m, tt * 128:(tt + 1) * 128],
                                wdc[wpar][:, mm, :], start=(m == 0), stop=False),
                                reads=[b_etile, b_wdc[wpar]],
                                writes=([b_pM2[tt]] if first else []),
                                writes_d=([b_pM2[tt]] if not first else []),
                                inc=(1 if tt == 3 else 0))
                    for tt in range(4):
                        sc.op("T", lambda e, tt=tt, half=half: e.matmul(
                            pM2[tt][:], onesc_b[:],
                            bdrow_b[:, half * 512:(half + 1) * 512],
                            start=False, stop=True),
                            reads=[b_onesb, b_bdrowb], writes_d=[b_pM2[tt]])
                        sc.op("V", lambda e, tt=tt, half=half: e.tensor_copy(
                            out=outs[:, tt, half * 512:(half + 1) * 512],
                            in_=pM2[tt][:]),
                            reads=[b_pM2[tt]], writes_d=[b_outs])
                for tt in range(4):
                    ti = t0 + tt
                    par = ti % 2
                    sc.dma("S", x2l[:], X2S[ti * 128:(ti + 1) * 128, :], X2L,
                           writes=[b_x2l])
                    sc.dma("S", exl[:], EXS[ti * 128:(ti + 1) * 128, :], EXL,
                           writes=[b_exl])
                    sc.op("V", lambda e, par=par, tt=tt: e.tensor_tensor(
                        out=outb[par][:], in0=exl[:], in1=outs[:, tt, :],
                        op=ALU.add),
                        reads=[b_x2l, b_exl, b_outs], writes=[b_outb[par]])
                    sc.op("V", lambda e, par=par: e.tensor_tensor(
                        out=outb[par][:], in0=outb[par][:], in1=gbB2[:],
                        op=ALU.mult),
                        reads=[b_gbB2, b_outb[par]], writes_d=[b_outb[par]])
                    sc.op("V", lambda e, par=par: e.tensor_tensor(
                        out=outb[par][:], in0=outb[par][:], in1=x2l[:],
                        op=ALU.add),
                        reads=[b_x2l, b_outb[par]], writes_d=[b_outb[par]])
                    sc.dma("S", OUT[ti * 128:(ti + 1) * 128, :], outb[par][:],
                           OST[par], reads=[b_outb[par]])
            sc.barrier()


        block = es.enter_context(nc.Block())
        sc.emit(block, upto=upto)
    return nc


def make_in_maps(inputs, ncores=B):
    t = host_tables()
    x = np.ascontiguousarray(inputs["L0_attn_x"], dtype=np.float32)
    c = np.ascontiguousarray(inputs["L0_c"], dtype=np.float32)

    def dmaj(v, cols):
        return np.ascontiguousarray(v.reshape(cols, 128).T)

    def wblk(w, G, N):
        return np.ascontiguousarray(
            np.asarray(w, np.float32).reshape(8, 128, G, N).transpose(2, 1, 0, 3))

    keys = inputs["L0_ffn_keys"]
    keysT = np.transpose(keys, (3, 2, 0, 1)).reshape(64, 8 * 64)
    keysT = np.ascontiguousarray(np.concatenate([keysT, keysT], axis=0))
    det = np.ascontiguousarray(
        inputs["L0_ffn_down_embed"].T.reshape(8, 128, 4096).transpose(1, 0, 2))
    ue = np.ascontiguousarray(
        np.asarray(inputs["L0_ffn_up_embed"], np.float32).reshape(32, 128, 1024)
        .transpose(1, 0, 2))
    wo = np.ascontiguousarray(
        np.asarray(inputs["L0_attn_Wo"], np.float32).reshape(4, 4, 128, 1024)
        .transpose(0, 2, 1, 3))
    fwq = np.ascontiguousarray(
        np.asarray(inputs["L0_ffn_Wq"], np.float32).reshape(8, 128, 512)
        .transpose(1, 0, 2))
    fwd = np.ascontiguousarray(
        np.asarray(inputs["L0_ffn_Wd"], np.float32).reshape(32, 128, 1024))
    wmod = np.ascontiguousarray(
        np.asarray(inputs["L0_wmod"], np.float32).reshape(8, 128, 12, 512)
        .transpose(2, 1, 0, 3))

    common = {
        "Wq": wblk(inputs["L0_attn_Wq"], 4, 256),
        "Wk": wblk(inputs["L0_attn_Wk"], 4, 256),
        "Wv": wblk(inputs["L0_attn_Wv"], 4, 512),
        "Wg": wblk(inputs["L0_attn_Wg"], 4, 512),
        "Wo": wo,
        "BQ": dmaj(inputs["L0_attn_bq"], 8), "BK": dmaj(inputs["L0_attn_bk"], 8),
        "BV": inputs["L0_attn_bv"].reshape(1, -1),
        "BGA": inputs["L0_attn_bg"].reshape(1, -1),
        "BO": inputs["L0_attn_bo"].reshape(1, -1),
        "FWg": wblk(inputs["L0_ffn_Wg"], 32, 128).astype(ml_dtypes.bfloat16),
        "FWu": wblk(inputs["L0_ffn_Wu"], 32, 128).astype(ml_dtypes.bfloat16),
        "FWd": fwd.astype(ml_dtypes.bfloat16),
        "FBg": dmaj(inputs["L0_ffn_bg"], 32), "FBu": dmaj(inputs["L0_ffn_bu"], 32),
        "FBd": inputs["L0_ffn_bd"].reshape(1, -1),
        "FWq": fwq,
        "KEYS": keysT, "DET": det, "UEW": ue,
        "WMOD": wmod,
        "COSB": t["COSB"], "SINB": t["SINB"], "QS16": t["QS16"],
        "KS16": t["KS16"],
        "SELQ": t["SELQ"].reshape(16, 8 * 128),
        "MASKS": t["MASKS"].reshape(128, 4 * 512),
        "PT": t["PT"], "IDNB": t["IDNB"], "IDNF": np.eye(128, dtype=np.float32),
        "ONESC": t["ONESC"], "IOTA8": t["IOTA8"], "MTRI2": t["MTRI2"],
    }
    in_maps = []
    for b in range(ncores):
        m = dict(common)
        m["X"] = np.ascontiguousarray(x[b])
        m["CC"] = dmaj(c[b], 8)
        in_maps.append(m)
    return in_maps


def kernel(**inputs):
    nc = build_program(dbg=DEBUG)
    lower_extended_insts(nc)
    in_maps = make_in_maps(inputs, B)
    res = run_bass_kernel_spmd(nc, in_maps, core_ids=list(range(B)))
    out = np.stack([res.results[b]["OUT"] for b in range(B)], axis=0)
    return out.astype(np.float32)


# version 22
# speedup vs baseline: 1.0383x; 1.0093x over previous
import numpy as np
import ml_dtypes
from contextlib import ExitStack

import concourse.bass as bass
import concourse.mybir as mybir
from concourse.bass_utils import run_bass_kernel_spmd
from concourse.library_overlay import lower_extended_insts
from concourse.library_config import local_scatter as LS_LIB

F32 = mybir.dt.float32
F32R = mybir.dt.float32r
BF16 = mybir.dt.bfloat16
I16 = mybir.dt.int16
U32 = mybir.dt.uint32
ALU = mybir.AluOpType
AF = mybir.ActivationFunctionType
AX = mybir.AxisListType

B, S, D = 8, 2048, 1024
H, KD, HD = 16, 64, 128
DV = 2048
DCD = 4096
NEXP = 4096
NT = S // 128
EPS = 1e-5
FCH = 256
NCH = S // FCH
NTC = FCH // 128

DEBUG = False
DBG_BARRIERS = False
DBG_HEADS = 'qk,evac,av,epi,tp'
DBG_EPI_N = 10**9
DBG_F1_N = 10**9
DBG_F1_TRACE = False


def host_tables():
    t = {}
    f32 = np.float32
    angle = (1.0 / (10000.0 ** np.linspace(0.0, 1.0, KD // 2, dtype=f32))).astype(f32)
    angle = np.repeat(angle, 2).astype(f32)
    idx = np.arange(S, dtype=f32)
    arg = (idx[:, None] * angle[None, :]).astype(f32)
    sin = np.sin(arg.astype(np.float64)).astype(f32)
    cos = np.cos(arg.astype(np.float64)).astype(f32)
    t["COSB"] = np.concatenate([cos.T, cos.T], axis=0).astype(ml_dtypes.bfloat16)
    t["SINB"] = np.concatenate([sin.T, sin.T], axis=0).astype(ml_dtypes.bfloat16)

    decay = np.log1p(-(2.0 ** (-5.0 - np.arange(H, dtype=np.float64))))
    s64 = np.arange(S, dtype=np.float64)
    rs = (1.0 - np.exp(decay[:, None] * (s64[None, :] + 1.0))) / (1.0 - np.exp(decay[:, None]))
    t["QS16"] = (np.exp(decay[:, None] * s64[None, :]) / np.sqrt(rs)).astype(f32)
    t["KS16"] = (0.125 * np.exp(-decay[:, None] * s64[None, :])).astype(f32)

    selq = np.zeros((16, 8, 128), dtype=f32)
    for dt_ in range(8):
        selq[2 * dt_, dt_, 0:64] = 1.0
        selq[2 * dt_ + 1, dt_, 64:128] = 1.0
    t["SELQ"] = selq

    m = np.zeros((128, 4, 512), dtype=f32)
    for r in range(4):
        for p in range(128):
            m[p, r, 128 * r + p:] = 1.0
    t["MASKS"] = m.astype(ml_dtypes.bfloat16)

    PT = np.zeros((128, 128), dtype=f32)
    for i in range(64):
        PT[2 * i + 1, 2 * i] = -1.0
        PT[2 * i, 2 * i + 1] = 1.0
    t["PT"] = PT.astype(ml_dtypes.bfloat16)

    t["IDNB"] = np.eye(128, dtype=f32).astype(ml_dtypes.bfloat16)
    t["ONESC"] = np.ones((1, 128), dtype=f32)
    t["IOTA8"] = np.tile(np.arange(8, dtype=f32), (128, 1)).astype(f32)
    xy = np.arange(32) >> 3
    z = np.arange(4)
    up = (z[None, :] > xy[:, None]).astype(f32)
    t["MTRI2"] = np.tile(up.reshape(1, 128), (128, 1)).astype(f32)
    return t


class Buf:
    __slots__ = ("name", "wr", "rd")

    def __init__(self, name):
        self.name = name
        self.wr = {}
        self.rd = {}


def _merge(dst, src):
    for k, v in src.items():
        if dst.get(k, 0) < v:
            dst[k] = v


class Sched:
    ENGINES = ("S", "P", "T", "V", "A")

    def __init__(self, nc, es):
        self.nc = nc
        self.es = es
        self.ops = {e: [] for e in self.ENGINES}
        self.sems = {}
        self.cnt = {}
        self.lastwait = {e: {} for e in self.ENGINES}
        self.checkpoints = []
        for e in self.ENGINES:
            self._mk(e)

    def _mk(self, name):
        self.sems[name] = self.es.enter_context(self.nc.semaphore("sem_" + name))
        self.cnt[name] = 0

    def dma_sem(self, name):
        if name not in self.sems:
            self._mk(name)
        return name

    def buf(self, name):
        return Buf(name)

    def op(self, eng, fn, reads=(), writes=(), writes_d=(), sem=None, inc=1,
           extra=()):
        waits = {}
        for b in reads:
            _merge(waits, b.wr)
        for b in writes:
            _merge(waits, b.wr)
            _merge(waits, b.rd)
        for b in writes_d:
            _merge(waits, b.rd)
        for tok in extra:
            if tok:
                _merge(waits, {tok[0]: tok[1]})
        semname = sem if sem is not None else eng
        if inc:
            self.cnt[semname] += inc
            tok = (semname, self.cnt[semname])
        else:
            tok = None
        lw = self.lastwait[eng]
        wl = []
        for s, v in waits.items():
            if lw.get(s, 0) < v:
                wl.append((s, v))
                lw[s] = v
        self.ops[eng].append((wl, fn, semname if inc else None, inc))
        if tok:
            td = {tok[0]: tok[1]}
            for b in reads:
                _merge(b.rd, td)
            for b in writes:
                b.wr = dict(td)
                b.rd = {}
            for b in writes_d:
                _merge(b.wr, td)
        else:
            pass
        return tok

    def dma(self, eng, out, in_, sem, reads=(), writes=(), writes_d=(), extra=(),
            transpose=False, accum=False):
        if accum:
            fn = lambda e, o=out, i=in_: e.dma_start(
                out=o, in_=i, accum_op=mybir.AluOpType.add)
        elif transpose:
            fn = lambda e, o=out, i=in_: e.dma_start(out=o, in_=i, transpose=True)
        else:
            fn = lambda e, o=out, i=in_: e.dma_start(out=o, in_=i)
        prev = self.cnt.get(sem, 0)
        ex = list(extra)
        if prev:
            ex.append((sem, prev))
        return self.op(eng, fn, reads=reads, writes=writes, writes_d=writes_d,
                       sem=sem, inc=16, extra=ex)

    def load_group(self, sem, loads):
        used = set()
        bufs = []
        for (eng, out, in_, buf) in loads:
            s = self.dma_sem(sem + "_p") if eng == "P" else self.dma_sem(sem)
            used.add(s)
            self.op(eng, lambda e, o=out, i=in_: e.dma_start(out=o, in_=i),
                    sem=s, inc=16)
            bufs.append(buf)
        final = {s: self.cnt[s] for s in used}
        for b in bufs:
            b.wr = dict(final)
            b.rd = {}

    def barrier(self):
        allsems = {s: v for s, v in self.cnt.items() if v > 0}
        for e in self.ENGINES:
            lw = self.lastwait[e]
            wl = []
            for s, v in allsems.items():
                if lw.get(s, 0) < v:
                    wl.append((s, v))
                    lw[s] = v
            if wl:
                self.ops[e].append((wl, lambda eng: None, None, 0))
        self.checkpoints.append({e: len(self.ops[e]) for e in self.ENGINES})

    def emit(self, block, upto=None):
        decos = {"S": block.sync, "P": block.gpsimd, "T": block.tensor,
                 "V": block.vector, "A": block.scalar}
        sems = self.sems
        cut = self.checkpoints[upto] if upto is not None else None
        for e in self.ENGINES:
            ops = self.ops[e] if cut is None else self.ops[e][:cut[e]]

            def make(ops):
                def run(eng):
                    for wl, fn, semname, inc in ops:
                        for s, v in wl:
                            eng.wait_ge(sems[s], v)
                        instr = fn(eng)
                        if semname is not None:
                            instr.then_inc(sems[semname], inc)
                return run

            decos[e](make(ops))


def build_program(dbg=False, upto=None):
    nc = bass.Bass("TRN2", target_bir_lowering=False)
    dt = nc.dram_tensor

    X = dt("X", [S, D], F32, kind="ExternalInput").ap()
    Wq = dt("Wq", [4, 128, 8, 256], BF16, kind="ExternalInput").ap()
    Wk = dt("Wk", [4, 128, 8, 256], BF16, kind="ExternalInput").ap()
    Wv = dt("Wv", [4, 128, 8, 512], BF16, kind="ExternalInput").ap()
    Wg = dt("Wg", [4, 128, 8, 512], BF16, kind="ExternalInput").ap()
    Wo = dt("Wo", [4, 128, 4, D], BF16, kind="ExternalInput").ap()
    BQ = dt("BQ", [128, 8], F32, kind="ExternalInput").ap()
    BK = dt("BK", [128, 8], F32, kind="ExternalInput").ap()
    BV = dt("BV", [1, DV], F32, kind="ExternalInput").ap()
    BGA = dt("BGA", [1, DV], F32, kind="ExternalInput").ap()
    BO = dt("BO", [1, D], F32, kind="ExternalInput").ap()
    FWg = dt("FWg", [32, 128, 8, 128], BF16, kind="ExternalInput").ap()
    FWu = dt("FWu", [32, 128, 8, 128], BF16, kind="ExternalInput").ap()
    FWd = dt("FWd", [32, 128, D], BF16, kind="ExternalInput").ap()
    FBg = dt("FBg", [128, 32], F32, kind="ExternalInput").ap()
    FBu = dt("FBu", [128, 32], F32, kind="ExternalInput").ap()
    FBd = dt("FBd", [1, D], F32, kind="ExternalInput").ap()
    FWq = dt("FWq", [128, 8, 512], F32, kind="ExternalInput").ap()
    KEYS = dt("KEYS", [128, 8 * 64], F32, kind="ExternalInput").ap()
    DET = dt("DET", [128, 8, NEXP], BF16, kind="ExternalInput").ap()
    UEW = dt("UEW", [128, 32, D], BF16, kind="ExternalInput").ap()
    CC = dt("CC", [128, 8], F32, kind="ExternalInput").ap()
    WMOD = dt("WMOD", [12, 128, 8, 512], F32, kind="ExternalInput").ap()

    COSB = dt("COSB", [128, S], BF16, kind="ExternalInput").ap()
    SINB = dt("SINB", [128, S], BF16, kind="ExternalInput").ap()
    QS16 = dt("QS16", [16, S], F32, kind="ExternalInput").ap()
    KS16 = dt("KS16", [16, S], F32, kind="ExternalInput").ap()
    SELQ = dt("SELQ", [16, 8 * 128], F32, kind="ExternalInput").ap()
    MASKS = dt("MASKS", [128, 4 * 512], BF16, kind="ExternalInput").ap()
    PTD = dt("PT", [128, 128], BF16, kind="ExternalInput").ap()
    IDNB = dt("IDNB", [128, 128], BF16, kind="ExternalInput").ap()
    ONESC = dt("ONESC", [1, 128], F32, kind="ExternalInput").ap()
    IOTA8 = dt("IOTA8", [128, 8], F32, kind="ExternalInput").ap()
    MTRI2 = dt("MTRI2", [128, 128], F32, kind="ExternalInput").ap()

    IDNF = dt("IDNF", [128, 128], F32, kind="ExternalInput").ap()

    OUT = dt("OUT", [S, D], F32, kind="ExternalOutput").ap()
    sk = "ExternalOutput" if dbg else "Internal"
    PG = [dt(f"P{g}s", [S, D], F32, kind=sk).ap() for g in range(4)]
    X2S = dt("X2S", [S, D], F32, kind=sk).ap()
    H2S = dt("H2S", [128, NT, 8, 128], BF16, kind=sk).ap()
    EXS = dt("EXS", [S, D], F32, kind=sk).ap()
    MODS = dt("MODS", [1, 6 * D], F32, kind=sk).ap()
    DBGR = dt("DBGR", [128, 5120], F32, kind=sk).ap() if dbg else None


    es = ExitStack()
    with es:
        sc = Sched(nc, es)

        def sb(stack, name, shape, dtype):
            return stack.enter_context(nc.sbuf_tensor(name, shape, dtype))

        def psm(stack, name, shape, dtype=F32):
            return stack.enter_context(nc.psum_tensor(name, shape, dtype))

        idnb = sb(es, "idnb", [128, 128], BF16)
        idnf = sb(es, "idnf", [128, 128], F32)
        onesc_r = sb(es, "onesc_r", [1, 128], F32R)
        onesc_b = sb(es, "onesc_b", [1, 128], BF16)
        iota8 = sb(es, "iota8", [128, 8], F32)
        epsc = sb(es, "epsc", [128, 1], F32)
        b_epsc = sc.buf("epsc")
        b_idnb = sc.buf("idnb")
        b_idnf = sc.buf("idnf")
        b_onesr = sc.buf("onesc_r")
        b_onesb = sc.buf("onesc_b")
        b_iota8 = sc.buf("iota8")
        pWO = psm(es, "pWO", [128, D], F32)
        b_pWO = sc.buf("pWO")

        LD = sc.dma_sem("ldc")
        sc.load_group(LD, [
            ("S", idnb[:], IDNB, b_idnb),
            ("S", idnf[:], IDNF, b_idnf),
            ("P", onesc_r[:], ONESC, b_onesr),
            ("P", onesc_b[:], ONESC, b_onesb),
            ("S", iota8[:], IOTA8, b_iota8),
        ])
        sc.op("P", lambda e: e.load_library(LS_LIB), inc=0)
        sc.op("V", lambda e: e.memset(epsc[:], EPS), writes=[b_epsc])

        def bcast_tile(dstap, b_dst, rowap, b_row, ncols, pbuf, b_pbuf):
            for chk in range(ncols // 512):
                sc.op("T", lambda e, chk=chk: e.matmul(
                    pbuf[:, 0:512], onesc_r[:],
                    rowap[:, chk * 512:(chk + 1) * 512], start=True, stop=True),
                    reads=[b_onesr, b_row], writes=[b_pbuf])
                sc.op("V", lambda e, chk=chk: e.tensor_copy(
                    out=dstap[:, chk * 512:(chk + 1) * 512], in_=pbuf[:, 0:512]),
                    reads=[b_pbuf], writes_d=[b_dst])

        with ExitStack() as esM:
            cc = sb(esM, "cc", [128, 8], F32)
            silc_f = sb(esM, "silc_f", [128, 8], F32)
            silc_r = sb(esM, "silc_r", [128, 8], F32R)
            wmod_r = [sb(esM, "wmod_r0", [128, 8, 512], F32R),
                      sb(esM, "wmod_r1", [128, 8, 512], F32R)]
            modrow = sb(esM, "modrow", [1, 6 * D], F32)
            tmprow = sb(esM, "tmprow", [1, D], F32)
            sgnrow = sb(esM, "sgnrow", [1, D], F32)
            pM = psm(esM, "pM", [1, 512], F32)
            b_cc = sc.buf("cc")
            b_silcr = sc.buf("silc_r")
            b_wmod = [sc.buf("wmod_r0"), sc.buf("wmod_r1")]
            b_modrow = sc.buf("modrow")
            b_tmprow = sc.buf("tmprow")
            b_sgnrow = sc.buf("sgnrow")
            b_pM = sc.buf("pM")
            WL = sc.dma_sem("wl0")
            WL1 = sc.dma_sem("wl1")
            MSP = sc.dma_sem("msp")

            b_silcf = sc.buf("silc_f")
            sc.dma("P", cc[:], CC, WL, writes=[b_cc])
            sc.op("A", lambda e: e.activation(out=silc_f[:], in_=cc[:],
                                              func=AF.Silu),
                  reads=[b_cc], writes=[b_silcf])
            sc.op("V", lambda e: e.tensor_copy(out=silc_r[:], in_=silc_f[:]),
                  reads=[b_silcf], writes=[b_silcr])
            sc.dma("P", wmod_r[0][:], WMOD[0], WL, writes=[b_wmod[0]])
            sc.dma("P", wmod_r[1][:], WMOD[1], WL1, writes=[b_wmod[1]])
            for ch in range(12):
                par = ch % 2
                for k in range(8):
                    sc.op("T", lambda e, k=k, par=par: e.matmul(
                        pM[:], silc_r[:, k:k + 1], wmod_r[par][:, k, :],
                        start=(k == 0), stop=(k == 7)),
                        reads=[b_silcr, b_wmod[par]],
                        writes=([b_pM] if k == 0 else []),
                        writes_d=([b_pM] if k > 0 else []),
                        inc=(1 if k == 7 else 0))
                if ch + 2 < 12:
                    sc.dma("P", wmod_r[par][:], WMOD[ch + 2],
                           [WL, WL1][par], writes=[b_wmod[par]])
                sc.op("V", lambda e, ch=ch: e.tensor_copy(
                    out=modrow[:, ch * 512:(ch + 1) * 512], in_=pM[:]),
                    reads=[b_pM], writes_d=[b_modrow])
            for off in (1 * D, 4 * D):
                sc.op("V", lambda e, off=off: e.tensor_scalar(
                    out=modrow[:, off:off + D], in0=modrow[:, off:off + D],
                    scalar1=1.0, scalar2=None, op0=ALU.add),
                    reads=[b_modrow], writes_d=[b_modrow])
            for off in (2 * D, 5 * D):
                sc.op("V", lambda e, off=off: e.tensor_scalar(
                    out=sgnrow[:], in0=modrow[:, off:off + D], scalar1=0.0,
                    scalar2=2.0, op0=ALU.is_ge, op1=ALU.mult),
                    reads=[b_modrow], writes=[b_sgnrow])
                sc.op("V", lambda e: e.tensor_scalar(
                    out=sgnrow[:], in0=sgnrow[:], scalar1=-1.0, scalar2=None,
                    op0=ALU.add), reads=[b_sgnrow], writes_d=[b_sgnrow])
                sc.op("A", lambda e, off=off: e.activation(
                    out=tmprow[:], in_=modrow[:, off:off + D], func=AF.Abs),
                    reads=[b_modrow], writes=[b_tmprow])
                sc.op("V", lambda e: e.tensor_scalar(
                    out=tmprow[:], in0=tmprow[:], scalar1=0.25,
                    scalar2=None, op0=ALU.max),
                    reads=[b_tmprow], writes_d=[b_tmprow])
                sc.op("V", lambda e, off=off: e.tensor_tensor(
                    out=modrow[:, off:off + D], in0=tmprow[:], in1=sgnrow[:],
                    op=ALU.mult),
                    reads=[b_tmprow, b_sgnrow, b_modrow], writes_d=[b_modrow])
            sc.dma("S", MODS, modrow[:], MSP, reads=[b_modrow])
            sc.barrier()

        with ExitStack() as esA:
            cosT = sb(esA, "cosT", [128, S], BF16)
            sinT = sb(esA, "sinT", [128, S], BF16)
            qs16 = sb(esA, "qs16", [16, S], F32R)
            ks16 = sb(esA, "ks16", [16, S], F32R)
            selq = sb(esA, "selq", [16, 8, 128], F32R)
            masks = sb(esA, "masks", [128, 4, 512], BF16)
            ptm = sb(esA, "ptm", [128, 128], BF16)
            bq = sb(esA, "bq", [128, 8], F32)
            bk = sb(esA, "bk", [128, 8], F32)
            saB = sb(esA, "saB", [128, D], F32)
            caB = sb(esA, "caB", [128, D], F32)
            gaB = sb(esA, "gaB", [128, D], F32)
            hT = sb(esA, "hT", [128, 8, S], BF16)
            b_cos = sc.buf("cosT"); b_sin = sc.buf("sinT")
            b_qs16 = sc.buf("qs16"); b_ks16 = sc.buf("ks16")
            b_selq = sc.buf("selq"); b_masks = sc.buf("masks")
            b_ptm = sc.buf("ptm"); b_bq = sc.buf("bq"); b_bk = sc.buf("bk")
            b_saB = sc.buf("saB"); b_caB = sc.buf("caB"); b_gaB = sc.buf("gaB")
            b_hT = sc.buf("hT")

            AL = sc.dma_sem("al")
            sc.load_group(AL, [
                ("P", cosT[:], COSB, b_cos),
                ("P", sinT[:], SINB, b_sin),
                ("P", qs16[:], QS16, b_qs16),
                ("P", ks16[:], KS16, b_ks16),
                ("P", selq[:].rearrange("a k p -> a (k p)"), SELQ, b_selq),
                ("S", masks[:].rearrange("p r n -> p (r n)"), MASKS, b_masks),
                ("S", ptm[:], PTD, b_ptm),
                ("S", bq[:], BQ, b_bq),
                ("S", bk[:], BK, b_bk),
            ])

            with ExitStack() as esL:
                modr = sb(esL, "modr", [1, D], F32R)
                b_modr = sc.buf("modr")
                MRL = sc.dma_sem("mrl")
                for (dst, bdst, off) in ((saB, b_saB, 0), (caB, b_caB, D),
                                         (gaB, b_gaB, 2 * D)):
                    sc.dma("P", modr[:], MODS[:, off:off + D], MRL,
                           writes=[b_modr])
                    bcast_tile(dst, bdst, modr, b_modr, D, pWO, b_pWO)
                xt = [sb(esL, "xt0", [128, D], F32), sb(esL, "xt1", [128, D], F32)]
                lnsq = sb(esL, "lnsq", [128, D], F32)
                lntmp = sb(esL, "lntmp", [128, D], F32)
                lnh = [sb(esL, "lnh0", [128, D], BF16),
                       sb(esL, "lnh1", [128, D], BF16)]
                scl = [sb(esL, "scl0", [128, 4], F32),
                       sb(esL, "scl1", [128, 4], F32)]
                pT1 = psm(esL, "pT1", [128, D], BF16)
                b_xt = [sc.buf("xt0"), sc.buf("xt1")]
                b_lnsq = sc.buf("lnsq"); b_lntmp = sc.buf("lntmp")
                b_lnh = [sc.buf("lnh0"), sc.buf("lnh1")]
                b_scl = [sc.buf("scl0"), sc.buf("scl1")]
                b_pT1 = sc.buf("pT1")
                XL = [sc.dma_sem("xl0"), sc.dma_sem("xl1")]

                def ln_tile(xsrc, bx, cB, bcB, sB, bsB, par):
                    scp, bscp = scl[par], b_scl[par]
                    lnhp, blnh = lnh[par], b_lnh[par]
                    sc.op("V", lambda e: e.reduce_sum(out=scp[:, 0:1], in_=xsrc[:],
                                                      axis=AX.X),
                          reads=[bx], writes=[bscp])
                    sc.op("A", lambda e: e.activation(out=lnsq[:], in_=xsrc[:],
                                                      func=AF.Square,
                                                      accum_out=scp[:, 1:2]),
                          reads=[bx], writes=[b_lnsq], writes_d=[bscp])
                    sc.op("V", lambda e: e.tensor_scalar(
                        out=scp[:, 2:3], in0=scp[:, 0:1], scalar1=1.0 / D,
                        scalar2=None, op0=ALU.mult),
                        reads=[bscp], writes_d=[bscp])
                    sc.op("V", lambda e: e.tensor_tensor(
                        out=scp[:, 3:4], in0=scp[:, 2:3], in1=scp[:, 2:3],
                        op=ALU.mult), reads=[bscp], writes_d=[bscp])
                    sc.op("V", lambda e: e.scalar_tensor_tensor(
                        out=scp[:, 3:4], in0=scp[:, 1:2], scalar=1.0 / D,
                        in1=scp[:, 3:4], op0=ALU.mult, op1=ALU.subtract),
                        reads=[bscp], writes_d=[bscp])
                    sc.op("A", lambda e: e.activation(out=scp[:, 3:4],
                                                      in_=scp[:, 3:4],
                                                      func=AF.Sqrt, bias=epsc[:]),
                          reads=[bscp, b_epsc], writes_d=[bscp])
                    sc.op("V", lambda e: e.reciprocal(out=scp[:, 3:4],
                                                      in_=scp[:, 3:4]),
                          reads=[bscp], writes_d=[bscp])
                    sc.op("V", lambda e: e.tensor_scalar(
                        out=lntmp[:], in0=xsrc[:], scalar1=scp[:, 2:3],
                        scalar2=scp[:, 3:4], op0=ALU.subtract, op1=ALU.mult),
                        reads=[bx, bscp], writes=[b_lntmp])
                    sc.op("V", lambda e: e.tensor_tensor(
                        out=lntmp[:], in0=lntmp[:], in1=cB[:], op=ALU.mult),
                        reads=[b_lntmp, bcB], writes_d=[b_lntmp])
                    sc.op("V", lambda e: e.tensor_tensor(
                        out=lnhp[:], in0=lntmp[:], in1=sB[:], op=ALU.add),
                        reads=[b_lntmp, bsB], writes=[blnh])

                def ln_tp(i):
                    par = i % 2
                    for k in range(8):
                        sc.op("T", lambda e, k=k, par=par: e.transpose(
                            pT1[:, k * 128:(k + 1) * 128],
                            lnh[par][:, k * 128:(k + 1) * 128], idnb[:]),
                            reads=[b_lnh[par], b_idnb],
                            writes=([b_pT1] if k == 0 else []),
                            writes_d=([b_pT1] if k > 0 else []),
                            inc=(1 if k == 7 else 0))
                    sc.op("A", lambda e, i=i: e.activation(
                        out=hT[:, :, i * 128:(i + 1) * 128],
                        in_=pT1[:].rearrange("p (k t) -> p k t", k=8),
                        func=AF.Copy),
                        reads=[b_pT1], writes_d=[b_hT])

                for i in range(NT + 1):
                    par = i % 2
                    if i < NT:
                        sc.dma("S", xt[par][:], X[i * 128:(i + 1) * 128, :],
                               XL[par], writes=[b_xt[par]])
                        ln_tile(xt[par], b_xt[par], caB, b_caB, saB, b_saB, par)
                    if i > 0:
                        ln_tp(i - 1)
                sc.barrier()

            qrT = sb(esA, "qrT", [128, 2, S], BF16)
            krT = sb(esA, "krT", [128, 2, S], BF16)
            v_aug = sb(esA, "v_aug", [128, NT, 4, 129], BF16)
            gts = sb(esA, "gts", [128, NT, 512], BF16)
            qkT = sb(esA, "qkT", [128, NT, 512], BF16)
            avst = sb(esA, "avst", [128, NT, 129], F32)
            sqd = sb(esA, "sqd", [128, 128], F32)
            gatc = sb(esA, "gatc", [128, NT, 128], BF16)
            gatedT = sb(esA, "gatedT", [128, 4, S], BF16)
            amean = sb(esA, "amean", [128, NT], F32)
            asum2 = sb(esA, "asum2", [128, NT], F32)
            aden = sb(esA, "aden", [128, NT], F32)
            atmp = sb(esA, "atmp", [128, NT], F32)
            wq_sl = sb(esA, "wq_sl", [128, 8, 256], BF16)
            wk_sl = sb(esA, "wk_sl", [128, 8, 256], BF16)
            wvg_sl = sb(esA, "wvg_sl", [128, 8, 512], BF16)
            wo_sl = sb(esA, "wo_sl", [128, 4, D], BF16)
            bvrow = sb(esA, "bvrow", [1, 512], BF16)
            bgrow = sb(esA, "bgrow", [1, 512], BF16)
            qs_bf = sb(esA, "qs_bf", [128, 512], BF16)
            t1b = sb(esA, "t1b", [128, 512], BF16)
            t2b = sb(esA, "t2b", [128, 512], BF16)
            spl_t = sb(esA, "spl0", [128, D], F32)
            spl_t2 = sb(esA, "spl1", [128, D], F32)
            borow = spl_t[0:1, :]
            spl = [spl_t, spl_t2]
            b_qrT = sc.buf("qrT"); b_krT = sc.buf("krT"); b_vaug = sc.buf("v_aug")
            b_gts = sc.buf("gts")
            b_qkTj = [sc.buf(f"qkTj{j}") for j in range(NT)]
            b_gatedT = sc.buf("gatedT")
            b_avst = sc.buf("avst")
            b_sqd = sc.buf("sqd"); b_gatc = sc.buf("gatc")
            b_amean = sc.buf("amean"); b_asum2 = sc.buf("asum2")
            b_aden = sc.buf("aden"); b_atmp = sc.buf("atmp")
            b_wq = sc.buf("wq_sl"); b_wk = sc.buf("wk_sl"); b_wvg = sc.buf("wvg_sl")
            b_wo = sc.buf("wo_sl")
            b_bvrow = sc.buf("bvrow"); b_bgrow = sc.buf("bgrow")
            b_qsbf = sc.buf("qs_bf"); b_t1 = sc.buf("t1b"); b_t2 = sc.buf("t2b")
            b_spl_t = sc.buf("spl0")
            b_spl = [b_spl_t, sc.buf("spl1")]
            WQL = sc.dma_sem("wql"); WKL = sc.dma_sem("wkl")
            WVL = sc.dma_sem("wvl"); BVL = sc.dma_sem("bvl"); BGL = sc.dma_sem("bgl")
            WOL = sc.dma_sem("wol"); BOL = sc.dma_sem("bol")
            SPS = [sc.dma_sem("sp0"), sc.dma_sem("sp1")]
            GTS = [sc.dma_sem(f"gtt{q}") for q in range(4)]

            pA0 = psm(esA, "pA0", [128, 512], F32)
            pA1 = psm(esA, "pA1", [128, 512], F32)
            pA2 = psm(esA, "pA2", [128, 512], F32)
            pA3 = psm(esA, "pA3", [128, 512], F32)
            pAV = [psm(esA, f"pAV{j}", [128, 132], F32) for j in range(2)]
            b_pA0 = sc.buf("pA0"); b_pA1 = sc.buf("pA1")
            b_pA2 = sc.buf("pA2"); b_pA3 = sc.buf("pA3")
            b_pAV = [sc.buf("pAV0"), sc.buf("pAV1")]

            for hh in range(4):
                sc.op("V", lambda e, hh=hh: e.memset(v_aug[:, :, hh, 128:129], 1.0),
                      writes_d=[b_vaug])

            def a_loads(g):
                sc.dma("P", wq_sl[:], Wq[g], WQL, writes=[b_wq])
                sc.dma("P", wk_sl[:], Wk[g], WKL, writes=[b_wk])
                sc.dma("P", wvg_sl[:], Wv[g], WVL, writes=[b_wvg])
                sc.dma("P", bvrow[:], BV[:, g * 512:(g + 1) * 512], BVL,
                       writes=[b_bvrow])
                sc.dma("P", bgrow[:], BGA[:, g * 512:(g + 1) * 512], BGL,
                       writes=[b_bgrow])

            def qk_proj(g, wsl, bw, bias_sb, bbias, sc16, bsc16, dstT, bdst):
                for m in range(2):
                    dt_ = g * 2 + m
                    for chk in range(4):
                        for k in range(8):
                            sc.op("T", lambda e, m=m, chk=chk, k=k, wsl=wsl:
                                  e.matmul(
                                pA0[:], wsl[:, k, m * 128:(m + 1) * 128],
                                hT[:, k, chk * 512:(chk + 1) * 512],
                                start=(k == 0), stop=(k == 7)),
                                reads=[bw, b_hT],
                                writes=([b_pA0] if k == 0 else []),
                                writes_d=([b_pA0] if k > 0 else []),
                                inc=(1 if k == 7 else 0))
                        sc.op("T", lambda e, dt_=dt_, chk=chk, sc16=sc16:
                              e.matmul(
                            pA2[:], selq[:, dt_, :],
                            sc16[:, chk * 512:(chk + 1) * 512],
                            start=True, stop=True),
                            reads=[b_selq, bsc16], writes=[b_pA2])
                        sc.op("A", lambda e, dt_=dt_, bias_sb=bias_sb:
                              e.activation(
                            out=qs_bf[:], in_=pA0[:], func=AF.Identity,
                            bias=bias_sb[:, dt_:dt_ + 1]),
                            reads=[b_pA0, bbias], writes=[b_qsbf])
                        sc.op("T", lambda e: e.matmul(
                            pA1[:], ptm[:], qs_bf[:], start=True, stop=True),
                            reads=[b_ptm, b_qsbf], writes=[b_pA1])
                        sc.op("V", lambda e, chk=chk: e.tensor_tensor(
                            out=t1b[:], in0=qs_bf[:],
                            in1=cosT[:, chk * 512:(chk + 1) * 512],
                            op=ALU.mult),
                            reads=[b_qsbf, b_cos], writes=[b_t1])
                        sc.op("V", lambda e, chk=chk: e.tensor_tensor(
                            out=t2b[:], in0=pA1[:],
                            in1=sinT[:, chk * 512:(chk + 1) * 512],
                            op=ALU.mult),
                            reads=[b_pA1, b_sin], writes=[b_t2])
                        sc.op("V", lambda e: e.tensor_tensor(
                            out=t2b[:], in0=t2b[:], in1=t1b[:], op=ALU.add),
                            reads=[b_t1, b_t2], writes_d=[b_t2])
                        sc.op("V", lambda e, m=m, chk=chk, dstT=dstT:
                              e.tensor_tensor(
                            out=dstT[:, m, chk * 512:(chk + 1) * 512],
                            in0=t2b[:], in1=pA2[:], op=ALU.mult),
                            reads=[b_t2, b_pA2], writes_d=[bdst])

            def a_proj(g):
                qk_proj(g, wq_sl, b_wq, bq, b_bq, qs16, b_qs16, qrT, b_qrT)
                qk_proj(g, wk_sl, b_wk, bk, b_bk, ks16, b_ks16, krT, b_krT)
                pvg = [pA0, pA1]
                bpvg = [b_pA0, b_pA1]
                for i in range(NT):
                    pv, bpv = pvg[i % 2], bpvg[i % 2]
                    for k in range(8):
                        sc.op("T", lambda e, i=i, k=k, pv=pv: e.matmul(
                            pv[:], hT[:, k, i * 128:(i + 1) * 128],
                            wvg_sl[:, k, :], start=(k == 0), stop=False),
                            reads=[b_wvg, b_hT],
                            writes=([bpv] if k == 0 else []),
                            writes_d=([bpv] if k > 0 else []),
                            inc=(1 if k == 7 else 0))
                    sc.op("T", lambda e, pv=pv: e.matmul(
                        pv[:], onesc_b[:], bvrow[:], start=False, stop=True),
                        reads=[b_onesb, b_bvrow], writes_d=[bpv])
                    sc.op("A", lambda e, i=i, pv=pv: e.activation(
                        out=v_aug[:, i, :, 0:128],
                        in_=pv[:].rearrange("p (h t) -> p h t", h=4),
                        func=AF.Copy),
                        reads=[bpv], writes_d=[b_vaug])
                sc.dma("P", wvg_sl[:], Wg[g], WVL, writes=[b_wvg])
                for i in range(NT):
                    pv, bpv = pvg[i % 2], bpvg[i % 2]
                    for k in range(8):
                        sc.op("T", lambda e, i=i, k=k, pv=pv: e.matmul(
                            pv[:], hT[:, k, i * 128:(i + 1) * 128],
                            wvg_sl[:, k, :], start=(k == 0), stop=False),
                            reads=[b_wvg, b_hT],
                            writes=([bpv] if k == 0 else []),
                            writes_d=([bpv] if k > 0 else []),
                            inc=(1 if k == 7 else 0))
                    sc.op("T", lambda e, pv=pv: e.matmul(
                        pv[:], onesc_b[:], bgrow[:], start=False, stop=True),
                        reads=[b_onesb, b_bgrow], writes_d=[bpv])
                    sc.op("A", lambda e, i=i, pv=pv: e.activation(
                        out=gts[:, i, :], in_=pv[:], func=AF.Silu),
                        reads=[bpv], writes_d=[b_gts])

            pq = [pA0, pA1, pA2, pA3]
            bpq = [b_pA0, b_pA1, b_pA2, b_pA3]

            def a_qkb(hl, c):
                mrow = hl // 2
                prow = 64 * (hl % 2)
                jmax = 4 * c + 4
                for j in range(jmax if 'qk' in DBG_HEADS else 0):
                    jj = j % 4
                    sc.op("T", lambda e, j=j, c=c, jj=jj, mrow=mrow,
                          prow=prow: e.matmul(
                        pq[jj][:],
                        krT[prow:prow + 64, mrow, j * 128:(j + 1) * 128],
                        qrT[prow:prow + 64, mrow, c * 512:(c + 1) * 512],
                        start=True, stop=True),
                        reads=[b_krT, b_qrT], writes=[bpq[jj]])
                    r = j - 4 * c
                    if 'evac' not in DBG_HEADS:
                        continue
                    if r >= 0:
                        sc.op("V", lambda e, j=j, r=r, jj=jj: e.tensor_tensor(
                            out=qkT[:, j, :], in0=pq[jj][:],
                            in1=masks[:, r, :], op=ALU.mult),
                            reads=[bpq[jj], b_masks],
                            writes=[b_qkTj[j]])
                    elif j % 2 == 0:
                        sc.op("V", lambda e, j=j, jj=jj: e.tensor_copy(
                            out=qkT[:, j, :], in_=pq[jj][:]),
                            reads=[bpq[jj]], writes=[b_qkTj[j]])
                    else:
                        sc.op("A", lambda e, j=j, jj=jj: e.activation(
                            out=qkT[:, j, :], in_=pq[jj][:],
                            func=AF.Copy),
                            reads=[bpq[jj]], writes=[b_qkTj[j]])

            def a_avb(hl, c):
                for r in range(4 if 'av' in DBG_HEADS else 0):
                    st = 4 * c + r
                    pa = pAV[st % 2]
                    bpa = b_pAV[st % 2]
                    njs = st + 1
                    for j in range(njs):
                        sc.op("T", lambda e, j=j, r=r, hl=hl, pa=pa,
                              njs=njs: e.matmul(
                            pa[:, 0:129],
                            qkT[:, j, r * 128:(r + 1) * 128],
                            v_aug[:, j, hl, 0:129],
                            start=(j == 0), stop=(j == njs - 1)),
                            reads=[b_qkTj[j], b_vaug],
                            writes=([bpa] if j == 0 else []),
                            writes_d=([bpa] if j > 0 else []))
                    if 'epi' not in DBG_HEADS:
                        continue
                    sc.op("A", lambda e, pa=pa, st=st: e.activation(
                        out=avst[:, st, 0:129], in_=pa[:, 0:129],
                        func=AF.Copy),
                        reads=[bpa], writes_d=[b_avst])

            def a_epi(hl):
                av0 = avst[:, :, 0:128]
                dcol = avst[:, :, 128:129].rearrange("p s o -> p (s o)")
                for st in range(NT):
                    sc.op("A", lambda e, st=st: e.activation(
                        out=sqd[:], in_=avst[:, st, 0:128],
                        func=AF.Square, accum_out=asum2[:, st:st + 1]),
                        reads=([b_avst] if st == 0 else
                               [b_avst, b_sqd]),
                        writes=([b_sqd, b_asum2] if st == 0 else []),
                        writes_d=([b_sqd, b_asum2] if st > 0 else []))
                sc.op("V", lambda e: e.reduce_sum(out=amean[:], in_=av0,
                                                  axis=AX.X),
                      reads=[b_avst], writes=[b_amean])
                sc.op("V", lambda e: e.tensor_scalar(
                    out=amean[:], in0=amean[:], scalar1=1.0 / 128,
                    scalar2=None, op0=ALU.mult),
                    reads=[b_amean], writes_d=[b_amean])
                sc.op("V", lambda e: e.tensor_scalar(
                    out=aden[:], in0=dcol, scalar1=-1.0, scalar2=1.0,
                    op0=ALU.mult, op1=ALU.max),
                    reads=[b_avst], writes=[b_aden])
                sc.op("V", lambda e: e.tensor_tensor(
                    out=aden[:], in0=dcol, in1=aden[:], op=ALU.max),
                    reads=[b_avst, b_aden], writes_d=[b_aden])
                sc.op("V", lambda e: e.tensor_tensor(
                    out=aden[:], in0=aden[:], in1=aden[:], op=ALU.mult),
                    reads=[b_aden], writes_d=[b_aden])
                sc.op("V", lambda e: e.tensor_scalar(
                    out=asum2[:], in0=asum2[:], scalar1=1.0 / 128,
                    scalar2=None, op0=ALU.mult),
                    reads=[b_asum2], writes_d=[b_asum2])
                sc.op("V", lambda e: e.tensor_tensor(
                    out=atmp[:], in0=amean[:], in1=amean[:], op=ALU.mult),
                    reads=[b_amean], writes=[b_atmp])
                sc.op("V", lambda e: e.tensor_tensor(
                    out=asum2[:], in0=asum2[:], in1=atmp[:],
                    op=ALU.subtract),
                    reads=[b_atmp, b_asum2], writes_d=[b_asum2])
                sc.op("V", lambda e: e.scalar_tensor_tensor(
                    out=asum2[:], in0=aden[:], scalar=EPS, in1=asum2[:],
                    op0=ALU.mult, op1=ALU.add),
                    reads=[b_aden, b_asum2], writes_d=[b_asum2])
                sc.op("A", lambda e: e.activation(
                    out=asum2[:], in_=asum2[:], func=AF.Sqrt),
                    reads=[b_asum2], writes_d=[b_asum2])
                sc.op("V", lambda e: e.reciprocal(out=asum2[:], in_=asum2[:]),
                      reads=[b_asum2], writes_d=[b_asum2])
                sc.op("V", lambda e: e.tensor_tensor(
                    out=av0, in0=av0,
                    in1=amean[:].unsqueeze(2).to_broadcast([128, NT, 128]),
                    op=ALU.subtract),
                    reads=[b_amean, b_avst], writes_d=[b_avst])
                sc.op("V", lambda e: e.tensor_tensor(
                    out=av0, in0=av0,
                    in1=asum2[:].unsqueeze(2).to_broadcast([128, NT, 128]),
                    op=ALU.mult),
                    reads=[b_asum2, b_avst], writes_d=[b_avst])
                sc.op("V", lambda e, hl=hl: e.tensor_tensor(
                    out=gatc[:], in0=av0,
                    in1=gts[:].rearrange("p s (h t) -> p s h t", h=4)[
                        :, :, hl, :],
                    op=ALU.mult),
                    reads=[b_avst, b_gts], writes=[b_gatc])
                if 'tp' in DBG_HEADS:
                    for st in range(NT):
                        sc.dma("S", gatedT[:, hl, st * 128:(st + 1) * 128],
                               gatc[:, st, :], GTS[st % 4], reads=[b_gatc],
                               writes_d=[b_gatedT], transpose=True)

            def a_heads(g):
                for hl in range(4):
                    a_qkb(hl, 0)
                    if hl > 0 and 'epi' in DBG_HEADS:
                        a_epi(hl - 1)
                    a_avb(hl, 0)
                    for c in range(1, 4):
                        a_qkb(hl, c)
                        a_avb(hl, c)
                if 'epi' in DBG_HEADS:
                    a_epi(3)

            def a_wo_prep(g):
                sc.dma("P", wo_sl[:], Wo[g], WOL, writes=[b_wo])
                for kk in range(4):
                    sc.op("V", lambda e, kk=kk: e.tensor_tensor(
                        out=wo_sl[:, kk, :], in0=wo_sl[:, kk, :], in1=gaB[:],
                        op=ALU.mult), reads=[b_gaB, b_wo], writes_d=[b_wo])
                if g == 0:
                    sc.dma("S", borow, BO, BOL, writes=[b_spl_t])
                    sc.op("V", lambda e: e.tensor_tensor(
                        out=bvrow[:], in0=borow[:, 0:512],
                        in1=gaB[0:1, 0:512], op=ALU.mult),
                        reads=[b_spl_t, b_gaB], writes=[b_bvrow])
                    sc.op("V", lambda e: e.tensor_tensor(
                        out=bgrow[:], in0=borow[:, 512:1024],
                        in1=gaB[0:1, 512:1024], op=ALU.mult),
                        reads=[b_spl_t, b_gaB], writes=[b_bgrow])

            b_pWOh = [sc.buf("pWOh0"), sc.buf("pWOh1")]

            def a_wo_tiles(g):
                for i in range(NT):
                    par = i % 2
                    for half in range(2):
                        for kk in range(4):
                            first = (kk == 0)
                            stop_here = (kk == 3 and g != 0)
                            sc.op("T", lambda e, i=i, kk=kk, half=half,
                                  stop_here=stop_here: e.matmul(
                                pWO[:, half * 512:(half + 1) * 512],
                                gatedT[:, kk, i * 128:(i + 1) * 128],
                                wo_sl[:, kk, half * 512:(half + 1) * 512],
                                start=(kk == 0), stop=stop_here),
                                reads=[b_gatedT, b_wo],
                                writes=([b_pWOh[half]] if first else []),
                                writes_d=([b_pWOh[half]] if not first else []),
                                inc=(1 if kk == 3 else 0))
                        if g == 0:
                            brow, bbrow = [(bvrow, b_bvrow),
                                           (bgrow, b_bgrow)][half]
                            sc.op("T", lambda e, brow=brow, half=half: e.matmul(
                                pWO[:, half * 512:(half + 1) * 512], onesc_b[:],
                                brow[:], start=False, stop=True),
                                reads=[b_onesb, bbrow],
                                writes_d=[b_pWOh[half]])
                        sc.op("V", lambda e, par=par, half=half: e.tensor_copy(
                            out=spl[par][:, half * 512:(half + 1) * 512],
                            in_=pWO[:, half * 512:(half + 1) * 512]),
                            reads=[b_pWOh[half]],
                            writes=([b_spl[par]] if half == 0 else []),
                            writes_d=([b_spl[par]] if half == 1 else []))
                    sc.dma("S", PG[g][i * 128:(i + 1) * 128, :], spl[par][:],
                           SPS[par], reads=[b_spl[par]])

            a_loads(0)
            a_proj(0)
            for g in range(4):
                a_wo_prep(g)
                a_heads(g)
                if g == 0:
                    a_wo_tiles(0)
                    a_loads(1)
                    a_proj(1)
                else:
                    if g + 1 < 4:
                        a_loads(g + 1)
                        a_proj(g + 1)
                    a_wo_tiles(g)
            sc.barrier()

        with ExitStack() as esF:
            det = sb(esF, "det", [128, 8, NEXP], BF16)
            ue = sb(esF, "ue", [128, 32, D], BF16)
            wqf = sb(esF, "wqf", [128, 8, 512], F32R)
            keys = sb(esF, "keys", [128, 8, 64], F32)
            sbB = sb(esF, "sbB", [128, D], F32)
            cbB = sb(esF, "cbB", [128, D], F32)
            b_det = sc.buf("det"); b_ue = sc.buf("ue"); b_wqf = sc.buf("wqf")
            b_keys = sc.buf("keys")
            b_sbB = sc.buf("sbB"); b_cbB = sc.buf("cbB")
            FL = sc.dma_sem("fl")
            mtri2 = sb(esF, "mtri2", [128, 32, 4], F32)
            b_mtri2 = sc.buf("mtri2")
            sc.load_group(FL, [
                ("P", det[:], DET, b_det),
                ("P", ue[:], UEW, b_ue),
                ("P", wqf[:], FWq, b_wqf),
                ("S", keys[:].rearrange("p a k -> p (a k)"), KEYS, b_keys),
                ("S", mtri2[:].rearrange("p a b -> p (a b)"), MTRI2, b_mtri2),
            ])
            with ExitStack() as esF0:
                modr2 = sb(esF0, "modr2", [1, D], F32R)
                b_modr2 = sc.buf("modr2")
                MRL2 = sc.dma_sem("mrl2")
                for (dstp, bdst, off) in ((sbB, b_sbB, 3 * D), (cbB, b_cbB, 4 * D)):
                    sc.dma("P", modr2[:], MODS[:, off:off + D], MRL2,
                           writes=[b_modr2])
                    bcast_tile(dstp, bdst, modr2, b_modr2, D, pWO, b_pWO)
                sc.barrier()

            pgl = sb(esF, "pgl0", [128, D], F32)
            lnh2 = sb(esF, "lnh2", [128, D], BF16)
            keep2 = sb(esF, "keep2", [128, D], F32)
            scl2 = sb(esF, "scl2", [128, 4], F32)
            h2t = sb(esF, "h2t", [128, 8, 128], BF16)
            h2tf = sb(esF, "h2tf", [128, 8, 128], F32R)
            qft = sb(esF, "qft", [128, 4, 128], F32)
            simsbd = [sb(esF, "simsb0", [128, 512], F32),
                      sb(esF, "simsb1", [128, 512], F32)]
            qfed = [s[0:64, :].rearrange("p (a b) -> p a b", a=4)
                    for s in simsbd]
            mx1 = sb(esF, "mx1", [128, 2, 4, 8], F32)
            ix1u = sb(esF, "ix1u", [128, 2, 4, 8], U32)
            ix1f = sb(esF, "ix1f", [128, 2, 4, 8], F32)
            sc8 = sb(esF, "sc8", [128, 4, 8], F32)
            pku = sb(esF, "pku", [128, 4, 8], U32)
            iu = sb(esF, "iu", [128, 4, 8], U32)
            if_ = sb(esF, "if_", [128, 4, 8], F32)
            jf = sb(esF, "jf", [128, 4, 8], F32)
            cmp8 = sb(esF, "cmp8", [128, 4, 8, 8], BF16)
            idxf = sb(esF, "idxf", [128, 4, 8], F32)
            tmp48 = sb(esF, "tmp48", [128, 4, 8], F32)
            wsm = sb(esF, "wsm", [128, 4, 8], F32)
            zrow = sb(esF, "zrow", [128, 4, 1], F32)
            eq5 = sb(esF, "eq5", [128, 32, 4, 8], F32)
            kr2 = sb(esF, "kr2", [128, 32], F32)
            rt4 = sb(esF, "rt4", [128, 32, 4], F32)
            idx16 = sb(esF, "idx16", [128, 32], I16)
            w32b = sb(esF, "w32b", [128, 32], BF16)
            qsc1 = sb(esF, "qsc1", [128, 32], F32)
            qsc2 = sb(esF, "qsc2", [128, 32], F32)
            Sbf = sb(esF, "Sbf", [128, NEXP], BF16)
            wdense = sb(esF, "wdense", [128, NEXP], BF16)
            b_pgl = sc.buf("pgl0")
            b_lnh2 = sc.buf("lnh2"); b_keep2 = sc.buf("keep2"); b_scl2 = sc.buf("scl2")
            b_h2t = sc.buf("h2t"); b_h2tf = sc.buf("h2tf"); b_qft = sc.buf("qft")
            b_simsbd = [sc.buf("simsb0"), sc.buf("simsb1")]
            b_mx1 = sc.buf("mx1"); b_ix1u = sc.buf("ix1u"); b_ix1f = sc.buf("ix1f")
            b_sc8 = sc.buf("sc8"); b_pku = sc.buf("pku")
            b_iu = sc.buf("iu"); b_if = sc.buf("if_"); b_jf = sc.buf("jf")
            b_cmp8 = sc.buf("cmp8"); b_idxf = sc.buf("idxf")
            b_tmp48 = sc.buf("tmp48"); b_wsm = sc.buf("wsm"); b_zrow = sc.buf("zrow")
            b_eq5 = sc.buf("eq5")
            b_kr2 = sc.buf("kr2"); b_rt4 = sc.buf("rt4")
            b_idx16 = sc.buf("idx16"); b_w32b = sc.buf("w32b")
            b_qsc1 = sc.buf("qsc1"); b_qsc2 = sc.buf("qsc2")
            b_Sq = [sc.buf(f"Sbf{q}") for q in range(4)]
            b_wd = [sc.buf(f"wd{ec}") for ec in range(8)]
            wdT = Sbf[:].rearrange("p (e t) -> p e t", e=32)
            b_wdT = b_Sq
            PGL0 = sc.dma_sem("pgl0s")
            PGL1 = sc.dma_sem("pgl1s")
            X2W = sc.dma_sem("x2w")
            H2W = sc.dma_sem("h2w")
            EXW = sc.dma_sem("exw")

            pF0 = psm(esF, "pF0", [128, 512], F32)
            pF1 = psm(esF, "pF1", [128, 512], F32)
            pTRf = psm(esF, "pTRf", [128, D], F32)
            pTRb = psm(esF, "pTRb", [128, 2048], BF16)
            b_pF0 = sc.buf("pF0"); b_pF1 = sc.buf("pF1")
            b_pTRf = sc.buf("pTRf"); b_pTRb = sc.buf("pTRb")

            def stP1(ti, P, bP, semA, semB):
                sc.dma("S", P[:], PG[0][ti * 128:(ti + 1) * 128, :], semA,
                       writes=[bP])
                for src in (PG[1], PG[2], PG[3], X):
                    sc.dma("P", P[:], src[ti * 128:(ti + 1) * 128, :], semB,
                           reads=[bP], writes_d=[bP], accum=True)
                sc.dma("S", X2S[ti * 128:(ti + 1) * 128, :], P[:], X2W,
                       reads=[bP])

            def stP2(ti, P, bP):
                x2s, bx2 = P, bP
                sc.op("V", lambda e: e.reduce_sum(out=scl2[:, 0:1], in_=x2s[:],
                                                  axis=AX.X),
                      reads=[bx2], writes=[b_scl2])
                sc.op("A", lambda e: e.activation(out=keep2[:], in_=x2s[:],
                                                  func=AF.Square,
                                                  accum_out=scl2[:, 1:2]),
                      reads=[bx2], writes=[b_keep2], writes_d=[b_scl2])
                sc.op("V", lambda e: e.tensor_scalar(
                    out=scl2[:, 2:3], in0=scl2[:, 0:1], scalar1=1.0 / D,
                    scalar2=None, op0=ALU.mult),
                    reads=[b_scl2], writes_d=[b_scl2])
                sc.op("V", lambda e: e.tensor_tensor(
                    out=scl2[:, 3:4], in0=scl2[:, 2:3], in1=scl2[:, 2:3],
                    op=ALU.mult), reads=[b_scl2], writes_d=[b_scl2])
                sc.op("V", lambda e: e.scalar_tensor_tensor(
                    out=scl2[:, 3:4], in0=scl2[:, 1:2], scalar=1.0 / D,
                    in1=scl2[:, 3:4], op0=ALU.mult, op1=ALU.subtract),
                    reads=[b_scl2], writes_d=[b_scl2])
                sc.op("A", lambda e: e.activation(out=scl2[:, 3:4],
                                                  in_=scl2[:, 3:4],
                                                  func=AF.Sqrt, bias=epsc[:]),
                      reads=[b_scl2, b_epsc], writes_d=[b_scl2])
                sc.op("V", lambda e: e.reciprocal(out=scl2[:, 3:4],
                                                  in_=scl2[:, 3:4]),
                      reads=[b_scl2], writes_d=[b_scl2])
                sc.op("V", lambda e: e.tensor_scalar(
                    out=keep2[:], in0=x2s[:], scalar1=scl2[:, 2:3],
                    scalar2=scl2[:, 3:4], op0=ALU.subtract, op1=ALU.mult),
                    reads=[bx2, b_scl2], writes=[b_keep2])
                sc.op("V", lambda e: e.tensor_tensor(
                    out=keep2[:], in0=keep2[:], in1=cbB[:], op=ALU.mult),
                    reads=[b_keep2, b_cbB], writes_d=[b_keep2])
                sc.op("V", lambda e: e.tensor_tensor(
                    out=keep2[:], in0=keep2[:], in1=sbB[:], op=ALU.add),
                    reads=[b_keep2, b_sbB], writes_d=[b_keep2])
                sc.op("V", lambda e: e.tensor_copy(out=lnh2[:], in_=keep2[:]),
                      reads=[b_keep2], writes=[b_lnh2])

            def stP3(ti, hp, bhp):
                for k in range(8):
                    sc.op("T", lambda e, k=k: e.transpose(
                        pTRb[:, k * 128:(k + 1) * 128],
                        lnh2[:, k * 128:(k + 1) * 128], idnb[:]),
                        reads=[b_lnh2, b_idnb],
                        writes=([b_pTRb] if k == 0 else []),
                        writes_d=([b_pTRb] if k > 0 else []),
                        inc=(1 if k == 7 else 0))
                sc.op("V", lambda e, hp=hp: e.tensor_copy(
                    out=hp[:].rearrange("p k t -> p (k t)"),
                    in_=pTRb[:, 0:1024]),
                    reads=[b_pTRb], writes=[bhp])
                sc.dma("S", H2S[:, ti, :, :], hp[:], H2W, reads=[bhp])
                for k in range(8):
                    sc.op("T", lambda e, k=k: e.transpose(
                        pTRf[:, k * 128:(k + 1) * 128],
                        keep2[:, k * 128:(k + 1) * 128], idnf[:]),
                        reads=[b_keep2, b_idnf],
                        writes=([b_pTRf] if k == 0 else []),
                        writes_d=([b_pTRf] if k > 0 else []),
                        inc=(1 if k == 7 else 0))
                sc.op("V", lambda e: e.tensor_copy(
                    out=h2tf[:].rearrange("p k t -> p (k t)"), in_=pTRf[:]),
                    reads=[b_pTRf], writes=[b_h2tf])

            def stP4(ti):
                simsb, b_simsb = simsbd[ti % 2], b_simsbd[ti % 2]
                qfe = qfed[ti % 2]
                for m in range(4):
                    for k in range(8):
                        sc.op("T", lambda e, m=m, k=k: e.matmul(
                            pTRf[:, m * 128:(m + 1) * 128],
                            wqf[:, k, m * 128:(m + 1) * 128],
                            h2tf[:, k, :], start=(k == 0), stop=(k == 7)),
                            reads=[b_wqf, b_h2tf],
                            writes=([b_pTRf] if m == 0 and k == 0 else []),
                            writes_d=([b_pTRf] if m > 0 or k > 0 else []),
                            inc=(1 if k == 7 else 0))
                sc.op("V", lambda e: e.tensor_copy(
                    out=qft[:].rearrange("p m t -> p (m t)"),
                    in_=pTRf[:, 0:512]),
                    reads=[b_pTRf], writes=[b_qft])
                for hh in range(4):
                    p_, h_ = hh // 2, (hh % 2) * 2 + 1
                    rt = p_ * 2 + h_ // 2
                    sc.op("V", lambda e, hh=hh, rt=rt: e.tensor_copy(
                        out=qfe[:, hh, :], in_=qft[64:128, rt, :]),
                        reads=[b_qft], writes_d=[b_simsb])
                for ph in range(8):
                    p_, h_ = ph // 4, ph % 4
                    rt = p_ * 2 + h_ // 2
                    if h_ % 2 == 0:
                        lh = (lambda e, ph=ph, rt=rt: e.matmul(
                            pTRf[:, 512 + ph * 64:576 + ph * 64],
                            qft[0:64, rt, :], keys[0:64, ph, :],
                            start=True, stop=True))
                    else:
                        hh = p_ * 2 + (h_ - 1) // 2
                        lh = (lambda e, ph=ph, hh=hh: e.matmul(
                            pTRf[:, 512 + ph * 64:576 + ph * 64],
                            qfe[:, hh, :], keys[0:64, ph, :],
                            start=True, stop=True))
                    sc.op("T", lh,
                        reads=[b_qft, b_simsb, b_keys],
                        writes_d=[b_pTRf],
                        inc=(1 if ph == 7 else 0))
                sc.op("V", lambda e: e.tensor_copy(out=simsb[:],
                                                   in_=pTRf[:, 512:1024]),
                      reads=[b_pTRf], writes=[b_simsb])

            def stP5(ti):
                simsb, b_simsb = simsbd[ti % 2], b_simsbd[ti % 2]
                all64 = eq5[:].rearrange("p a b c -> p (a b c)")[
                    :, 0:256].rearrange("p (h i) -> p h i", h=4)
                b_all64 = b_eq5
                for ph in range(8):
                    p_, h_ = ph // 4, ph % 4
                    sc.op("V", lambda e, ph=ph, p_=p_, h_=h_: e.max(
                        out=mx1[:, p_, h_, :], in_=simsb[:, ph * 64:(ph + 1) * 64]),
                        reads=[b_simsb], writes_d=[b_mx1])
                for ph in range(8):
                    p_, h_ = ph // 4, ph % 4
                    sc.op("V", lambda e, ph=ph, p_=p_, h_=h_: e.max_index(
                        out=ix1u[:, p_, h_, :], in_max=mx1[:, p_, h_, :],
                        in_values=simsb[:, ph * 64:(ph + 1) * 64]),
                        reads=[b_simsb, b_mx1], writes_d=[b_ix1u])
                sc.op("V", lambda e: e.tensor_copy(
                    out=ix1f[:].rearrange("p a h k -> p (a h k)"),
                    in_=ix1u[:].rearrange("p a h k -> p (a h k)")),
                    reads=[b_ix1u], writes=[b_ix1f])
                sc.op("V", lambda e: e.tensor_tensor(
                    out=all64[:].rearrange("p h (i j) -> p h i j", i=8),
                    in0=mx1[:, 0, :, :].unsqueeze(3).to_broadcast(
                        [128, 4, 8, 8]),
                    in1=mx1[:, 1, :, :].unsqueeze(2).to_broadcast(
                        [128, 4, 8, 8]),
                    op=ALU.add),
                    reads=[b_mx1], writes=[b_all64])
                for h_ in range(4):
                    sc.op("V", lambda e, h_=h_: e.max(
                        out=sc8[:, h_, :], in_=all64[:, h_, :]),
                        reads=[b_all64], writes_d=[b_sc8])
                for h_ in range(4):
                    sc.op("V", lambda e, h_=h_: e.max_index(
                        out=pku[:, h_, :], in_max=sc8[:, h_, :],
                        in_values=all64[:, h_, :]),
                        reads=[b_all64, b_sc8], writes_d=[b_pku])
                sc.op("V", lambda e: e.tensor_scalar(
                    out=iu[:].rearrange("p h k -> p (h k)"),
                    in0=pku[:].rearrange("p h k -> p (h k)"),
                    scalar1=3, scalar2=None, op0=ALU.logical_shift_right),
                    reads=[b_pku], writes=[b_iu])
                sc.op("V", lambda e: e.tensor_copy(
                    out=if_[:].rearrange("p h k -> p (h k)"),
                    in_=iu[:].rearrange("p h k -> p (h k)")),
                    reads=[b_iu], writes=[b_if])
                sc.op("V", lambda e: e.tensor_scalar(
                    out=iu[:].rearrange("p h k -> p (h k)"),
                    in0=pku[:].rearrange("p h k -> p (h k)"),
                    scalar1=7, scalar2=None, op0=ALU.bitwise_and),
                    reads=[b_pku, b_iu], writes_d=[b_iu])
                sc.op("V", lambda e: e.tensor_copy(
                    out=jf[:].rearrange("p h k -> p (h k)"),
                    in_=iu[:].rearrange("p h k -> p (h k)")),
                    reads=[b_iu], writes=[b_jf])

                def gather8(valsrc, possrc, bpos, dst, bdst):
                    sc.op("V", lambda e, possrc=possrc: e.tensor_tensor(
                        out=cmp8[:],
                        in0=iota8[:].unsqueeze(1).unsqueeze(2).to_broadcast(
                            [128, 4, 8, 8]),
                        in1=possrc[:].unsqueeze(3).to_broadcast(
                            [128, 4, 8, 8]),
                        op=ALU.is_equal),
                        reads=[bpos, b_iota8], writes=[b_cmp8])
                    sc.op("V", lambda e, valsrc=valsrc: e.tensor_tensor(
                        out=cmp8[:], in0=cmp8[:],
                        in1=valsrc.unsqueeze(2).to_broadcast(
                            [128, 4, 8, 8]),
                        op=ALU.mult),
                        reads=[b_ix1f, b_cmp8], writes_d=[b_cmp8])
                    sc.op("V", lambda e, dst=dst: e.reduce_sum(
                        out=dst[:], in_=cmp8[:], axis=AX.X),
                        reads=[b_cmp8], writes=[bdst])
                gather8(ix1f[:, 0, :, :], if_, b_if, idxf, b_idxf)
                sc.op("V", lambda e: e.tensor_scalar(
                    out=idxf[:].rearrange("p h k -> p (h k)"),
                    in0=idxf[:].rearrange("p h k -> p (h k)"),
                    scalar1=64.0, scalar2=None, op0=ALU.mult),
                    reads=[b_idxf], writes_d=[b_idxf])
                gather8(ix1f[:, 1, :, :], jf, b_jf, tmp48, b_tmp48)
                sc.op("V", lambda e: e.tensor_tensor(
                    out=idxf[:].rearrange("p h k -> p (h k)"),
                    in0=idxf[:].rearrange("p h k -> p (h k)"),
                    in1=tmp48[:].rearrange("p h k -> p (h k)"), op=ALU.add),
                    reads=[b_tmp48, b_idxf], writes_d=[b_idxf])
                sc.op("V", lambda e: e.tensor_tensor(
                    out=wsm[:], in0=sc8[:],
                    in1=sc8[:, :, 0:1].to_broadcast([128, 4, 8]),
                    op=ALU.subtract),
                    reads=[b_sc8], writes=[b_wsm])
                sc.op("A", lambda e: e.activation(
                    out=wsm[:].rearrange("p h k -> p (h k)"),
                    in_=wsm[:].rearrange("p h k -> p (h k)"), func=AF.Exp),
                    reads=[b_wsm], writes_d=[b_wsm])
                sc.op("V", lambda e: e.reduce_sum(out=zrow[:], in_=wsm[:],
                                                  axis=AX.X),
                      reads=[b_wsm], writes=[b_zrow])
                sc.op("V", lambda e: e.reciprocal(
                    out=zrow[:].rearrange("p h o -> p (h o)"),
                    in_=zrow[:].rearrange("p h o -> p (h o)")),
                    reads=[b_zrow], writes_d=[b_zrow])
                sc.op("V", lambda e: e.tensor_tensor(
                    out=wsm[:], in0=wsm[:],
                    in1=zrow[:].to_broadcast([128, 4, 8]), op=ALU.mult),
                    reads=[b_zrow, b_wsm], writes_d=[b_wsm])
                idxfl = idxf[:].rearrange("p h k -> p (h k)")
                sc.op("V", lambda e: e.tensor_tensor(
                    out=eq5[:],
                    in0=idxfl.unsqueeze(2).unsqueeze(3).to_broadcast(
                        [128, 32, 4, 8]),
                    in1=idxf[:].unsqueeze(1).to_broadcast([128, 32, 4, 8]),
                    op=ALU.is_equal),
                    reads=[b_idxf], writes=[b_eq5])
                sc.op("V", lambda e: e.reduce_sum(out=rt4[:], in_=eq5[:],
                                                  axis=AX.X),
                      reads=[b_eq5], writes=[b_rt4])
                sc.op("V", lambda e: e.reduce_sum(out=kr2[:], in_=rt4[:],
                                                  axis=AX.X),
                      reads=[b_rt4], writes=[b_kr2])
                sc.op("V", lambda e: e.tensor_tensor(
                    out=eq5[:], in0=eq5[:],
                    in1=mtri2[:].unsqueeze(3).to_broadcast([128, 32, 4, 8]),
                    op=ALU.mult),
                    reads=[b_eq5, b_mtri2], writes_d=[b_eq5])
                sc.op("V", lambda e: e.reduce_sum(out=rt4[:], in_=eq5[:],
                                                  axis=AX.X),
                      reads=[b_eq5], writes=[b_rt4])
                sc.op("V", lambda e: e.reduce_sum(
                    out=tmp48[:].rearrange("p h k -> p (h k)"), in_=rt4[:],
                    axis=AX.X),
                      reads=[b_rt4], writes=[b_tmp48])
                sc.op("V", lambda e: e.tensor_tensor(
                    out=kr2[:], in0=kr2[:],
                    in1=tmp48[:].rearrange("p h k -> p (h k)"),
                    op=ALU.subtract),
                    reads=[b_tmp48, b_kr2], writes_d=[b_kr2])
                sc.op("V", lambda e: e.tensor_scalar(
                    out=kr2[:], in0=kr2[:],
                    scalar1=1.0, scalar2=None, op0=ALU.is_gt),
                    reads=[b_kr2], writes_d=[b_kr2])
                sc.op("V", lambda e: e.tensor_tensor(
                    out=eq5[:], in0=eq5[:],
                    in1=wsm[:].unsqueeze(1).to_broadcast([128, 32, 4, 8]),
                    op=ALU.mult),
                    reads=[b_eq5, b_wsm], writes_d=[b_eq5])
                sc.op("V", lambda e: e.reduce_sum(out=rt4[:], in_=eq5[:],
                                                  axis=AX.X),
                      reads=[b_eq5], writes=[b_rt4])
                sc.op("V", lambda e: e.reduce_sum(
                    out=tmp48[:].rearrange("p h k -> p (h k)"), in_=rt4[:],
                    axis=AX.X),
                      reads=[b_rt4], writes=[b_tmp48])
                sc.op("V", lambda e: e.tensor_tensor(
                    out=wsm[:], in0=wsm[:],
                    in1=tmp48[:], op=ALU.add),
                    reads=[b_tmp48, b_wsm], writes_d=[b_wsm])
                sc.op("V", lambda e: e.tensor_scalar(
                    out=tmp48[:].rearrange("p h k -> p (h k)"),
                    in0=idxf[:].rearrange("p h k -> p (h k)"),
                    scalar1=1.0, scalar2=None, op0=ALU.add),
                    reads=[b_idxf, b_tmp48], writes_d=[b_tmp48])
                sc.op("V", lambda e: e.tensor_tensor(
                    out=tmp48[:].rearrange("p h k -> p (h k)"),
                    in0=tmp48[:].rearrange("p h k -> p (h k)"),
                    in1=kr2[:], op=ALU.mult),
                    reads=[b_kr2, b_tmp48], writes_d=[b_tmp48])
                sc.op("V", lambda e: e.tensor_tensor(
                    out=idxf[:], in0=idxf[:], in1=tmp48[:], op=ALU.subtract),
                    reads=[b_tmp48, b_idxf], writes_d=[b_idxf])

            def stP6(ti):
                sc.op("V", lambda e: e.tensor_copy(
                    out=w32b[:], in_=wsm[:].rearrange("p h k -> p (h k)")),
                    reads=[b_wsm], writes=[b_w32b])
                idxflat = idxf[:].rearrange("p h k -> p (h k)")
                for qd in range(4):
                    sc.op("V", lambda e, qd=qd: e.tensor_scalar(
                        out=qsc1[:], in0=idxflat, scalar1=float(1024 * qd),
                        scalar2=None, op0=ALU.subtract),
                        reads=[b_idxf], writes=[b_qsc1])
                    sc.op("V", lambda e: e.tensor_scalar(
                        out=qsc2[:], in0=qsc1[:], scalar1=0.0, scalar2=None,
                        op0=ALU.is_ge),
                        reads=[b_qsc1], writes=[b_qsc2])
                    sc.op("V", lambda e: e.scalar_tensor_tensor(
                        out=qsc2[:], in0=qsc1[:], scalar=1024.0, in1=qsc2[:],
                        op0=ALU.is_lt, op1=ALU.mult),
                        reads=[b_qsc1, b_qsc2], writes_d=[b_qsc2])
                    sc.op("V", lambda e: e.tensor_scalar(
                        out=qsc1[:], in0=qsc1[:], scalar1=1.0, scalar2=None,
                        op0=ALU.add),
                        reads=[b_qsc1], writes_d=[b_qsc1])
                    sc.op("V", lambda e: e.tensor_tensor(
                        out=qsc1[:], in0=qsc1[:], in1=qsc2[:], op=ALU.mult),
                        reads=[b_qsc1, b_qsc2], writes_d=[b_qsc1])
                    sc.op("V", lambda e: e.tensor_scalar(
                        out=qsc1[:], in0=qsc1[:], scalar1=-1.0, scalar2=None,
                        op0=ALU.add),
                        reads=[b_qsc1], writes_d=[b_qsc1])
                    sc.op("V", lambda e: e.tensor_copy(out=idx16[:], in_=qsc1[:]),
                          reads=[b_qsc1], writes=[b_idx16])
                    sc.op("P", lambda e, qd=qd: e.local_scatter(
                        out_ap=Sbf[:, qd * 1024:(qd + 1) * 1024],
                        data_ap=w32b[:], idxs_ap=idx16[:],
                        channels=128, num_elems=1024, num_idxs=32),
                        reads=[b_w32b, b_idx16], writes=[b_Sq[qd]])

            def stP7a(ti, hp, bhp):
                for ec in range(8):
                    pe_ = [pF0, pF1][ec % 2]
                    bpe = [b_pF0, b_pF1][ec % 2]
                    for k in range(8):
                        sc.op("T", lambda e, ec=ec, k=k, pe_=pe_, hp=hp: e.matmul(
                            pe_[:], hp[:, k, :],
                            det[:, k, ec * 512:(ec + 1) * 512],
                            start=(k == 0), stop=(k == 7)),
                            reads=[bhp, b_det],
                            writes=([bpe] if k == 0 else []),
                            writes_d=([bpe] if k > 0 else []),
                            inc=(1 if k == 7 else 0))
                    sc.op("A", lambda e, ec=ec, pe_=pe_: e.activation(
                        out=wdense[:, ec * 512:(ec + 1) * 512], in_=pe_[:],
                        func=AF.Silu),
                        reads=[bpe], writes=[b_wd[ec]])

            def stP7b(ti):
                for ec in range(8):
                    sc.op("V", lambda e, ec=ec: e.tensor_tensor(
                        out=wdense[:, ec * 512:(ec + 1) * 512],
                        in0=wdense[:, ec * 512:(ec + 1) * 512],
                        in1=Sbf[:, ec * 512:(ec + 1) * 512], op=ALU.mult),
                        reads=[b_wd[ec], b_Sq[ec // 2]], writes_d=[b_wd[ec]])

            def stP8(ti, xv, bxv):
                for half in range(2):
                    for kb in range(16):
                        sc.op("T", lambda e, half=half, kb=kb: e.transpose(
                            pTRb[:, kb * 128:(kb + 1) * 128],
                            wdense[:, half * 2048 + kb * 128:
                                   half * 2048 + (kb + 1) * 128], idnb[:]),
                            reads=[b_wd[half * 4 + kb // 4], b_idnb],
                            writes=([b_pTRb] if kb == 0 else []),
                            writes_d=([b_pTRb] if kb > 0 else []),
                            inc=(1 if kb % 4 == 3 else 0))
                    sc.op("V", lambda e, half=half: e.tensor_copy(
                        out=wdT[:, half * 16:(half + 1) * 16, :].rearrange(
                            "p e t -> p (e t)"),
                        in_=pTRb[:]),
                        reads=[b_pTRb],
                        writes=[b_Sq[half * 2], b_Sq[half * 2 + 1]])
                for et in range(32):
                    for half in range(2):
                        first = (et == 0 and half == 0)
                        last = (et == 31 and half == 1)
                        sc.op("T", lambda e, et=et, half=half, first=first: e.matmul(
                            pWO[:, half * 512:(half + 1) * 512],
                            wdT[:, et, :], ue[:, et, half * 512:(half + 1) * 512],
                            start=(et == 0), stop=(et == 31)),
                            reads=[b_Sq[et // 8], b_ue],
                            writes=([b_pWO] if first else []),
                            writes_d=([b_pWO] if not first else []),
                            inc=(1 if last else 0))
                sc.op("V", lambda e, xv=xv: e.tensor_copy(out=xv[:], in_=pWO[:]),
                      reads=[b_pWO], writes=[bxv])
                sc.dma("S", EXS[ti * 128:(ti + 1) * 128, :], xv[:], EXW,
                       reads=[bxv])

            pgl2 = sb(esF, "pgl2", [128, D], F32)
            b_pgl2 = sc.buf("pgl2")
            h2t2 = sb(esF, "h2t2", [128, 8, 128], BF16)
            b_h2t2 = sc.buf("h2t2")
            PGL2 = sc.dma_sem("pgl2s")
            PGL3 = sc.dma_sem("pgl3s")
            PP = [(pgl, b_pgl, PGL0, PGL1), (pgl2, b_pgl2, PGL2, PGL3)]
            HH = [(h2t, b_h2t), (h2t2, b_h2t2)]

            stP1(0, *PP[0][0:2], PP[0][2], PP[0][3])
            for ti in range(NT + 1):
                par = ti % 2
                if ti < NT:
                    stP2(ti, PP[par][0], PP[par][1])
                    if ti + 1 < NT:
                        stP1(ti + 1, *PP[1 - par][0:2], PP[1 - par][2],
                             PP[1 - par][3])
                if ti > 0:
                    stP7a(ti - 1, *HH[1 - par])
                    stP5(ti - 1)
                    stP6(ti - 1)
                    stP7b(ti - 1)
                if ti < NT:
                    stP3(ti, *HH[par])
                    stP4(ti)
                if ti > 0:
                    stP8(ti - 1, PP[par][0], PP[par][1])
            sc.barrier()

        with ExitStack() as esG:
            gbB2 = sb(esG, "gbB2", [128, D], F32)
            modr3 = sb(esG, "modr3", [1, D], F32R)
            fbg = sb(esG, "fbg", [128, 32], F32)
            fbu = sb(esG, "fbu", [128, 32], F32)
            bdrow = sb(esG, "bdrow", [1, D], F32)
            bdrow_b = sb(esG, "bdrow_b", [1, D], BF16)
            h2cd = [sb(esG, "h2c0", [128, 4, 8, 128], BF16),
                    sb(esG, "h2c1", [128, 4, 8, 128], BF16)]
            wgc = [sb(esG, "wgc0", [128, 4, 8, 128], BF16),
                   sb(esG, "wgc1", [128, 4, 8, 128], BF16)]
            wuc = [sb(esG, "wuc0", [128, 4, 8, 128], BF16),
                   sb(esG, "wuc1", [128, 4, 8, 128], BF16)]
            wdc = [sb(esG, "wdc0", [128, 4, 512], BF16),
                   sb(esG, "wdc1", [128, 4, 512], BF16)]
            silu_g = sb(esG, "silu_g", [128, 512], BF16)
            etile = sb(esG, "etile", [128, 32, 512], BF16)
            x2l = sb(esG, "x2l", [128, D], F32)
            exl = sb(esG, "exl", [128, D], F32)
            outs = sb(esG, "outs", [128, 4, D], F32)
            outb = [sb(esG, "outb0", [128, D], F32), sb(esG, "outb1", [128, D], F32)]
            b_gbB2 = sc.buf("gbB2"); b_modr3 = sc.buf("modr3")
            b_fbg = sc.buf("fbg"); b_fbu = sc.buf("fbu")
            b_bdrow = sc.buf("bdrow"); b_bdrowb = sc.buf("bdrow_b")
            b_h2cd = [sc.buf("h2c0"), sc.buf("h2c1")]
            b_wgc = [sc.buf("wgc0"), sc.buf("wgc1")]
            b_wuc = [sc.buf("wuc0"), sc.buf("wuc1")]
            b_wdc = [sc.buf("wdc0"), sc.buf("wdc1")]
            b_silug = sc.buf("silu_g"); b_etile = sc.buf("etile")
            b_x2l = sc.buf("x2l"); b_exl = sc.buf("exl")
            b_outs = sc.buf("outs")
            b_outb = [sc.buf("outb0"), sc.buf("outb1")]
            GL = sc.dma_sem("gl")
            H2L = sc.dma_sem("h2l")
            X2L = sc.dma_sem("x2l_s")
            EXL = sc.dma_sem("exl_s")
            WGS = [sc.dma_sem("wgs0"), sc.dma_sem("wgs1")]
            WUS = [sc.dma_sem("wus0"), sc.dma_sem("wus1")]
            WDS = [sc.dma_sem("wds0"), sc.dma_sem("wds1")]
            OST = [sc.dma_sem("ost0"), sc.dma_sem("ost1")]

            pG = psm(esG, "pG", [128, 512], F32)
            pU = psm(esG, "pU", [128, 512], F32)
            pM2 = [psm(esG, f"pM2{j}", [128, 512], F32) for j in range(4)]
            b_pG = sc.buf("pG"); b_pU = sc.buf("pU")
            b_pM2 = [sc.buf(f"pM2{j}") for j in range(4)]

            sc.load_group(GL, [
                ("S", fbg[:], FBg, b_fbg),
                ("S", fbu[:], FBu, b_fbu),
                ("S", bdrow[:], FBd, b_bdrow),
                ("P", modr3[:], MODS[:, 5 * D:6 * D], b_modr3),
            ])
            bcast_tile(gbB2, b_gbB2, modr3, b_modr3, D, pWO, b_pWO)
            sc.op("V", lambda e: e.tensor_copy(out=bdrow_b[:], in_=bdrow[:]),
                  reads=[b_bdrow], writes=[b_bdrowb])

            FCH2 = 512

            def ld_gu(m, par):
                sc.dma("P", wgc[par][:],
                       FWg.rearrange("m p k n -> p m k n")[:, m:m + 4],
                       WGS[par], writes=[b_wgc[par]])
                sc.dma("P", wuc[par][:],
                       FWu.rearrange("m p k n -> p m k n")[:, m:m + 4],
                       WUS[par], writes=[b_wuc[par]])

            ld_gu(0, 0)
            sc.dma("S", h2cd[0][:], H2S[:, 0:4, :, :], H2L, writes=[b_h2cd[0]])
            for ci in range(S // FCH2):
                t0 = ci * 4
                h2c, b_h2c = h2cd[ci % 2], b_h2cd[ci % 2]
                for m in range(32):
                    par = (m // 4) % 2
                    mm = m % 4
                    if mm == 0:
                        if m + 4 < 32:
                            ld_gu(m + 4, 1 - par)
                        elif ci + 1 < S // FCH2:
                            ld_gu(0, 0)
                    for k in range(8):
                        sc.op("T", lambda e, k=k, par=par, mm=mm, h2c=h2c:
                              e.matmul(
                            pG[:], wgc[par][:, mm, k, :],
                            h2c[:, :, k, :], start=(k == 0), stop=(k == 7)),
                            reads=[b_wgc[par], b_h2c],
                            writes=([b_pG] if k == 0 else []),
                            writes_d=([b_pG] if k > 0 else []),
                            inc=(1 if k == 7 else 0))
                    for k in range(8):
                        sc.op("T", lambda e, k=k, par=par, mm=mm, h2c=h2c:
                              e.matmul(
                            pU[:], wuc[par][:, mm, k, :],
                            h2c[:, :, k, :], start=(k == 0), stop=(k == 7)),
                            reads=[b_wuc[par], b_h2c],
                            writes=([b_pU] if k == 0 else []),
                            writes_d=([b_pU] if k > 0 else []),
                            inc=(1 if k == 7 else 0))
                    sc.op("A", lambda e, m=m: e.activation(
                        out=silu_g[:], in_=pG[:], func=AF.Silu,
                        bias=fbg[:, m:m + 1]),
                        reads=[b_pG, b_fbg], writes=[b_silug])
                    sc.op("V", lambda e, m=m: e.scalar_tensor_tensor(
                        out=etile[:, m, :], in0=pU[:], scalar=fbu[:, m:m + 1],
                        in1=silu_g[:], op0=ALU.add, op1=ALU.mult),
                        reads=[b_pU, b_fbu, b_silug], writes_d=[b_etile])
                if ci + 1 < S // FCH2:
                    sc.dma("S", h2cd[(ci + 1) % 2][:],
                           H2S[:, t0 + 4:t0 + 8, :, :], H2L,
                           writes=[b_h2cd[(ci + 1) % 2]])
                def ld_wd(m, wpar, half):
                    sc.dma("P", wdc[wpar][:],
                           FWd.rearrange("m p n -> p m n")[
                               :, m:m + 4, half * 512:(half + 1) * 512],
                           WDS[wpar], writes=[b_wdc[wpar]])

                ld_wd(0, 0, 0)
                for half in range(2):
                    for m in range(32):
                        wpar = (m // 4) % 2
                        mm = m % 4
                        if mm == 0:
                            if m + 4 < 32:
                                ld_wd(m + 4, 1 - wpar, half)
                            elif half == 0:
                                ld_wd(0, 0, 1)
                        for tt in range(4):
                            first = (m == 0)
                            sc.op("T", lambda e, m=m, tt=tt, wpar=wpar,
                                  mm=mm: e.matmul(
                                pM2[tt][:],
                                etile[:, m, tt * 128:(tt + 1) * 128],
                                wdc[wpar][:, mm, :], start=(m == 0), stop=False),
                                reads=[b_etile, b_wdc[wpar]],
                                writes=([b_pM2[tt]] if first else []),
                                writes_d=([b_pM2[tt]] if not first else []),
                                inc=(1 if tt == 3 else 0))
                    for tt in range(4):
                        sc.op("T", lambda e, tt=tt, half=half: e.matmul(
                            pM2[tt][:], onesc_b[:],
                            bdrow_b[:, half * 512:(half + 1) * 512],
                            start=False, stop=True),
                            reads=[b_onesb, b_bdrowb], writes_d=[b_pM2[tt]])
                        sc.op("V", lambda e, tt=tt, half=half: e.tensor_copy(
                            out=outs[:, tt, half * 512:(half + 1) * 512],
                            in_=pM2[tt][:]),
                            reads=[b_pM2[tt]], writes_d=[b_outs])
                for tt in range(4):
                    ti = t0 + tt
                    par = ti % 2
                    sc.dma("S", x2l[:], X2S[ti * 128:(ti + 1) * 128, :], X2L,
                           writes=[b_x2l])
                    sc.dma("S", exl[:], EXS[ti * 128:(ti + 1) * 128, :], EXL,
                           writes=[b_exl])
                    sc.op("V", lambda e, par=par, tt=tt: e.tensor_tensor(
                        out=outb[par][:], in0=exl[:], in1=outs[:, tt, :],
                        op=ALU.add),
                        reads=[b_x2l, b_exl, b_outs], writes=[b_outb[par]])
                    sc.op("V", lambda e, par=par: e.tensor_tensor(
                        out=outb[par][:], in0=outb[par][:], in1=gbB2[:],
                        op=ALU.mult),
                        reads=[b_gbB2, b_outb[par]], writes_d=[b_outb[par]])
                    sc.op("V", lambda e, par=par: e.tensor_tensor(
                        out=outb[par][:], in0=outb[par][:], in1=x2l[:],
                        op=ALU.add),
                        reads=[b_x2l, b_outb[par]], writes_d=[b_outb[par]])
                    sc.dma("S", OUT[ti * 128:(ti + 1) * 128, :], outb[par][:],
                           OST[par], reads=[b_outb[par]])
            sc.barrier()


        block = es.enter_context(nc.Block())
        sc.emit(block, upto=upto)
    return nc


def make_in_maps(inputs, ncores=B):
    t = host_tables()
    x = np.ascontiguousarray(inputs["L0_attn_x"], dtype=np.float32)
    c = np.ascontiguousarray(inputs["L0_c"], dtype=np.float32)

    def dmaj(v, cols):
        return np.ascontiguousarray(v.reshape(cols, 128).T)

    def wblk(w, G, N):
        return np.ascontiguousarray(
            np.asarray(w, np.float32).reshape(8, 128, G, N).transpose(2, 1, 0, 3))

    keys = inputs["L0_ffn_keys"]
    keysT = np.transpose(keys, (3, 2, 0, 1)).reshape(64, 8 * 64)
    keysT = np.ascontiguousarray(np.concatenate([keysT, keysT], axis=0))
    det = np.ascontiguousarray(
        inputs["L0_ffn_down_embed"].T.reshape(8, 128, 4096).transpose(1, 0, 2))
    ue = np.ascontiguousarray(
        np.asarray(inputs["L0_ffn_up_embed"], np.float32).reshape(32, 128, 1024)
        .transpose(1, 0, 2))
    wo = np.ascontiguousarray(
        np.asarray(inputs["L0_attn_Wo"], np.float32).reshape(4, 4, 128, 1024)
        .transpose(0, 2, 1, 3))
    fwq = np.ascontiguousarray(
        np.asarray(inputs["L0_ffn_Wq"], np.float32).reshape(8, 128, 512)
        .transpose(1, 0, 2))
    fwd = np.ascontiguousarray(
        np.asarray(inputs["L0_ffn_Wd"], np.float32).reshape(32, 128, 1024))
    wmod = np.ascontiguousarray(
        np.asarray(inputs["L0_wmod"], np.float32).reshape(8, 128, 12, 512)
        .transpose(2, 1, 0, 3))

    common = {
        "Wq": wblk(inputs["L0_attn_Wq"], 4, 256).astype(ml_dtypes.bfloat16),
        "Wk": wblk(inputs["L0_attn_Wk"], 4, 256).astype(ml_dtypes.bfloat16),
        "Wv": wblk(inputs["L0_attn_Wv"], 4, 512).astype(ml_dtypes.bfloat16),
        "Wg": wblk(inputs["L0_attn_Wg"], 4, 512).astype(ml_dtypes.bfloat16),
        "Wo": wo.astype(ml_dtypes.bfloat16),
        "BQ": dmaj(inputs["L0_attn_bq"], 8), "BK": dmaj(inputs["L0_attn_bk"], 8),
        "BV": inputs["L0_attn_bv"].reshape(1, -1),
        "BGA": inputs["L0_attn_bg"].reshape(1, -1),
        "BO": inputs["L0_attn_bo"].reshape(1, -1),
        "FWg": wblk(inputs["L0_ffn_Wg"], 32, 128).astype(ml_dtypes.bfloat16),
        "FWu": wblk(inputs["L0_ffn_Wu"], 32, 128).astype(ml_dtypes.bfloat16),
        "FWd": fwd.astype(ml_dtypes.bfloat16),
        "FBg": dmaj(inputs["L0_ffn_bg"], 32), "FBu": dmaj(inputs["L0_ffn_bu"], 32),
        "FBd": inputs["L0_ffn_bd"].reshape(1, -1),
        "FWq": fwq,
        "KEYS": keysT, "DET": det.astype(ml_dtypes.bfloat16),
        "UEW": ue.astype(ml_dtypes.bfloat16),
        "WMOD": wmod,
        "COSB": t["COSB"], "SINB": t["SINB"], "QS16": t["QS16"],
        "KS16": t["KS16"],
        "SELQ": t["SELQ"].reshape(16, 8 * 128),
        "MASKS": t["MASKS"].reshape(128, 4 * 512),
        "PT": t["PT"], "IDNB": t["IDNB"], "IDNF": np.eye(128, dtype=np.float32),
        "ONESC": t["ONESC"], "IOTA8": t["IOTA8"], "MTRI2": t["MTRI2"],
    }
    in_maps = []
    for b in range(ncores):
        m = dict(common)
        m["X"] = np.ascontiguousarray(x[b])
        m["CC"] = dmaj(c[b], 8)
        in_maps.append(m)
    return in_maps


def kernel(**inputs):
    nc = build_program(dbg=DEBUG)
    lower_extended_insts(nc)
    in_maps = make_in_maps(inputs, B)
    res = run_bass_kernel_spmd(nc, in_maps, core_ids=list(range(B)))
    out = np.stack([res.results[b]["OUT"] for b in range(B)], axis=0)
    return out.astype(np.float32)
